# version 12
# baseline (speedup 1.0000x reference)
from contextlib import ExitStack

import numpy as np
import ml_dtypes
import concourse.bass as bass
import concourse.mybir as mybir
from concourse.bass_utils import run_bass_kernel_spmd

F32 = mybir.dt.float32
BF16 = mybir.dt.bfloat16
AF = mybir.ActivationFunctionType
ALU = mybir.AluOpType
AX = mybir.AxisListType

ENGS = ("sync", "tensor", "vector", "scalar", "gpsimd")


def sb_ap(t, p0, np_, dims, col0=0):
    w = t.shape[-1]
    return bass.AP(t, p0 * w + col0, [[w, np_]] + [list(d) for d in dims])


def dr_ap(t_ap, row0, nrow, dims, col0=0):
    w = t_ap.tensor.shape[-1]
    return bass.AP(t_ap.tensor, t_ap.offset + row0 * w + col0,
                   [[w, nrow]] + [list(d) for d in dims])


class _Cap:

    def __init__(self):
        self.calls = []

    def __getattr__(self, name):
        def m(*a, **k):
            self.calls.append((name, a, k))
            return None
        return m

    def __call__(self, engine):
        name, a, k = self.calls[0]
        return getattr(engine, name)(*a, **k)


class Sched:

    def __init__(self, nc):
        self.nc = nc
        self.ops = {e: [] for e in ENGS}
        self.sem_cnt = {}
        self.sem_order = []
        self.writer = {}
        self.readers = {}
        self.n_ops = 0
        self.epoch = 0

    def new_epoch(self):
        self.barrier(f"epoch{self.epoch}")
        self.epoch += 1

    def _touch_sem(self, name):
        if name not in self.sem_cnt:
            self.sem_cnt[name] = 0
            self.sem_order.append(name)

    def op(self, eng, fn, reads=(), writes=(), dma=None, tag=""):
        cap = _Cap()
        fn(cap)
        assert len(cap.calls) == 1, f"op must emit exactly 1 instruction, got {len(cap.calls)} ({tag})"
        fn = cap
        semname = ((self.epoch, "D_" + dma) if dma is not None else eng)
        inc = 16 if dma is not None else 1
        self._touch_sem(semname)
        if dma is not None:
            cls = "sw" if eng == "gpsimd" else "hw"
            prev = getattr(self, "sem_qcls", None)
            if prev is None:
                self.sem_qcls = {}
            old = self.sem_qcls.get(semname)
            assert old in (None, cls), f"group {dma} used from both queue classes"
            self.sem_qcls[semname] = cls
        deps = {}

        def add(tok):
            if tok is None:
                return
            sn, v = tok
            if deps.get(sn, 0) < v:
                deps[sn] = v

        for b in reads:
            add(self.writer.get(b))
        for b in writes:
            add(self.writer.get(b))
            for r in self.readers.get(b, ()):
                add(r)
        if eng == "tensor" and dma is None:
            deps.pop("tensor", None)
        val = self.sem_cnt[semname] + inc
        self.sem_cnt[semname] = val
        self.ops[eng].append((fn, sorted(deps.items(), key=lambda kv: str(kv[0])), semname, inc, tag, self.epoch))
        self.n_ops += 1
        tok = (semname, val)
        for b in reads:
            self.readers.setdefault(b, []).append(tok)
        for b in writes:
            self.writer[b] = tok
            self.readers[b] = []
        return tok

    def dma(self, queue, out, in_, reads=(), writes=(), group=None, tag="",
            slow_ok=False):
        assert group is not None
        return self.op(queue,
                       lambda e: e.dma_start(out=out, in_=in_,
                                             allow_slow_non_contiguous=slow_ok),
                       reads=reads, writes=writes, dma=group, tag=tag)

    def barrier(self, tag="barrier"):
        snap = dict(self.sem_cnt)
        deps = sorted(snap.items(), key=lambda kv: str(kv[0]))
        for eng in ENGS:
            self.ops[eng].append((None, deps, None, 0, tag, self.epoch))

    def emit(self, exitstack):
        nc = self.nc
        eng_names = [n for n in self.sem_order if not isinstance(n, tuple)]
        phys = {}
        for i, n in enumerate(eng_names):
            phys[n] = i
        npool_base = len(eng_names)
        epochs = sorted({n[0] for n in self.sem_order if isinstance(n, tuple)})
        qcls = getattr(self, "sem_qcls", {})
        pools = {"sw": [], "hw": []}
        base = {}
        slot_of = {}
        for ep in epochs:
            names = [n for n in self.sem_order
                     if isinstance(n, tuple) and n[0] == ep]
            per_cls = {"sw": 0, "hw": 0}
            for n in names:
                c = qcls.get(n, "hw")
                j = per_cls[c]
                per_cls[c] += 1
                pool = pools[c]
                while j >= len(pool):
                    pool.append(0)
                slot_of[n] = (c, j)
                base[n] = pool[j]
                pool[j] += self.sem_cnt[n]
        for n, (c, j) in slot_of.items():
            off_c = 0 if c == "sw" else len(pools["sw"])
            phys[n] = npool_base + off_c + j
        nphys = npool_base + len(pools["sw"]) + len(pools["hw"])
        assert nphys <= 100, f"too many physical sems: {nphys}"
        handles = [exitstack.enter_context(nc.semaphore(f"s{i}"))
                   for i in range(nphys)]
        sems = {n: handles[p] for n, p in phys.items()}
        off = {n: base.get(n, 0) for n in self.sem_order}
        self._off = off
        block = exitstack.enter_context(nc.Block())
        for eng in ENGS:
            recs = self.ops[eng]
            if not recs:
                continue

            def body(engine, recs=recs):
                es_lp = ExitStack()
                es_lp.enter_context(nc.allow_low_precision(
                    reason="deliberate bf16 production of matmul operands"))
                vclock = {}
                cur_ep = -1
                cur_sid = None
                for fn, deps, semname, inc, tag, ep in recs:
                    if ep != cur_ep and eng == "sync":
                        if cur_sid is not None:
                            nc.leave_named_scope(f"ph{cur_ep}", cur_sid, False)
                        cur_sid, _ = nc.enter_named_scope(f"ph{ep}", False)
                        cur_ep = ep
                    elif ep != cur_ep:
                        cur_ep = ep
                    for sn, v in deps:
                        if vclock.get(sn, 0) < v:
                            engine.wait_ge(sems[sn], off[sn] + v)
                            vclock[sn] = v
                    if fn is None:
                        continue
                    try:
                        instr = fn(engine)
                    except Exception as ex:
                        raise RuntimeError(
                            f"emit failed for op tag={tag} eng={eng} "
                            f"call={fn.calls[0][0]}") from ex
                    instr.then_inc(sems[semname], inc)
                if cur_sid is not None:
                    nc.leave_named_scope(f"ph{cur_ep}", cur_sid, False)
                es_lp.close()

            getattr(block, eng)(body)
        return sems


class MK:
    def __init__(self, nc, sch, B, T):
        self.nc = nc
        self.sch = sch
        self.B = B
        self.T = T
        self.TT = B * T
        self.uid = 0

    def name(self, base):
        self.uid += 1
        return f"{base}_{self.uid}"


def cast_stream(mk, es, dram_ap, row0, D, col0, C, out16, out_key,
                queue="gpsimd", cast_eng=None, out_col_stride=None):
    nc, sch = mk.nc, mk.sch
    KT = D // 128
    ocs = out_col_stride or C
    gname = mk.name("cs")
    for j in range(KT):
        sch.dma("gpsimd", out16[:, j * ocs: j * ocs + C],
                dr_ap(dram_ap, row0 + j * 128, 128, [[1, C]], col0),
                writes=[(out_key, j), ("chain", gname, j % 2)],
                group=f"{gname}_{j % 2}", tag="wload")


def emit_phaseA(mk, es, xin, Wqkv, Wproj, host, x1_out):
    nc, sch = mk.nc, mk.sch
    B, T, TT = mk.B, mk.T, mk.TT
    D, H = 1024, 16
    KT = 8
    NG = T // 512
    QT = T // 128
    QPW = min(1024, T)
    NQP = T // QPW
    EPS = 1e-8

    qh_d = nc.dram_tensor("qh_d", [D, T], BF16).ap()
    kh_d = nc.dram_tensor("kh_d", [D, T], BF16).ap()
    mrow_d = nc.dram_tensor("mrow_d", [H, T], BF16).ap()

    def load_const(name, rows, cols, cast16=False):
        ap = host[name]
        if cast16:
            t16 = es.enter_context(nc.sbuf_tensor(mk.name(name + "16"), [rows, cols], BF16))
            sch.dma("gpsimd", t16[0:rows, 0:cols], ap,
                    writes=[name, ("chain", "hcA")], group="hcA")
            return t16
        t32 = es.enter_context(nc.sbuf_tensor(mk.name(name), [rows, cols], F32))
        sch.dma("sync", t32[0:rows, 0:cols], ap,
                writes=[name + "32", ("chain", "hcA2")], group="hcA2")
        return t32

    hm16 = load_const("hm16", 128, 8 * 16, cast16=True)
    ind16c = load_const("ind16c", 16, 8 * 128, cast16=True)
    ind16p = load_const("ind16p", 16, 8 * 128, cast16=True)
    ones1_32 = load_const("ones1", 1, 128)
    qs_col = load_const("qs_col", 128, 8)
    ks_col = load_const("ks_col", 128, 8)
    bproj_col = load_const("bproj_col", 128, 8)


    epsA2 = es.enter_context(nc.sbuf_tensor("epsA2", [16, 1], F32))
    ln8A = es.enter_context(nc.sbuf_tensor("ln8A", [16, 1], F32))
    sch.op("vector", lambda e: e.memset(epsA2[0:16, 0:1], 6.4e-15),
           writes=["epsA2c"], tag="epsA2c")
    sch.op("vector", lambda e: e.memset(ln8A[0:16, 0:1], 2.0794415416798357),
           writes=["ln8Ac"], tag="ln8Ac")

    wqkv16 = es.enter_context(nc.sbuf_tensor("wqkv16", [128, KT * 3072], BF16))
    cast_stream(mk, es, Wqkv, 0, D, 0, 3072, wqkv16, "wqkv16", out_col_stride=3072)
    wproj16 = es.enter_context(nc.sbuf_tensor("wproj16", [128, KT * 1024], BF16))
    cast_stream(mk, es, Wproj, 0, D, 0, 1024, wproj16, "wproj16", out_col_stride=1024)

    VBLK = H * 65

    for s in range(B):
        with ExitStack() as es_s:
            v_sb = es_s.enter_context(nc.sbuf_tensor(mk.name("v_sb"), [128, QT * VBLK], BF16))
            sch.op("vector", lambda e: e.memset(
                sb_ap(v_sb, 0, 128, [[VBLK, QT], [65, H]], col0=64), 1.0),
                writes=["v_ones"], tag="v_ones")
            obuf = es_s.enter_context(nc.sbuf_tensor(mk.name("obuf"), [128, KT * T], BF16))
            obuf_keys = []

            with ExitStack() as es_1:
                x16r = [es_1.enter_context(nc.sbuf_tensor(mk.name("x16r"),
                                                          [128, KT * 512], BF16))
                        for _ in range(2)]
                qkraw = es_1.enter_context(nc.sbuf_tensor(mk.name("qkraw"), [128, 16 * 512], BF16))
                sqr = [es_1.enter_context(nc.sbuf_tensor(mk.name("sqr"), [128, 512], BF16))
                       for _ in range(2)]
                fq_sb = es_1.enter_context(nc.sbuf_tensor(mk.name("fq_sb"), [16, T], BF16))
                fk_sb = es_1.enter_context(nc.sbuf_tensor(mk.name("fk_sb"), [16, T], BF16))
                qn_sb = es_1.enter_context(nc.sbuf_tensor(mk.name("qn_sb"), [16, T], F32))
                kn_sb = es_1.enter_context(nc.sbuf_tensor(mk.name("kn_sb"), [16, T], F32))
                knorm = es_1.enter_context(nc.sbuf_tensor(mk.name("knorm"), [16, 1], F32))
                mneg16 = es_1.enter_context(nc.sbuf_tensor(mk.name("mneg16"), [16, T], BF16))
                sc1 = es_1.enter_context(nc.sbuf_tensor(mk.name("sc1"), [16, 512], F32))
                sc2 = sc1
                qstage = es_1.enter_context(nc.sbuf_tensor(mk.name("qstage"), [128, 8 * 512], BF16))
                kstage = qstage

                ps_mm = [es_1.enter_context(nc.psum_tensor(mk.name("ps_mm"),
                                                           [128, 512], F32))
                         for _ in range(2)]
                ps_ss = [es_1.enter_context(nc.psum_tensor(mk.name("ps_ss"),
                                                           [16, 512], F32))
                         for _ in range(4)]
                ps_fb = [es_1.enter_context(nc.psum_tensor(mk.name("ps_fb"),
                                                           [128, 512], F32))
                        for _ in range(2)]

                for g in range(NG):
                    tok0 = s * T + g * 512
                    xr = g % 2
                    for k in range(KT):
                        sch.dma("gpsimd", x16r[xr][:, k * 512:(k + 1) * 512],
                                dr_ap(xin, k * 128, 128, [[1, 512]], col0=tok0),
                                reads=[], writes=[("x16r", xr, k)],
                                group=f"x16r{xr}_{k}", tag="xcast")
                    for ttl in range(4):
                        tt = g * 4 + ttl
                        for vc in range(2):
                            pi = (ttl * 2 + vc) % 2
                            pkey = ("ps_mm", pi)
                            for k in range(KT):
                                sch.op("tensor", lambda e, xr=xr, ttl=ttl, vc=vc, k=k, pi=pi: e.matmul(
                                    ps_mm[pi][:, :],
                                    x16r[xr][:, k * 512 + ttl * 128: k * 512 + ttl * 128 + 128],
                                    wqkv16[:, k * 3072 + 2048 + vc * 512:
                                           k * 3072 + 2048 + vc * 512 + 512],
                                    start=(k == 0), stop=(k == KT - 1)),
                                    reads=[("x16r", xr, k), ("wqkv16", k)],
                                    writes=[pkey], tag="vmm")
                            base = tt * VBLK + vc * 8 * 65
                            sch.op("vector", lambda e, base=base, pi=pi: e.tensor_copy(
                                out=sb_ap(v_sb, 0, 128, [[65, 8], [1, 64]], col0=base),
                                in_=ps_mm[pi][:, :]),
                                reads=[pkey, "v_ones"],
                                writes=[("v_sb", tt, vc)], tag="vevac")
                    for mt in range(16):
                        pi = mt % 2
                        pkey = ("ps_mm", pi)
                        isq = mt < 8
                        j = mt % 8
                        for k in range(KT):
                            sch.op("tensor", lambda e, xr=xr, mt=mt, k=k, pi=pi: e.matmul(
                                ps_mm[pi][:, :],
                                wqkv16[:, k * 3072 + mt * 128: k * 3072 + mt * 128 + 128],
                                x16r[xr][:, k * 512:(k + 1) * 512],
                                start=(k == 0), stop=(k == KT - 1)),
                                reads=[("wqkv16", k), ("x16r", xr, k)],
                                writes=[pkey], tag="qkmm")
                        sch.op("vector", lambda e, mt=mt, pi=pi: e.tensor_copy(
                            out=qkraw[:, mt * 512:(mt + 1) * 512], in_=ps_mm[pi][:, :]),
                            reads=[pkey], writes=[("qkraw", mt)], tag="qkevac")
                        sch.op("gpsimd", lambda e, mt=mt, pi=pi: e.tensor_mul(
                            sqr[pi][:, :],
                            qkraw[:, mt * 512:(mt + 1) * 512],
                            qkraw[:, mt * 512:(mt + 1) * 512]),
                            reads=[("qkraw", mt)], writes=[("sqr", pi)], tag="qksq")
                        wu = 0 if isq else 2
                        sch.op("tensor", lambda e, wu=wu, j=j, pi=pi: e.matmul(
                            ps_ss[wu][0:16, :], hm16[:, j * 16:(j + 1) * 16],
                            sqr[pi][:, :], start=(j == 0), stop=(j == 7)),
                            reads=["hm16", ("sqr", pi)],
                            writes=[("ps_ss", wu)], tag="ssmm")
                    gsl = slice(g * 512, (g + 1) * 512)
                    sch.op("scalar", lambda e: e.activation(
                        sc1[:, :], ps_ss[0][0:16, :], AF.Ln, bias=epsA2[0:16, 0:1]),
                        reads=[("ps_ss", 0), "epsA2c"], writes=["sc1"], tag="fq1")
                    sch.op("scalar", lambda e, gsl=gsl: e.activation(
                        fq_sb[:, gsl], sc1[:, :], AF.Exp, scale=-0.5,
                        bias=ln8A[0:16, 0:1]),
                        reads=["sc1", "ln8Ac"], writes=[("fq_sb", g)], tag="fq2")
                    sch.op("scalar", lambda e: e.activation(
                        sc2[:, :], ps_ss[2][0:16, :], AF.Ln, bias=epsA2[0:16, 0:1]),
                        reads=[("ps_ss", 2), "epsA2c"], writes=["sc1"], tag="fk1")
                    sch.op("scalar", lambda e, gsl=gsl: e.activation(
                        fk_sb[:, gsl], sc2[:, :], AF.Exp, scale=-0.5,
                        bias=ln8A[0:16, 0:1]),
                        reads=["sc1", "ln8Ac"], writes=[("fk_sb", g)], tag="fk2")
                    for mt in list(range(8)) + ["QDMA"] + list(range(8, 16)):
                        if mt == "QDMA":
                            sch.dma("scalar",
                                    dr_ap(qh_d, 0, 128, [[128 * T, 8], [1, 512]], col0=g * 512),
                                    sb_ap(qstage, 0, 128, [[512, 8], [1, 512]]),
                                    reads=[("qstage", jj) for jj in range(8)],
                                    writes=[("qh_d", g)], group="qh_st", tag="qst")
                            continue
                        isq = mt < 8
                        j = mt % 8
                        pi = mt % 2
                        ind = ind16c if isq else ind16p
                        indk = "ind16c" if isq else "ind16p"
                        fsb = fq_sb if isq else fk_sb
                        fkey = ("fq_sb", g) if isq else ("fk_sb", g)
                        scol = qs_col if isq else ks_col
                        sckey = "qs_col32" if isq else "ks_col32"
                        stgb = qstage
                        stgkey = ("qstage", j) if isq else ("kstage", j)
                        stgkey2 = ("kstage", j) if isq else ("qstage", j)
                        sch.op("tensor", lambda e, ind=ind, j=j, fsb=fsb, gsl=gsl, pi=pi: e.matmul(
                            ps_fb[pi][:, :], ind[0:16, j * 128:(j + 1) * 128], fsb[:, gsl],
                            start=True, stop=True),
                            reads=[indk, fkey], writes=[("ps_fb", pi)], tag="fbmm")
                        sch.op("vector", lambda e, stgb=stgb, j=j, scol=scol, mt=mt, pi=pi:
                               e.scalar_tensor_tensor(
                                   stgb[:, j * 512:(j + 1) * 512], ps_fb[pi][:, :],
                                   scol[:, j:j + 1], qkraw[:, mt * 512:(mt + 1) * 512],
                                   ALU.mult, ALU.mult),
                               reads=[("ps_fb", pi), sckey, ("qkraw", mt)],
                               writes=[stgkey, stgkey2], tag="qhat")
                    sch.dma("scalar",
                            dr_ap(kh_d, 0, 128, [[128 * T, 8], [1, 512]], col0=g * 512),
                            sb_ap(kstage, 0, 128, [[512, 8], [1, 512]]),
                            reads=[("kstage", j) for j in range(8)],
                            writes=[("kh_d", g)], group="kh_st", tag="kst")
                sch.barrier("endA1")

            with ExitStack() as es_2:
                qaug = [es_2.enter_context(nc.sbuf_tensor(mk.name("qaug"), [128, T], BF16))
                        for _ in range(2)]
                kaug = [es_2.enter_context(nc.sbuf_tensor(mk.name("kaug"), [128, T], BF16))
                        for _ in range(2)]
                for i in range(2):
                    sch.op("vector", lambda e, i=i: e.memset(kaug[i][64:65, :], 1.0),
                           writes=[("kaug_ones", i)], tag="kones")
                    sch.op("vector", lambda e, i=i: e.memset(qaug[i][64:65, :], 0.0),
                           writes=[("qaug_m", i)], tag="qmz")
                pbuf = es_2.enter_context(nc.sbuf_tensor(mk.name("pbuf"), [128, QT * QPW], BF16))
                rden = es_2.enter_context(nc.sbuf_tensor(mk.name("rden"), [1, 512], F32))
                rdenb = es_2.enter_context(nc.sbuf_tensor(mk.name("rdenb"), [64, 512], F32))
                ps_s2 = [es_2.enter_context(nc.psum_tensor(mk.name("ps_s2"),
                                                           [128, QPW], F32))
                         for _ in range(2)]
                ps_o = [es_2.enter_context(nc.psum_tensor(mk.name("ps_o"), [65, 512], F32))
                        for _ in range(2)]
                ps_r = es_2.enter_context(nc.psum_tensor(mk.name("ps_r"), [64, 512], F32))
                for h in range(H):
                    bi = h % 2
                    sch.dma("sync", qaug[bi][0:64, 0:T],
                            qh_d[h * 64:(h + 1) * 64, :],
                            reads=[("qh_d", g) for g in range(NG)],
                            writes=[("qaug", bi)], group=f"qaug{bi}", tag="qld")
                    sch.dma("sync", kaug[bi][0:64, 0:T],
                            kh_d[h * 64:(h + 1) * 64, :],
                            reads=[("kh_d", g) for g in range(NG)],
                            writes=[("kaug", bi)], group=f"kaug{bi}", tag="kld")
                    for qp in range(NQP):
                        for kt in range(QT):
                            pi = kt % 2
                            for qq in range(QPW // 512):
                                sch.op("tensor", lambda e, kt=kt, qp=qp, bi=bi, pi=pi, qq=qq: e.matmul(
                                    ps_s2[pi][:, qq * 512:(qq + 1) * 512],
                                    kaug[bi][0:65, kt * 128:(kt + 1) * 128],
                                    qaug[bi][0:65, qp * QPW + qq * 512:
                                             qp * QPW + qq * 512 + 512],
                                    start=True, stop=True),
                                    reads=[("kaug", bi), ("kaug_ones", bi),
                                           ("qaug", bi), ("qaug_m", bi)],
                                    writes=[("ps_s2", pi)], tag="s2mm")
                            sch.op("scalar", lambda e, kt=kt, pi=pi: e.activation(
                                pbuf[:, kt * QPW:(kt + 1) * QPW], ps_s2[pi][:, :], AF.Exp),
                                reads=[("ps_s2", pi)], writes=[("pbuf", kt)], tag="exp")
                        for qh_i in range(QPW // 512):
                            q0 = qp * QPW + qh_i * 512
                            po = (qp * (QPW // 512) + qh_i) % 2
                            for kt in range(QT):
                                sch.op("tensor", lambda e, kt=kt, h=h, qh_i=qh_i, po=po: e.matmul(
                                    ps_o[po][0:65, :],
                                    v_sb[:, kt * VBLK + h * 65: kt * VBLK + h * 65 + 65],
                                    pbuf[:, kt * QPW + qh_i * 512:
                                         kt * QPW + qh_i * 512 + 512],
                                    start=(kt == 0), stop=(kt == QT - 1)),
                                    reads=[("v_sb", kt, h // 8), "v_ones", ("pbuf", kt)],
                                    writes=[("ps_o", po)], tag="pvmm")
                            sch.op("vector", lambda e, po=po: e.reciprocal(
                                rden[0:1, :], ps_o[po][64:65, :]),
                                reads=[("ps_o", po)], writes=["rden"], tag="rden")
                            sch.op("tensor", lambda e: e.matmul(
                                ps_r[0:64, :], ones1_32[0:1, 0:64], rden[0:1, :],
                                start=True, stop=True),
                                reads=["ones132", "rden"], writes=["ps_r"], tag="rbmm")
                            sch.op("scalar", lambda e: e.copy(rdenb[:, :], ps_r[0:64, :]),
                                   reads=["ps_r"], writes=["rdenb"], tag="rbc")
                            ocol = (h // 2) * T + q0
                            orow = (h % 2) * 64
                            okey = ("obuf", h, q0)
                            obuf_keys.append(okey)
                            sch.op("vector", lambda e, ocol=ocol, orow=orow, po=po: e.tensor_mul(
                                obuf[orow:orow + 64, ocol:ocol + 512],
                                ps_o[po][0:64, :], rdenb[:, :]),
                                reads=[("ps_o", po), "rdenb"],
                                writes=[okey], tag="onorm")
                sch.barrier("endattn")

            emit_proj_residual_ln(mk, s, obuf, obuf_keys, wproj16,
                                  [("wproj16", k) for k in range(KT)],
                                  bproj_col, "bproj_col32", xin, x1_out)
            sch.barrier("endproj")


def emit_proj_residual_ln(mk, s, act16, act_keys, w16, w_keys, bias_col, biaskey,
                          xres_d, xout_d):
    nc, sch = mk.nc, mk.sch
    T, TT = mk.T, mk.TT
    KT = 8
    NG = T // 512
    with ExitStack() as es2:
        xstg = [es2.enter_context(nc.sbuf_tensor(mk.name("xstg"), [128, 512], F32))
                for _ in range(2)]
        xsum = es2.enter_context(nc.sbuf_tensor(mk.name("xsum"), [128, 8 * 512], F32))
        xs16 = es2.enter_context(nc.sbuf_tensor(mk.name("xs16"), [128, 8 * 512], BF16))
        xsq16 = es2.enter_context(nc.sbuf_tensor(mk.name("xsq16"), [128, 8 * 512], BF16))
        x1r = es2.enter_context(nc.sbuf_tensor(mk.name("x1r"), [128, 8 * 512], F32))
        st_mu = es2.enter_context(nc.sbuf_tensor(mk.name("st_mu"), [1, 512], F32))
        st_a = es2.enter_context(nc.sbuf_tensor(mk.name("st_a"), [1, 512], F32))
        st_t = es2.enter_context(nc.sbuf_tensor(mk.name("st_t"), [1, 512], F32))
        st_c = es2.enter_context(nc.sbuf_tensor(mk.name("st_c"), [1, 512], F32))
        ps_p = [es2.enter_context(nc.psum_tensor(mk.name("ps_p"), [128, 512], F32))
                for _ in range(2)]
        ps_sum = es2.enter_context(nc.psum_tensor(mk.name("ps_sum"), [1, 512], F32))
        ps_sq = es2.enter_context(nc.psum_tensor(mk.name("ps_sq"), [1, 512], F32))
        ps_A = es2.enter_context(nc.psum_tensor(mk.name("ps_A"), [128, 512], F32))
        ps_C = es2.enter_context(nc.psum_tensor(mk.name("ps_C"), [128, 512], F32))
        ones128 = es2.enter_context(nc.sbuf_tensor(mk.name("o128"), [128, 1], BF16))
        ones1r = es2.enter_context(nc.sbuf_tensor(mk.name("o1r"), [1, 128], F32))
        eps_ln = es2.enter_context(nc.sbuf_tensor(mk.name("epsln"), [1, 1], F32))
        sch.op("vector", lambda e: e.memset(ones128[:, :], 1.0), writes=["po128"])
        sch.op("vector", lambda e: e.memset(ones1r[0:1, :], 1.0), writes=["po1r"])
        sch.op("vector", lambda e: e.memset(eps_ln[0:1, 0:1], 1e-5), writes=["peps"])

        for g in range(NG):
            tok0 = s * T + g * 512
            ltok = g * 512
            for mt in range(KT):
                pi = mt % 2
                xi = mt % 2
                sch.dma("gpsimd", xstg[xi][:, :],
                        dr_ap(xres_d, mt * 128, 128, [[1, 512]], col0=tok0),
                        writes=[("xstg", xi)], group=f"xstg{xi}", tag="xld")
                for k in range(KT):
                    sch.op("tensor", lambda e, mt=mt, k=k, pi=pi, ltok=ltok: e.matmul(
                        ps_p[pi][:, :],
                        w16[:, k * 1024 + mt * 128: k * 1024 + mt * 128 + 128],
                        act16[:, k * T + ltok: k * T + ltok + 512],
                        start=(k == 0), stop=(k == KT - 1)),
                        reads=[w_keys[k]] + act_keys, writes=[("ps_p", pi)], tag="pmm")
                sch.op("vector", lambda e, mt=mt, pi=pi, xi=xi: e.scalar_tensor_tensor(
                    xsum[:, mt * 512:(mt + 1) * 512], ps_p[pi][:, :],
                    bias_col[:, mt:mt + 1], xstg[xi][:, :], ALU.add, ALU.add),
                    reads=[("ps_p", pi), biaskey, ("xstg", xi)],
                    writes=[("xsum", mt)], tag="xsum")
                sch.op("gpsimd", lambda e, mt=mt: e.tensor_copy(
                    out=xs16[:, mt * 512:(mt + 1) * 512],
                    in_=xsum[:, mt * 512:(mt + 1) * 512]),
                    reads=[("xsum", mt)], writes=[("xs16", mt)], tag="xs16")
                sch.op("gpsimd", lambda e, mt=mt: e.tensor_mul(
                    xsq16[:, mt * 512:(mt + 1) * 512],
                    xs16[:, mt * 512:(mt + 1) * 512], xs16[:, mt * 512:(mt + 1) * 512]),
                    reads=[("xs16", mt)], writes=[("xsq16", mt)], tag="xsq")
            for k in range(KT):
                sch.op("tensor", lambda e, k=k: e.matmul(
                    ps_sum[0:1, :], ones128[:, 0:1], xs16[:, k * 512:(k + 1) * 512],
                    start=(k == 0), stop=(k == KT - 1)),
                    reads=["po128", ("xs16", k)], writes=["ps_sum"], tag="summ")
                sch.op("tensor", lambda e, k=k: e.matmul(
                    ps_sq[0:1, :], ones128[:, 0:1], xsq16[:, k * 512:(k + 1) * 512],
                    start=(k == 0), stop=(k == KT - 1)),
                    reads=["po128", ("xsq16", k)], writes=["ps_sq"], tag="sqm")
            sch.op("vector", lambda e: e.tensor_scalar(
                st_mu[0:1, :], ps_sum[0:1, :], 1.0 / 1024.0, None, ALU.mult),
                reads=["ps_sum"], writes=["st_mu"], tag="mu")
            sch.op("vector", lambda e: e.tensor_mul(
                st_t[0:1, :], st_mu[0:1, :], st_mu[0:1, :]),
                reads=["st_mu"], writes=["st_mu2"], tag="mu2")
            sch.op("vector", lambda e: e.scalar_tensor_tensor(
                st_t[0:1, :], ps_sq[0:1, :], 1.0 / 1024.0, st_t[0:1, :],
                ALU.mult, ALU.subtract),
                reads=["ps_sq", "st_mu2"], writes=["st_var"], tag="var")
            sch.op("scalar", lambda e: e.activation(
                st_t[0:1, :], st_t[0:1, :], AF.Ln, bias=eps_ln[0:1, 0:1]),
                reads=["st_var", "peps"], writes=["st_sd"], tag="sd")
            sch.op("scalar", lambda e: e.activation(
                st_a[0:1, :], st_t[0:1, :], AF.Exp, scale=-0.5),
                reads=["st_sd"], writes=["st_a"], tag="rstd")
            sch.op("vector", lambda e: e.scalar_tensor_tensor(
                st_c[0:1, :], st_mu[0:1, :], -1.0, st_a[0:1, :], ALU.mult, ALU.mult),
                reads=["st_mu", "st_a"], writes=["st_c"], tag="stc")
            sch.op("tensor", lambda e: e.matmul(
                ps_A[:, :], ones1r[0:1, :], st_a[0:1, :], start=True, stop=True),
                reads=["po1r", "st_a"], writes=["ps_A"], tag="bA")
            sch.op("tensor", lambda e: e.matmul(
                ps_C[:, :], ones1r[0:1, :], st_c[0:1, :], start=True, stop=True),
                reads=["po1r", "st_c"], writes=["ps_C"], tag="bC")
            for mt in range(KT):
                sch.op("vector", lambda e, mt=mt: e.tensor_mul(
                    x1r[:, mt * 512:(mt + 1) * 512],
                    xsum[:, mt * 512:(mt + 1) * 512], ps_A[:, :]),
                    reads=[("xsum", mt), "ps_A"], writes=[("x1ra", mt)], tag="lnA")
                sch.op("vector", lambda e, mt=mt: e.tensor_add(
                    x1r[:, mt * 512:(mt + 1) * 512],
                    x1r[:, mt * 512:(mt + 1) * 512], ps_C[:, :]),
                    reads=[("x1ra", mt), "ps_C"], writes=[("x1r", mt)], tag="lnC")
            sch.dma("scalar",
                    dr_ap(xout_d, 0, 128, [[128 * TT, 8], [1, 512]], col0=tok0),
                    sb_ap(x1r, 0, 128, [[512, 8], [1, 512]]),
                    reads=[("x1r", mtt) for mtt in range(KT)],
                    writes=[("xout", s, g)], group="xout_st", tag="xoutst")


def emit_phaseB(mk, es, xin, wts, host, x2_out):
    nc, sch = mk.nc, mk.sch
    B, T, TT = mk.B, mk.T, mk.TT
    D = 1024
    KT = 8
    W = 512
    NW = T // W
    NG = T // 512

    xb = [nc.dram_tensor(mk.name("xb_d"), [D, TT], F32).ap() for _ in range(2)]
    mrow_d = nc.dram_tensor("mrowB_d", [8, 512], BF16).ap()

    def hconst_in(es_c, name, rows, cols, dt=BF16):
        t = es_c.enter_context(nc.sbuf_tensor(mk.name("hB_" + name), [rows, cols], dt))
        q = "gpsimd" if dt == BF16 else "sync"
        sch.dma(q, t[0:rows, 0:cols], host[name][0:rows, 0:cols],
                writes=["hB" + name, ("chain", "hcB", q)], group="hcB" + q)
        return t

    lnBg = hconst_in(es, "lnBg", 128, 8 * 2, F32)
    lnBb = hconst_in(es, "lnBb", 128, 8 * 2, F32)
    b1a = hconst_in(es, "b1a", 128, 22 * 2, F32)
    b1g = hconst_in(es, "b1g", 128, 22 * 2, F32)
    b2c = hconst_in(es, "b2c", 128, 8 * 2, F32)

    for l in range(2):
        xb_cur = xin if l == 0 else xb[1]
        xb_cur_keys = ([] if l == 0 else [("xbd", 1, s2, w2_, mt2)
                                          for s2 in range(B) for w2_ in range(NW)
                                          for mt2 in range(KT)])
        with ExitStack() as es_a:
            wqkv16 = es_a.enter_context(nc.sbuf_tensor(mk.name("wqkvB"),
                                                       [128, KT * 1536], BF16))
            cast_stream(mk, es_a, wts["wqkv"], l * 1024, D, 0, 1536, wqkv16,
                        "wqkvB" + str(l), out_col_stride=1536)
            wo16 = es_a.enter_context(nc.sbuf_tensor(mk.name("woB"),
                                                     [128, 4 * 1024], BF16))
            cast_stream(mk, es_a, wts["wo"], l * 512, 512, 0, 1024, wo16,
                        "woB" + str(l), out_col_stride=1024)
            wqk = "wqkvB" + str(l)
            wok = "woB" + str(l)
            hm8 = hconst_in(es_a, "hm8", 128, 4 * 8)
            ind8 = hconst_in(es_a, "ind8", 8, 4 * 128)
            prot = hconst_in(es_a, "prot", 128, 128)
            csq = hconst_in(es_a, "csq", 128, T)
            ssq = hconst_in(es_a, "ssq", 128, T)
            csk = hconst_in(es_a, "csk", 128, T)
            ssk = hconst_in(es_a, "ssk", 128, T)
            mcur = hconst_in(es_a, "mcur", 128, 4 * 512)
            mprev = hconst_in(es_a, "mprev", 128, 4 * 512)
            ones1_32 = hconst_in(es_a, "ones1f", 1, 128, F32)
            qs_col = hconst_in(es_a, "qs_col", 128, 4 * 2, F32)
            ks_col = hconst_in(es_a, "ks_col", 128, 4 * 2, F32)
            identB = hconst_in(es_a, "identB", 128, 128)
            o128B = hconst_in(es_a, "o128B", 128, 1)
            epsB24 = es_a.enter_context(nc.sbuf_tensor(mk.name("epsB24"),
                                                       [8, 1], F32))
            sch.op("vector", lambda e: e.memset(epsB24[0:8, 0:1], 1e-24),
                   writes=["epsB24c"], tag="epsB24c")

            h16 = es_a.enter_context(nc.sbuf_tensor(mk.name("h16B"),
                                                    [128, KT * 513], BF16))
            qraw = es_a.enter_context(nc.sbuf_tensor(mk.name("qrawB"),
                                                     [128, 4 * 512], BF16))
            kraw = es_a.enter_context(nc.sbuf_tensor(mk.name("krawB"),
                                                     [128, 4 * 512], BF16))
            sqr = [es_a.enter_context(nc.sbuf_tensor(mk.name("sqrB"),
                                                     [128, 512], BF16))
                   for _ in range(2)]
            qh = es_a.enter_context(nc.sbuf_tensor(mk.name("qhB"),
                                                   [128, 4 * 512], BF16))
            kh = [es_a.enter_context(nc.sbuf_tensor(mk.name("khB"),
                                                    [128, 4 * 512], BF16))
                  for _ in range(2)]
            v_sb = [es_a.enter_context(nc.sbuf_tensor(mk.name("vsbB"),
                                                      [128, 4 * 520], BF16))
                    for _ in range(2)]
            for i in range(2):
                sch.op("vector", lambda e, i=i: e.memset(
                    sb_ap(v_sb[i], 0, 128, [[520, 4], [65, 8]], col0=64), 1.0),
                    writes=[("vones", i)], tag="vones")
            rot = es_a.enter_context(nc.sbuf_tensor(mk.name("rotB"),
                                                    [128, 512], BF16))
            n8a = es_a.enter_context(nc.sbuf_tensor(mk.name("n8a"), [8, 512], F32))
            n8b = es_a.enter_context(nc.sbuf_tensor(mk.name("n8b"), [8, 512], F32))
            f8 = [es_a.enter_context(nc.sbuf_tensor(mk.name("f8"), [8, 512], BF16))
                  for _ in range(2)]
            kmax = es_a.enter_context(nc.sbuf_tensor(mk.name("kmax"), [8, 2], F32))
            mneg16 = es_a.enter_context(nc.sbuf_tensor(mk.name("mnegB"),
                                                       [8, 512], BF16))
            qaug2 = [es_a.enter_context(nc.sbuf_tensor(mk.name("qaugB"),
                                                       [128, 512], BF16))
                     for _ in range(2)]
            kaug = [es_a.enter_context(nc.sbuf_tensor(mk.name("kaugB"),
                                                      [128, 1024], BF16))
                    for _ in range(2)]
            pbuf2 = [es_a.enter_context(nc.sbuf_tensor(mk.name("pbufB"),
                                                       [128, 8 * 512], BF16))
                     for _ in range(2)]
            rden = es_a.enter_context(nc.sbuf_tensor(mk.name("rdenB"), [1, 512], F32))
            rdenb2 = [es_a.enter_context(nc.sbuf_tensor(mk.name("rdenbB"),
                                                        [64, 512], F32))
                      for _ in range(2)]
            rden2 = [es_a.enter_context(nc.sbuf_tensor(mk.name("rdenB2"),
                                                       [1, 512], F32))
                     for _ in range(2)]
            obuf = es_a.enter_context(nc.sbuf_tensor(mk.name("obufB"),
                                                     [128, 4 * 512], BF16))
            xstg = [es_a.enter_context(nc.sbuf_tensor(mk.name("xstgB"),
                                                      [128, 512], F32))
                    for _ in range(2)]
            xnew = es_a.enter_context(nc.sbuf_tensor(mk.name("xnewB"),
                                                     [128, 512], F32))
            psB = [es_a.enter_context(nc.psum_tensor(mk.name("psB"), [128, 512], F32))
                   for _ in range(6)]
            ps8 = [es_a.enter_context(nc.psum_tensor(mk.name("ps8"), [8, 512], F32))
                   for _ in range(2)]
            maxn = es_a.enter_context(nc.sbuf_tensor(mk.name("maxnB"), [128, 8], F32))
            rdiag = es_a.enter_context(nc.sbuf_tensor(mk.name("rdiagB"),
                                                      [128, 128], BF16))
            for i in range(2):
                sch.op("vector", lambda e, i=i: e.memset(kaug[i][64:65, :], 1.0),
                       writes=[("kaones", i), ("kaones", i, 1)], tag="kaonesB")

            for s in range(B):
                for w in range(NW):
                    tok0 = s * T + w * W
                    wb = w % 2
                    for j in range(KT):
                        if w == 0:
                            sch.dma("gpsimd", h16[:, j * 513 + 1: j * 513 + 513],
                                    dr_ap(xb_cur, j * 128, 128, [[1, 512]], col0=tok0),
                                    reads=xb_cur_keys,
                                    writes=[("h16", j), ("chain", "h16", j % 2)],
                                    group=f"h16_{j % 2}", tag="hld")
                            sch.op("vector", lambda e, j=j: e.memset(
                                h16[:, j * 513: j * 513 + 1], 0.0),
                                reads=[], writes=[("h16z", j)], tag="hz")
                        else:
                            sch.dma("gpsimd", h16[:, j * 513: j * 513 + 513],
                                    dr_ap(xb_cur, j * 128, 128, [[1, 513]],
                                          col0=tok0 - 1),
                                    reads=xb_cur_keys,
                                    writes=[("h16", j), ("h16z", j),
                                            ("chain", "h16", j % 2)],
                                    group=f"h16_{j % 2}", tag="hld")

                    def hsl(j):
                        if j < 4:
                            return h16[:, j * 513 + 1: j * 513 + 513]
                        return h16[:, j * 513: j * 513 + 512]

                    def hkeys(j):
                        return [("h16", j)] + ([("h16z", j)] if j >= 4 else [])

                    for mt in range(8):
                        isq = mt < 4
                        pi = mt % 2
                        for k in range(KT):
                            sch.op("tensor", lambda e, mt=mt, k=k, pi=pi: e.matmul(
                                psB[pi][:, :],
                                wqkv16[:, k * 1536 + mt * 128: k * 1536 + mt * 128 + 128],
                                hsl(k),
                                start=(k == 0), stop=(k == KT - 1)),
                                reads=[(wqk, k)] + hkeys(k),
                                writes=[("psB", pi)], tag="qkBmm")
                        dst, dkey = (qraw, "qraw") if isq else (kraw, "kraw")
                        ti = mt % 4
                        sch.op("vector", lambda e, dst=dst, ti=ti, pi=pi: e.tensor_copy(
                            out=dst[:, ti * 512:(ti + 1) * 512], in_=psB[pi][:, :]),
                            reads=[("psB", pi)], writes=[(dkey, ti)], tag="qkBevac")
                        sch.op("gpsimd", lambda e, dst=dst, ti=ti, pi=pi: e.tensor_mul(
                            sqr[pi][:, :], dst[:, ti * 512:(ti + 1) * 512],
                            dst[:, ti * 512:(ti + 1) * 512]),
                            reads=[(dkey, ti)], writes=[("sqrB", pi)], tag="qkBsq")
                        wu = 0 if isq else 1
                        sch.op("tensor", lambda e, wu=wu, ti=ti, pi=pi: e.matmul(
                            ps8[wu][0:8, :], hm8[:, ti * 8:(ti + 1) * 8],
                            sqr[pi][:, :], start=(ti == 0), stop=(ti == 3)),
                            reads=["hBhm8", ("sqrB", pi)], writes=[("ps8", wu)],
                            tag="ssBmm")
                    for wu, nm in ((0, "n8a"), (1, "n8b")):
                        t8 = n8a if wu == 0 else n8b
                        sch.op("scalar", lambda e, t8=t8, wu=wu: e.activation(
                            t8[:, :], ps8[wu][0:8, :], AF.Ln,
                            bias=epsB24[0:8, 0:1]),
                            reads=[("ps8", wu), "epsB24c"], writes=[(nm, 0)],
                            tag="fB1")
                        sch.op("scalar", lambda e, t8=t8, wu=wu: e.activation(
                            f8[wu][:, :], t8[:, :], AF.Exp, scale=-0.5),
                            reads=[(nm, 0)], writes=[(nm, 2)], tag="fB3")
                    for isq in (True, False):
                        t8, nm = (n8a, "n8a") if isq else (n8b, "n8b")
                        raw, rkey = (qraw, "qraw") if isq else (kraw, "kraw")
                        dst = qh if isq else kh[wb]
                        dkey = "qh" if isq else ("kh", wb)
                        scol, sckey = (qs_col, "hBqs_col") if isq else (ks_col, "hBks_col")
                        cs, cskey = (csq, "hBcsq") if isq else (csk, "hBcsk")
                        ss, sskey = (ssq, "hBssq") if isq else (ssk, "hBssk")
                        wu0 = 0 if isq else 1
                        for ti in range(4):
                            pi = ti % 2
                            sch.op("tensor", lambda e, ti=ti, wu0=wu0, pi=pi: e.matmul(
                                psB[pi][:, :], ind8[0:8, ti * 128:(ti + 1) * 128],
                                f8[wu0][:, :], start=True, stop=True),
                                reads=["hBind8", (nm, 2)], writes=[("psB", pi)],
                                tag="fbBmm")
                            sch.op("vector", lambda e, ti=ti, raw=raw, scol=scol,
                                   pi=pi: e.scalar_tensor_tensor(
                                sqr[pi][:, :], psB[pi][:, :],
                                scol[:, l * 4 + ti: l * 4 + ti + 1],
                                raw[:, ti * 512:(ti + 1) * 512], ALU.mult, ALU.mult),
                                reads=[("psB", pi), sckey, (rkey, ti)],
                                writes=[("sqrB", pi)], tag="nrmB")
                            sch.op("tensor", lambda e, pi=pi: e.matmul(
                                psB[2 + pi][:, :], prot[:, :], sqr[pi][:, :],
                                start=True, stop=True),
                                reads=["hBprot", ("sqrB", pi)],
                                writes=[("psB", 2 + pi)], tag="rotBmm")
                            wt0 = w * W
                            sch.op("vector", lambda e, ti=ti, cs=cs, wt0=wt0, pi=pi:
                                   e.tensor_mul(
                                       rot[:, :], sqr[pi][:, :],
                                       cs[:, wt0: wt0 + 512]),
                                   reads=[("sqrB", pi), cskey], writes=["rotB"],
                                   tag="rcs")
                            sch.op("vector", lambda e, ti=ti, ss=ss, wt0=wt0, dst=dst,
                                   pi=pi: e.tensor_mul(
                                       dst[:, ti * 512:(ti + 1) * 512],
                                       psB[2 + pi][:, :], ss[:, wt0: wt0 + 512]),
                                   reads=[("psB", 2 + pi), sskey],
                                   writes=[(dkey, ti, "a")], tag="rss")
                            sch.op("vector", lambda e, ti=ti, dst=dst: e.tensor_add(
                                dst[:, ti * 512:(ti + 1) * 512],
                                dst[:, ti * 512:(ti + 1) * 512], rot[:, :]),
                                reads=[(dkey, ti, "a"), "rotB"],
                                writes=[(dkey, ti)], tag="radd")
                    for ti in range(4):
                        pi = ti % 2
                        for k in range(KT):
                            sch.op("tensor", lambda e, ti=ti, k=k, pi=pi: e.matmul(
                                psB[pi][:, :],
                                hsl(k)[:, ti * 128: ti * 128 + 128],
                                wqkv16[:, k * 1536 + 1024: k * 1536 + 1536],
                                start=(k == 0), stop=(k == KT - 1)),
                                reads=[(wqk, k)] + hkeys(k),
                                writes=[("psB", pi)], tag="vBmm")
                        sch.op("vector", lambda e, ti=ti, wb=wb, pi=pi: e.tensor_copy(
                            out=sb_ap(v_sb[wb], 0, 128, [[65, 8], [1, 64]],
                                      col0=ti * 520),
                            in_=psB[pi][:, :]),
                            reads=[("psB", pi), ("vones", wb)],
                            writes=[("vsb", wb, ti)], tag="vBevac")
                    for h in range(8):
                        ht = h // 2
                        hr = (h % 2) * 64
                        hp = h % 2
                        pbuf = pbuf2[hp]
                        rden = rden2[hp]
                        rdenb = rdenb2[hp]
                        kts = range(8) if w > 0 else range(4, 8)
                        for kt in kts:
                            sc_t = (psB[2], psB[3], psB[5])[kt % 3]
                            sc_k = (("psB", 2), ("psB", 3), ("psB", 5))[kt % 3]
                            ksrc, skey = ((kh[1 - wb], ("kh", 1 - wb, ht))
                                          if kt < 4 else (kh[wb], ("kh", wb, ht)))
                            sch.op("tensor", lambda e, kt=kt, ksrc=ksrc, ht=ht,
                                   hr=hr, sc_t=sc_t: e.matmul(
                                sc_t[:, :],
                                ksrc[hr:hr + 64, ht * 512 + (kt % 4) * 128:
                                     ht * 512 + (kt % 4) * 128 + 128],
                                qh[hr:hr + 64, ht * 512:(ht + 1) * 512],
                                start=True, stop=True),
                                reads=[skey, ("qh", ht)],
                                writes=[sc_k], tag="sBmm")
                            sch.op("scalar", lambda e, kt=kt, sc_t=sc_t: e.activation(
                                pbuf[:, kt * 512:(kt + 1) * 512], sc_t[:, :],
                                AF.Exp),
                                reads=[sc_k], writes=[("pbufB", hp, kt, 0)],
                                tag="eBexp")
                            if kt % 2 == 1:
                                msk = mprev if kt < 4 else mcur
                                mkk = "hBmprev" if kt < 4 else "hBmcur"
                                mi = (kt - 1) % 4
                                sch.op("vector", lambda e, kt=kt, msk=msk, mi=mi:
                                       e.tensor_mul(
                                           pbuf[:, (kt - 1) * 512:(kt + 1) * 512],
                                           pbuf[:, (kt - 1) * 512:(kt + 1) * 512],
                                           msk[:, mi * 512:(mi + 2) * 512]),
                                       reads=[("pbufB", hp, kt - 1, 0),
                                              ("pbufB", hp, kt, 0), mkk],
                                       writes=[("pbufB", hp, kt - 1),
                                               ("pbufB", hp, kt)], tag="eBmask")
                        for i, kt in enumerate(kts):
                            vs = v_sb[1 - wb] if kt < 4 else v_sb[wb]
                            vkey = ("vsb", 1 - wb, kt % 4) if kt < 4 else \
                                ("vsb", wb, kt % 4)
                            vok = ("vones", 1 - wb) if kt < 4 else ("vones", wb)
                            sch.op("tensor", lambda e, kt=kt, vs=vs, h=h, i=i,
                                   n=len(kts): e.matmul(
                                psB[0 if hp == 0 else 4][0:65, :],
                                vs[:, (kt % 4) * 520 + h * 65:
                                   (kt % 4) * 520 + h * 65 + 65],
                                pbuf[:, kt * 512:(kt + 1) * 512],
                                start=(i == 0), stop=(i == n - 1)),
                                reads=[vkey, vok, ("pbufB", hp, kt)],
                                writes=[("psB", 0 if hp == 0 else 4)], tag="pvB")
                        pv_b = 0 if hp == 0 else 4
                        rb_b = 1 if hp == 0 else 5
                        sch.op("scalar", lambda e, pv_b=pv_b: e.activation(
                            rden[0:1, :], psB[pv_b][64:65, :], AF.Ln),
                            reads=[("psB", pv_b)], writes=[("rdenB", hp, "ln")],
                            tag="rdBln")
                        sch.op("scalar", lambda e: e.activation(
                            rden[0:1, :], rden[0:1, :], AF.Exp, scale=-1.0),
                            reads=[("rdenB", hp, "ln")], writes=[("rdenB", hp)],
                            tag="rdB")
                        sch.op("tensor", lambda e, rb_b=rb_b: e.matmul(
                            psB[rb_b][0:64, :], ones1_32[0:1, 0:64], rden[0:1, :],
                            start=True, stop=True),
                            reads=["hBones1f", ("rdenB", hp)], writes=[("psB", rb_b)],
                            tag="rbB")
                        sch.op("scalar", lambda e, rb_b=rb_b: e.copy(
                            rdenb[:, :], psB[rb_b][0:64, :]),
                            reads=[("psB", rb_b)], writes=[("rdenbB", hp)], tag="rbcB")
                        sch.op("vector", lambda e, ht=ht, hr=hr, pv_b=pv_b: e.tensor_mul(
                            obuf[hr:hr + 64, ht * 512:(ht + 1) * 512],
                            psB[pv_b][0:64, :], rdenb[:, :]),
                            reads=[("psB", pv_b), ("rdenbB", hp)],
                            writes=[("obufB", h)], tag="onB")
                    for mt in range(KT):
                        pi = mt % 2
                        xi = mt % 2
                        sch.dma("sync", xstg[xi][:, :],
                                dr_ap(xb_cur, mt * 128, 128, [[1, 512]], col0=tok0),
                                reads=xb_cur_keys,
                                writes=[("xstgB", xi)], group=f"xstB{xi}", tag="xldB")
                        for k in range(4):
                            sch.op("tensor", lambda e, mt=mt, k=k, pi=pi: e.matmul(
                                psB[2 + pi][:, :],
                                wo16[:, k * 1024 + mt * 128: k * 1024 + mt * 128 + 128],
                                obuf[:, k * 512:(k + 1) * 512],
                                start=(k == 0), stop=(k == 3)),
                                reads=[(wok, k)] + [("obufB", hh) for hh in range(8)],
                                writes=[("psB", 2 + pi)], tag="woBmm")
                        sch.op("vector", lambda e, pi=pi, xi=xi: e.tensor_add(
                            xnew[:, :], psB[2 + pi][:, :], xstg[xi][:, :]),
                            reads=[("psB", 2 + pi), ("xstgB", xi)],
                            writes=["xnewB"], tag="xnB")
                        sch.dma("scalar",
                                dr_ap(xb[0], mt * 128, 128, [[1, 512]], col0=tok0),
                                xnew[:, :],
                                reads=["xnewB"], writes=[("xbd", 0, s, w, mt)],
                                group="xb0_st", tag="xstB")
            sch.barrier("endBattn")

        with ExitStack() as es_f:
            w116 = es_f.enter_context(nc.sbuf_tensor(mk.name("w1B"),
                                                     [128, KT * 5460], BF16))
            cast_stream(mk, es_f, wts["w1"], l * 1024, D, 0, 5460, w116,
                        "w1B" + str(l), out_col_stride=5460)
            w216 = es_f.enter_context(nc.sbuf_tensor(mk.name("w2B"),
                                                     [128, 22 * 1024], BF16))
            for kk in range(22):
                rows = 128 if kk < 21 else 42
                sch.dma("gpsimd", w216[0:rows, kk * 1024:(kk + 1) * 1024],
                        dr_ap(wts["w2"], l * 2730 + kk * 128, rows, [[1, 1024]]),
                        writes=[("w2B" + str(l), kk), ("chain", "w2B", kk % 2)],
                        group=f"w2B_{kk % 2}", tag="w2l")
            w1k = "w1B" + str(l)
            w2k = "w2B" + str(l)
            psF = [es_f.enter_context(nc.psum_tensor(mk.name("psFB"),
                                                     [128, 512], F32))
                   for _ in range(4)]
            ext = {"ps_sum": (psF[0], ("psFB", 0)), "ps_sq": (psF[1], ("psFB", 1)),
                   "ps_A": (psF[2], ("psFB", 2)), "ps_C": (psF[3], ("psFB", 3))}
            lnt = alloc_ln_tail(mk, es_f, ext_psums=ext, need_stage=False)
            hln16 = es_f.enter_context(nc.sbuf_tensor(mk.name("hlnB"),
                                                      [128, KT * 512], BF16))
            gg16 = [es_f.enter_context(nc.sbuf_tensor(mk.name("ggB"),
                                                      [128, 512], BF16))
                    for _ in range(2)]
            hh16 = es_f.enter_context(nc.sbuf_tensor(mk.name("hhB"),
                                                     [128, 22 * 512], BF16))
            xstg2 = [es_f.enter_context(nc.sbuf_tensor(mk.name("xstg2B"),
                                                       [128, 512], F32))
                     for _ in range(2)]
            xnew2 = es_f.enter_context(nc.sbuf_tensor(mk.name("xnew2B"),
                                                      [128, 512], F32))

            for s in range(B):
                for w in range(NW):
                    tok0 = s * T + w * W
                    for mt in range(KT):
                        if mt < 4 or w > 0:
                            c0 = tok0 if mt < 4 else tok0 - 1
                            sch.dma("sync", lnt["xsum"][:, mt * 512:(mt + 1) * 512],
                                    dr_ap(xb[0], mt * 128, 128, [[1, 512]], col0=c0),
                                    reads=[("xbd", 0, s, w2_, mt2)
                                           for w2_ in range(NW)
                                           for mt2 in range(KT)],
                                    writes=[(lnt["kid"], "pre", mt),
                                            ("chain", "lnB", mt % 2)],
                                    group=f"lnB{mt % 2}", tag="lnBi")
                        else:
                            sch.dma("sync",
                                    lnt["xsum"][:, mt * 512 + 1:(mt + 1) * 512],
                                    dr_ap(xb[0], mt * 128, 128, [[1, 511]],
                                          col0=tok0),
                                    reads=[("xbd", 0, s, w2_, mt2)
                                           for w2_ in range(NW)
                                           for mt2 in range(KT)],
                                    writes=[(lnt["kid"], "pre", mt, "d"),
                                            ("chain", "lnB", mt % 2)],
                                    group=f"lnB{mt % 2}", tag="lnBi0")
                            sch.op("vector", lambda e, mt=mt: e.memset(
                                lnt["xsum"][:, mt * 512: mt * 512 + 1], 0.0),
                                reads=[(lnt["kid"], "pre", mt, "d")],
                                writes=[(lnt["kid"], "pre", mt)], tag="lnBz")
                    emit_ln_gen(mk, lnt, f"BL{l}", [],
                                [("sbuf", lambda mt: (
                                    hln16[:, mt * 512:(mt + 1) * 512],
                                    ("hlnB", mt), []))],
                                gamma=lnBg[:, l * 8: l * 8 + 8],
                                beta=lnBb[:, l * 8: l * 8 + 8],
                                gkeys=["hBlnBg", "hBlnBb"], pre_filled=True)
                    for i in range(22):
                        mw = 128 if i < 21 else 42
                        pi = i % 2
                        for k in range(KT):
                            sch.op("tensor", lambda e, i=i, k=k, mw=mw, pi=pi: e.matmul(
                                psF[2 + pi][0:mw, :],
                                w116[:, k * 5460 + 2730 + i * 128:
                                     k * 5460 + 2730 + i * 128 + mw],
                                hln16[:, k * 512:(k + 1) * 512],
                                start=(k == 0), stop=(k == KT - 1)),
                                reads=[(w1k, k)] + [("hlnB", kk2) for kk2 in range(KT)],
                                writes=[("psFB", 2 + pi)], tag="gBmm")
                        sch.op("scalar", lambda e, i=i, mw=mw, pi=pi: e.activation(
                            gg16[pi][0:mw, :], psF[2 + pi][0:mw, :], AF.Gelu,
                            bias=b1g[0:mw, l * 22 + i: l * 22 + i + 1]),
                            reads=[("psFB", 2 + pi), "hBb1g"],
                            writes=[("ggB", pi)], tag="gBgelu")
                        for k in range(KT):
                            sch.op("tensor", lambda e, i=i, k=k, mw=mw, pi=pi: e.matmul(
                                psF[2 + pi][0:mw, :],
                                w116[:, k * 5460 + i * 128: k * 5460 + i * 128 + mw],
                                hln16[:, k * 512:(k + 1) * 512],
                                start=(k == 0), stop=(k == KT - 1)),
                                reads=[(w1k, k)] + [("hlnB", kk2) for kk2 in range(KT)],
                                writes=[("psFB", 2 + pi)], tag="aBmm")
                        sch.op("vector", lambda e, i=i, mw=mw, pi=pi:
                               e.scalar_tensor_tensor(
                                   hh16[0:mw, i * 512:(i + 1) * 512],
                                   psF[2 + pi][0:mw, :],
                                   b1a[0:mw, l * 22 + i: l * 22 + i + 1],
                                   gg16[pi][0:mw, :], ALU.add, ALU.mult),
                               reads=[("psFB", 2 + pi), "hBb1a", ("ggB", pi)],
                               writes=[("hhB", i)], tag="hB")
                    for mt in range(KT):
                        pi = mt % 2
                        xi = mt % 2
                        sch.dma("sync", xstg2[xi][:, :],
                                dr_ap(xb[0], mt * 128, 128, [[1, 512]], col0=tok0),
                                reads=[("xbd", 0, s, w, mt)],
                                writes=[("xstg2B", xi)], group=f"xs2B{xi}",
                                tag="xldB2")
                        for kk in range(22):
                            kw = 128 if kk < 21 else 42
                            sch.op("tensor", lambda e, mt=mt, kk=kk, kw=kw, pi=pi:
                                   e.matmul(
                                psF[2 + pi][:, :],
                                w216[0:kw, kk * 1024 + mt * 128:
                                     kk * 1024 + mt * 128 + 128],
                                hh16[0:kw, kk * 512:(kk + 1) * 512],
                                start=(kk == 0), stop=(kk == 21)),
                                reads=[(w2k, kk), ("hhB", kk)],
                                writes=[("psFB", 2 + pi)], tag="w2Bmm")
                        sch.op("vector", lambda e, mt=mt, pi=pi, xi=xi:
                               e.scalar_tensor_tensor(
                                   xnew2[:, :], psF[2 + pi][:, :],
                                   b2c[:, l * 8 + mt: l * 8 + mt + 1],
                                   xstg2[xi][:, :], ALU.add, ALU.add),
                               reads=[("psFB", 2 + pi), "hBb2c", ("xstg2B", xi)],
                               writes=["xnew2B"], tag="xnB2")
                        sch.dma("scalar",
                                dr_ap(xb[1], mt * 128, 128, [[1, 512]], col0=tok0),
                                xnew2[:, :],
                                reads=["xnew2B"], writes=[("xbd", 1, s, w, mt)],
                                group="xb1_st", tag="xstB2")
            sch.barrier("endBffn")

    with ExitStack() as es_l:
        lnt2 = alloc_ln_tail(mk, es_l)
        for s in range(B):
            for g in range(NG):
                tok0 = s * T + g * 512
                emit_ln_gen(mk, lnt2, "BF",
                            [(xb[1], tok0, [("xbd", 1, s, g, mt2)
                                            for mt2 in range(KT)]),
                             (xin, tok0, [])],
                            [("dram", x2_out, tok0)])
        sch.barrier("endB")


def emit_ln_gen(mk, d, name, inputs, sinks, gamma=None, beta=None, gkeys=(),
                pre_filled=False):
    sch, nc = mk.sch, mk.nc
    KT = 8
    xsum = d["xsum"]
    kid = d["kid"]
    for ii, (ap, tok0, rkeys) in enumerate(inputs):
        init = (not pre_filled) and ii == 0
        for mt in range(KT):
            xi = mt % 2
            if init:
                sch.dma("sync", xsum[:, mt * 512:(mt + 1) * 512],
                        dr_ap(ap, mt * 128, 128, [[1, 512]], col0=tok0),
                        reads=list(rkeys),
                        writes=[(kid, "pre", mt), ("chain", name, "xi", mt % 2)],
                        group=f"{name}xi{mt % 2}", tag="lnxi")
            else:
                sch.dma("sync", d["xstg"][xi][:, :],
                        dr_ap(ap, mt * 128, 128, [[1, 512]], col0=tok0),
                        reads=list(rkeys),
                        writes=[(kid, "xstg", xi)], group=f"{name}xs{xi}",
                        tag="lnxa")
                sch.op("vector", lambda e, mt=mt, xi=xi: e.tensor_add(
                    xsum[:, mt * 512:(mt + 1) * 512],
                    xsum[:, mt * 512:(mt + 1) * 512], d["xstg"][xi][:, :]),
                    reads=[(kid, "pre", mt), (kid, "xstg", xi)],
                    writes=[(kid, "pre", mt)], tag="lnadd")
    for mt in range(KT):
        xi = mt % 2
        sch.op("gpsimd", lambda e, mt=mt, xi=xi: e.tensor_copy(
            out=d["xs16"][xi][:, :], in_=xsum[:, mt * 512:(mt + 1) * 512]),
            reads=[(kid, "pre", mt)], writes=[(kid, "xs16", xi)], tag="xs16")
        sch.op("gpsimd", lambda e, mt=mt, xi=xi: e.tensor_mul(
            d["sqr"][xi][:, :], d["xs16"][xi][:, :], d["xs16"][xi][:, :]),
            reads=[(kid, "xs16", xi)], writes=[(kid, "sqr", xi)], tag="xsq")
        sch.op("tensor", lambda e, mt=mt, xi=xi: e.matmul(
            d["ps_sum"][0:1, :], d["o128"][:, 0:1], d["xs16"][xi][:, :],
            start=(mt == 0), stop=(mt == KT - 1)),
            reads=[d["k_o128"], (kid, "xs16", xi)], writes=[d["k_ps_sum"]],
            tag="summ")
        sch.op("tensor", lambda e, mt=mt, xi=xi: e.matmul(
            d["ps_sq"][0:1, :], d["o128"][:, 0:1], d["sqr"][xi][:, :],
            start=(mt == 0), stop=(mt == KT - 1)),
            reads=[d["k_o128"], (kid, "sqr", xi)], writes=[d["k_ps_sq"]],
            tag="sqm")
    sch.op("vector", lambda e: e.tensor_scalar(
        d["st_mu"][0:1, :], d["ps_sum"][0:1, :], 1.0 / 1024.0, None, ALU.mult),
        reads=[d["k_ps_sum"]], writes=[(kid, "st_mu")], tag="mu")
    sch.op("vector", lambda e: e.tensor_mul(
        d["st_t"][0:1, :], d["st_mu"][0:1, :], d["st_mu"][0:1, :]),
        reads=[(kid, "st_mu")], writes=[(kid, "st_mu2")], tag="mu2")
    sch.op("vector", lambda e: e.scalar_tensor_tensor(
        d["st_t"][0:1, :], d["ps_sq"][0:1, :], 1.0 / 1024.0, d["st_t"][0:1, :],
        ALU.mult, ALU.subtract),
        reads=[d["k_ps_sq"], (kid, "st_mu2")], writes=[(kid, "st_var")], tag="var")
    sch.op("scalar", lambda e: e.activation(
        d["st_t"][0:1, :], d["st_t"][0:1, :], AF.Ln, bias=d["eps"][0:1, 0:1]),
        reads=[(kid, "st_var"), d["k_eps"]], writes=[(kid, "st_sd")], tag="sd")
    sch.op("scalar", lambda e: e.activation(
        d["st_a"][0:1, :], d["st_t"][0:1, :], AF.Exp, scale=-0.5),
        reads=[(kid, "st_sd")], writes=[(kid, "st_a")], tag="rstd")
    sch.op("vector", lambda e: e.scalar_tensor_tensor(
        d["st_c"][0:1, :], d["st_mu"][0:1, :], -1.0, d["st_a"][0:1, :],
        ALU.mult, ALU.mult),
        reads=[(kid, "st_mu"), (kid, "st_a")], writes=[(kid, "st_c")], tag="stc")
    sch.op("tensor", lambda e: e.matmul(
        d["ps_A"][:, :], d["o1r"][0:1, :], d["st_a"][0:1, :], start=True, stop=True),
        reads=[d["k_o1r"], (kid, "st_a")], writes=[d["k_ps_A"]], tag="bA")
    sch.op("tensor", lambda e: e.matmul(
        d["ps_C"][:, :], d["o1r"][0:1, :], d["st_c"][0:1, :], start=True, stop=True),
        reads=[d["k_o1r"], (kid, "st_c")], writes=[d["k_ps_C"]], tag="bC")
    for mt in range(KT):
        xi = mt % 2
        sch.op("vector", lambda e, mt=mt, xi=xi: e.tensor_mul(
            d["x1r"][xi][:, :], xsum[:, mt * 512:(mt + 1) * 512], d["ps_A"][:, :]),
            reads=[(kid, "pre", mt), d["k_ps_A"]],
            writes=[(kid, "x1r", xi)], tag="lnA")
        sch.op("vector", lambda e, mt=mt, xi=xi: e.tensor_add(
            d["x1r"][xi][:, :], d["x1r"][xi][:, :], d["ps_C"][:, :]),
            reads=[(kid, "x1r", xi), d["k_ps_C"]],
            writes=[(kid, "x1r", xi)], tag="lnC")
        if gamma is not None:
            sch.op("vector", lambda e, mt=mt, xi=xi: e.tensor_scalar(
                d["x1r"][xi][:, :], d["x1r"][xi][:, :], gamma[:, mt:mt + 1],
                beta[:, mt:mt + 1], ALU.mult, ALU.add),
                reads=[(kid, "x1r", xi)] + list(gkeys),
                writes=[(kid, "x1r", xi)], tag="lngb")
        for sink in sinks:
            if sink[0] == "dram":
                _, ap, tok0s = sink
                sch.dma("scalar",
                        dr_ap(ap, mt * 128, 128, [[1, 512]], col0=tok0s),
                        d["x1r"][xi][:, :],
                        reads=[(kid, "x1r", xi)],
                        writes=[(name, "snk", tok0s, mt)], group=f"{name}xo{xi}",
                        tag="lnout")
            elif sink[0] == "dram16":
                _, ap, tok0s, stg = sink
                sch.op("vector", lambda e, xi=xi: e.tensor_copy(
                    out=stg[xi][:, :], in_=d["x1r"][xi][:, :]),
                    reads=[(kid, "x1r", xi)], writes=[(kid, "st16", xi)],
                    tag="ln16")
                sch.dma("scalar",
                        dr_ap(ap, mt * 128, 128, [[1, 512]], col0=tok0s),
                        stg[xi][:, :],
                        reads=[(kid, "st16", xi)],
                        writes=[(name, "snk", tok0s, mt)], group=f"{name}x6{xi}",
                        tag="lnout16")
            else:
                _, fn = sink
                out_ap, wkey, rk2 = fn(mt)
                sch.op("vector", lambda e, out_ap=out_ap, xi=xi: e.tensor_copy(
                    out=out_ap, in_=d["x1r"][xi][:, :]),
                    reads=[(kid, "x1r", xi)] + list(rk2),
                    writes=[wkey], tag="lnsb")


def emit_phaseC(mk, es, xin, wts, host, x3_out):
    nc, sch = mk.nc, mk.sch
    B, T, TT = mk.B, mk.T, mk.TT
    D, H = 1024, 16
    KT = 8
    CH = 256
    NCH = T // CH
    SUB = 128
    NSB = CH // SUB
    NG = T // 512

    xln0_d = nc.dram_tensor("xln0_d", [D, TT], F32).ap()
    xint_d = nc.dram_tensor("xint_d", [D, TT], F32).ap()
    xf_d = nc.dram_tensor("xf_d", [D, TT + B], BF16).ap()
    xa_d = nc.dram_tensor("xa_d", [D, TT + B], BF16).ap()

    def hconst(name, rows, cols, dt=BF16):
        t = es.enter_context(nc.sbuf_tensor(mk.name("hc_" + name), [rows, cols], dt))
        q = "gpsimd" if dt == BF16 else "sync"
        sch.dma(q, t[0:rows, 0:cols], host[name],
                writes=["hc" + name, ("chain", "hcC", q)], group="hcC" + q)
        return t

    hm16 = hconst("hm16", 128, 8 * 16)
    ind16p = hconst("ind16p", 16, 8 * 128)
    u_col = hconst("u_col", 128, 8, F32)
    ident = hconst("ident", 128, 128)
    u_inc = hconst("u_inc", 128, 128)
    u_str = hconst("u_str", 128, 128)
    u01s = hconst("u01s", 128, 128)
    maa = {nm: hconst("maa_" + nm, 128, 8, F32)
           for nm in ("x", "w", "k", "v", "r", "g", "fk", "fr")}
    lncol = {nm: hconst("ln_" + nm, 128, 8, F32)
             for nm in ("g0", "b0", "g1", "b1", "g2", "b2", "gx", "bx")}

    with ExitStack() as es_w:
        def wload(name, ap, din, dout):
            t = es_w.enter_context(nc.sbuf_tensor(mk.name("w_" + name),
                                                  [128, (din // 128) * dout], BF16))
            cast_stream(mk, es_w, ap, 0, din, 0, dout, t, "w" + name,
                        out_col_stride=dout)
            return t

        wr16 = wload("wr", wts["Wr"], D, 1024)
        wk16 = wload("wk", wts["Wk"], D, 1024)
        wv16 = wload("wv", wts["Wv"], D, 1024)
        wg16 = wload("wg", wts["Wg"], D, 1024)
        wo16 = wload("wo", wts["Wo"], D, 1024)
        tm1_16 = wload("tm1", wts["tm_w1"], D, 160)
        td1_16 = wload("td1", wts["td_w1"], D, 64)
        tm2t = [es_w.enter_context(nc.sbuf_tensor(mk.name("tm2t"),
                                                   [64, 1024], BF16))
                for _ in range(2)] + \
               [es_w.enter_context(nc.sbuf_tensor(mk.name("tm2t"), [32, 1024], BF16))]
        for f in range(5):
            tt_, r0 = (tm2t[f // 2], (f % 2) * 32) if f < 4 else (tm2t[2], 0)
            sch.dma("gpsimd", tt_[r0:r0 + 32, 0:1024],
                    dr_ap(wts["tm_w2"], f * 32, 32, [[1, 1024]]),
                    writes=[("tm2", f), ("chain", "tm2")], group="tm2_ld", tag="tm2")
        td2aug = es_w.enter_context(nc.sbuf_tensor("td2aug", [128, 1024], BF16))
        sch.dma("gpsimd", td2aug[0:64, :], dr_ap(wts["td_w2"], 0, 64, [[1, 1024]]),
                writes=["td2a"], group="td2_ld", tag="td2")
        sch.dma("gpsimd", td2aug[64:65, :], host["tdrow"],
                writes=["td2b"], group="td2b_ld", tag="td2b")

        for s in range(B):
            with ExitStack() as es_a:
                xa1s = [es_a.enter_context(
                    nc.sbuf_tensor(mk.name("xa1s"), [128, 512], BF16))
                    for _ in range(2)]
                xac = es_a.enter_context(
                    nc.sbuf_tensor(mk.name("xac"), [128, KT * (CH + 1)], BF16))
                zcola = es_a.enter_context(
                    nc.sbuf_tensor(mk.name("zcola"), [128, 8], BF16))
                soff = s * T + s
                sch.op("vector", lambda e: e.memset(zcola[:, :], 0.0),
                       writes=["zcola"], tag="zca")
                sch.dma("scalar",
                        dr_ap(xa_d, 0, 128, [[128 * (TT + B), 8], [1, 1]], col0=soff),
                        zcola[:, 0:8],
                        reads=["zcola"], writes=[("xaz", s)], group="xaz_st",
                        tag="xaz", slow_ok=True)
                S_all = es_a.enter_context(nc.sbuf_tensor(mk.name("S_all"),
                                                          [128, 8 * 64], F32))
                S16 = es_a.enter_context(nc.sbuf_tensor(mk.name("S16"),
                                                        [128, 8 * 64], BF16))
                sch.op("vector", lambda e: e.memset(S_all[:, :], 0.0),
                       writes=["S_all"], tag="S0")
                sch.op("vector", lambda e: e.memset(S16[:, :], 0.0),
                       writes=["S16"], tag="S016")
                psC = [es_a.enter_context(nc.psum_tensor(mk.name("psC"),
                                                         [128, 512], F32))
                       for _ in range(4)]
                psW = [es_a.enter_context(nc.psum_tensor(mk.name("psW"),
                                                         [128, 1024], F32))
                       for _ in range(2)]
                ext = {"ps_sum": (psC[0], ("psC", 0)), "ps_sq": (psC[1], ("psC", 1)),
                       "ps_A": (psC[2], ("psC", 2)), "ps_C": (psC[3], ("psC", 3))}
                lnt = alloc_ln_tail(mk, es_a, ext_psums=ext, need_stage=False)

                for g in range(NG):
                    tok0 = s * T + g * 512
                    ltok = g * 512
                    emit_ln_gen(mk, lnt, "C0", [(xin, tok0, [])],
                                [("dram", xln0_d, tok0)],
                                gamma=lncol["g0"], beta=lncol["b0"],
                                gkeys=["hcln_g0", "hcln_b0"])
                    emit_ln_gen(mk, lnt, "C1",
                                [(xln0_d, tok0, [("C0", "snk", tok0, mt2)
                                                 for mt2 in range(KT)])],
                                [("dram16", xa_d, soff + 1 + ltok, xa1s)],
                                gamma=lncol["g1"], beta=lncol["b1"],
                                gkeys=["hcln_g1", "hcln_b1"])

                dx16 = es_a.enter_context(nc.sbuf_tensor(mk.name("dx16"),
                                                         [128, KT * CH], BF16))
                xmix = es_a.enter_context(nc.sbuf_tensor(mk.name("xmix"),
                                                         [128, KT * CH], BF16))
                rT = es_a.enter_context(nc.sbuf_tensor(mk.name("rT"),
                                                       [128, KT * CH], BF16))
                kT = es_a.enter_context(nc.sbuf_tensor(mk.name("kT"),
                                                       [128, KT * CH], BF16))
                g16 = es_a.enter_context(nc.sbuf_tensor(mk.name("g16"),
                                                        [128, KT * CH], BF16))
                ktm = es_a.enter_context(nc.sbuf_tensor(mk.name("ktm"),
                                                        [128, NSB * 1024], BF16))
                vtm = es_a.enter_context(nc.sbuf_tensor(mk.name("vtm"),
                                                        [128, NSB * 1024], BF16))
                eh16 = es_a.enter_context(nc.sbuf_tensor(mk.name("eh16"),
                                                         [128, NSB * 1024], BF16))
                el16 = es_a.enter_context(nc.sbuf_tensor(mk.name("el16"),
                                                         [128, NSB * 1024], BF16))
                e32 = es_a.enter_context(nc.sbuf_tensor(mk.name("e32"),
                                                        [128, 512], F32))
                att16 = es_a.enter_context(nc.sbuf_tensor(mk.name("att16"),
                                                          [128, KT * CH], BF16))
                htd = es_a.enter_context(nc.sbuf_tensor(mk.name("htd"),
                                                        [128, CH], BF16))
                sch.op("vector", lambda e: e.memset(htd[64:65, :], 1.0),
                       writes=["htd_ones"], tag="htdo")
                ebv = es_a.enter_context(nc.sbuf_tensor(mk.name("ebv"), [128, 1024], BF16))
                ebvf = es_a.enter_context(nc.sbuf_tensor(mk.name("ebvf"), [128, 1024], BF16))
                ea16 = es_a.enter_context(nc.sbuf_tensor(mk.name("ea16"), [128, 1024], BF16))
                al16 = es_a.enter_context(nc.sbuf_tensor(mk.name("al16"), [128, 8], F32))
                rtf = es_a.enter_context(nc.sbuf_tensor(mk.name("rtf"), [128, 1024], BF16))
                ktf = es_a.enter_context(nc.sbuf_tensor(mk.name("ktf"), [128, 1024], BF16))
                ruf = es_a.enter_context(nc.sbuf_tensor(mk.name("ruf"), [128, 1024], BF16))
                ktmf = es_a.enter_context(nc.sbuf_tensor(mk.name("ktmf"), [128, 1024], BF16))
                p2b = [es_a.enter_context(nc.sbuf_tensor(mk.name("p2b"), [128, 128], BF16))
                       for _ in range(2)]
                t3b2 = [es_a.enter_context(nc.sbuf_tensor(mk.name("t3b"), [128, 128], BF16))
                        for _ in range(2)]
                o16 = es_a.enter_context(nc.sbuf_tensor(mk.name("o16"), [128, 1024], BF16))
                osq = es_a.enter_context(nc.sbuf_tensor(mk.name("osq"), [128, 1024], BF16))
                d16s = es_a.enter_context(nc.sbuf_tensor(mk.name("d16s"), [16, 512], F32))
                eps16 = es_a.enter_context(nc.sbuf_tensor(mk.name("eps16"), [16, 1], F32))
                sch.op("vector", lambda e: e.memset(eps16[:, :], 1e-5),
                       writes=["eps16"], tag="eps16")
                gab16 = es_a.enter_context(nc.sbuf_tensor(mk.name("gab16"), [16, 256], BF16))
                onrm = [es_a.enter_context(nc.sbuf_tensor(mk.name("onrm"),
                                                          [128, 128], BF16))
                        for _ in range(2)]
                lorA = es_a.enter_context(nc.sbuf_tensor(mk.name("lorA"), [64, CH], BF16))
                lorB = es_a.enter_context(nc.sbuf_tensor(mk.name("lorB"), [64, CH], BF16))
                lorC = es_a.enter_context(nc.sbuf_tensor(mk.name("lorC"), [32, CH], BF16))
                xl0t = [es_a.enter_context(nc.sbuf_tensor(mk.name("xl0t"),
                                                          [128, CH], F32))
                        for _ in range(2)]
                xintt = es_a.enter_context(nc.sbuf_tensor(mk.name("xintt"),
                                                          [128, CH], F32))

                for c in range(NCH):
                    c0 = c * CH
                    for j in range(KT):
                        sch.dma("sync", xac[:, j * (CH + 1): (j + 1) * (CH + 1)],
                                dr_ap(xa_d, j * 128, 128, [[1, CH + 1]],
                                      col0=soff + c0),
                                reads=[("C1", "snk", soff + 1 + g2 * 512, mt2)
                                       for g2 in range(NG) for mt2 in range(KT)] +
                                      [("xaz", s)],
                                writes=[("xac", j), ("chain", "xac", j % 2)],
                                group=f"xac{j % 2}", tag="xacl")
                    for j in range(KT):
                        base = j * (CH + 1)
                        sch.op("vector", lambda e, j=j, base=base: e.tensor_sub(
                            dx16[:, j * CH:(j + 1) * CH],
                            xac[:, base: base + CH],
                            xac[:, base + 1: base + 1 + CH]),
                            reads=[("xac", j)],
                            writes=[("dx16", j)], tag="dx")
                    for j in range(KT):
                        base = j * (CH + 1)
                        sch.op("vector", lambda e, j=j, base=base: e.scalar_tensor_tensor(
                            xmix[:, j * CH:(j + 1) * CH], dx16[:, j * CH:(j + 1) * CH],
                            maa["x"][:, j:j + 1], xac[:, base + 1: base + 1 + CH],
                            ALU.mult, ALU.add),
                            reads=[("dx16", j), "hcmaa_x", ("xac", j)],
                            writes=[("xmix", j)], tag="xxx")
                    for mi, (lor, lkey, m0, mw) in enumerate(
                            ((lorA, "lorA", 0, 64), (lorB, "lorB", 64, 64),
                             (lorC, "lorC", 128, 32))):
                        pi = mi % 2
                        for k in range(KT):
                            sch.op("tensor", lambda e, k=k, m0=m0, mw=mw, pi=pi: e.matmul(
                                psC[pi][0:mw, 0:CH],
                                tm1_16[:, k * 160 + m0: k * 160 + m0 + mw],
                                xmix[:, k * CH:(k + 1) * CH],
                                start=(k == 0), stop=(k == KT - 1)),
                                reads=[("wtm1", k)] + [("xmix", kk) for kk in range(KT)],
                                writes=[("psC", pi)], tag="loramm")
                        sch.op("scalar", lambda e, lor=lor, mw=mw, pi=pi: e.activation(
                            lor[0:mw, 0:CH], psC[pi][0:mw, 0:CH], AF.Tanh),
                            reads=[("psC", pi)], writes=[lkey], tag="tanh")
                    for fname, f in (("w", 0), ("k", 1), ("v", 2), ("r", 3), ("g", 4)):
                        ltensor, lrow, lkey = (
                            (lorA, 0, "lorA"), (lorA, 32, "lorA"), (lorB, 0, "lorB"),
                            (lorB, 32, "lorB"), (lorC, 0, "lorC"))[f]
                        for j in range(KT):
                            pi = j % 2
                            tt_, r0 = (tm2t[f // 2], (f % 2) * 32) if f < 4 \
                                else (tm2t[2], 0)
                            sch.op("tensor", lambda e, j=j, pi=pi, tt_=tt_, r0=r0,
                                   ltensor=ltensor, lrow=lrow: e.matmul(
                                psC[2 + pi][:, 0:CH],
                                tt_[r0:r0 + 32, j * 128: j * 128 + 128],
                                ltensor[lrow:lrow + 32, 0:CH],
                                start=True, stop=True),
                                reads=[("tm2", f), lkey], writes=[("psC", 2 + pi)],
                                tag="memm")
                            sch.op("vector", lambda e, j=j, pi=pi, fname=fname:
                                   e.scalar_tensor_tensor(
                                       xmix[:, j * CH:(j + 1) * CH],
                                       psC[2 + pi][:, 0:CH], maa[fname][:, j:j + 1],
                                       dx16[:, j * CH:(j + 1) * CH],
                                       ALU.add, ALU.mult),
                                   reads=[("psC", 2 + pi), "hcmaa_" + fname,
                                          ("dx16", j)],
                                   writes=[("xmix", j)], tag="xfm")
                            base = j * (CH + 1)
                            sch.op("vector", lambda e, j=j, base=base: e.tensor_add(
                                xmix[:, j * CH:(j + 1) * CH],
                                xmix[:, j * CH:(j + 1) * CH],
                                xac[:, base + 1: base + 1 + CH]),
                                reads=[("xmix", j), ("xac", j)],
                                writes=[("xmix", j)], tag="xfa")
                        if fname == "w":
                            for k in range(KT):
                                sch.op("tensor", lambda e, k=k: e.matmul(
                                    psC[0][0:64, 0:CH],
                                    td1_16[:, k * 64:(k + 1) * 64],
                                    xmix[:, k * CH:(k + 1) * CH],
                                    start=(k == 0), stop=(k == KT - 1)),
                                    reads=[("wtd1", k)] + [("xmix", kk) for kk in range(KT)],
                                    writes=[("psC", 0)], tag="tdmm")
                            sch.op("scalar", lambda e: e.activation(
                                htd[0:64, 0:CH], psC[0][0:64, 0:CH], AF.Tanh),
                                reads=[("psC", 0)], writes=["htd"], tag="tdtanh")
                            for tl in range(NSB):
                                for nn in range(2):
                                    sch.op("tensor", lambda e, tl=tl, nn=nn: e.matmul(
                                        psW[0][:, nn * 512:(nn + 1) * 512],
                                        htd[0:65, tl * 128:(tl + 1) * 128],
                                        td2aug[0:65, nn * 512:(nn + 1) * 512],
                                        start=True, stop=True),
                                        reads=["htd", "htd_ones", "td2a", "td2b"],
                                        writes=[("psW", 0)], tag="wrawmm")
                                for hh2 in range(2):
                                    sch.op("scalar", lambda e, hh2=hh2: e.activation(
                                        e32[:, :], psW[0][:, hh2 * 512:(hh2 + 1) * 512],
                                        AF.Exp),
                                        reads=[("psW", 0)], writes=["e32"], tag="expw")
                                    c0e = tl * 1024 + hh2 * 512
                                    sch.op("gpsimd", lambda e, c0e=c0e: e.tensor_copy(
                                        out=eh16[:, c0e: c0e + 512], in_=e32[:, :]),
                                        reads=["e32"], writes=[("eh16", tl)], tag="ehi")
                                    sch.op("vector", lambda e, c0e=c0e: e.tensor_sub(
                                        el16[:, c0e: c0e + 512], e32[:, :],
                                        eh16[:, c0e: c0e + 512]),
                                        reads=["e32", ("eh16", tl)],
                                        writes=[("el16", tl)], tag="elo")
                        else:
                            wtile, wkey = {"k": (wk16, "wwk"), "v": (wv16, "wwv"),
                                           "r": (wr16, "wwr"), "g": (wg16, "wwg")}[fname]
                            if fname in ("r", "k", "g"):
                                dst = {"r": rT, "k": kT, "g": g16}[fname]
                                dkey = {"r": "rT", "k": "kT", "g": "g16"}[fname]
                                for mt in range(KT):
                                    pi = mt % 2
                                    for k in range(KT):
                                        sch.op("tensor", lambda e, mt=mt, k=k, pi=pi,
                                               wtile=wtile: e.matmul(
                                            psC[pi][:, 0:CH],
                                            wtile[:, k * 1024 + mt * 128:
                                                  k * 1024 + mt * 128 + 128],
                                            xmix[:, k * CH:(k + 1) * CH],
                                            start=(k == 0), stop=(k == KT - 1)),
                                            reads=[(wkey, k)] + [("xmix", kk)
                                                            for kk in range(KT)],
                                            writes=[("psC", pi)], tag="prmm")
                                    if fname == "g":
                                        sch.op("scalar", lambda e, mt=mt, pi=pi:
                                               e.activation(
                                                   dst[:, mt * CH:(mt + 1) * CH],
                                                   psC[pi][:, 0:CH], AF.Silu),
                                               reads=[("psC", pi)],
                                               writes=[(dkey, mt)], tag="pevacg")
                                    else:
                                        sch.op("vector", lambda e, mt=mt, pi=pi,
                                               dst=dst: e.tensor_copy(
                                                   out=dst[:, mt * CH:(mt + 1) * CH],
                                                   in_=psC[pi][:, 0:CH]),
                                               reads=[("psC", pi)],
                                               writes=[(dkey, mt)], tag="pevac")
                            if fname in ("k", "v"):
                                dst = ktm if fname == "k" else vtm
                                dkey = "ktm" if fname == "k" else "vtm"
                                for tl in range(NSB):
                                    for nn in range(2):
                                        pi = (tl * 2 + nn) % 2
                                        for k in range(KT):
                                            sch.op("tensor", lambda e, tl=tl, nn=nn,
                                                   k=k, pi=pi, wtile=wtile: e.matmul(
                                                psC[2 + pi][:, :],
                                                xmix[:, k * CH + tl * 128:
                                                     k * CH + tl * 128 + 128],
                                                wtile[:, k * 1024 + nn * 512:
                                                      k * 1024 + nn * 512 + 512],
                                                start=(k == 0), stop=(k == KT - 1)),
                                                reads=[(wkey, k)] + [("xmix", kk)
                                                                for kk in range(KT)],
                                                writes=[("psC", 2 + pi)], tag="ptmm")
                                        sch.op("vector", lambda e, tl=tl, nn=nn,
                                               pi=pi, dst=dst: e.tensor_copy(
                                                   out=dst[:, tl * 1024 + nn * 512:
                                                           tl * 1024 + nn * 512 + 512],
                                                   in_=psC[2 + pi][:, :]),
                                               reads=[("psC", 2 + pi)],
                                               writes=[(dkey, tl, nn)], tag="ptevac")
                    for sb in range(NSB):
                        for nn in range(2):
                            for ho in range(2):
                                sch.op("tensor", lambda e, sb=sb, ho=ho, nn=nn: e.matmul(
                                    psW[1][:, nn * 512:(nn + 1) * 512],
                                    u_inc[:, :],
                                    (eh16 if ho == 0 else el16)[
                                        :, sb * 1024 + nn * 512:
                                        sb * 1024 + nn * 512 + 512],
                                    start=(ho == 0), stop=(ho == 1)),
                                    reads=["hcu_inc", ("eh16", sb), ("el16", sb)],
                                    writes=[("psW", 1)], tag="cbtm")
                        sch.op("scalar", lambda e: e.activation(
                            ebv[:, :], psW[1][:, :], AF.Exp, scale=-1.0),
                            reads=[("psW", 1)], writes=["ebv"], tag="ebv")
                        for ct in range(8):
                            for ho in range(2):
                                sch.op("tensor", lambda e, sb=sb, ct=ct, ho=ho: e.matmul(
                                    psW[0][:, ct * 128:(ct + 1) * 128],
                                    (eh16 if ho == 0 else el16)[
                                        :, sb * 1024 + ct * 128:
                                        sb * 1024 + ct * 128 + 128],
                                    u_str[:, :],
                                    start=(ho == 0), stop=(ho == 1)),
                                    reads=["hcu_str", ("eh16", sb), ("el16", sb)],
                                    writes=[("psW", 0)], tag="cafm")
                        sch.op("scalar", lambda e: e.activation(
                            ea16[:, :], psW[0][:, :], AF.Exp),
                            reads=[("psW", 0)], writes=["ea16"], tag="ea")
                        for ct in range(8):
                            for ho in range(2):
                                sch.op("tensor", lambda e, sb=sb, ct=ct, ho=ho: e.matmul(
                                    psW[1][:, ct * 128:(ct + 1) * 128],
                                    (eh16 if ho == 0 else el16)[
                                        :, sb * 1024 + ct * 128:
                                        sb * 1024 + ct * 128 + 128],
                                    u_inc[:, :],
                                    start=(ho == 0), stop=(ho == 1)),
                                    reads=["hcu_inc", ("eh16", sb), ("el16", sb)],
                                    writes=[("psW", 1)], tag="cbfm")
                        sch.op("scalar", lambda e: e.activation(
                            ebvf[:, :], psW[1][:, :], AF.Exp, scale=-1.0),
                            reads=[("psW", 1)], writes=["ebvf"], tag="ebvf")
                        sch.op("scalar", lambda e: e.activation(
                            al16[:, 0:8],
                            sb_ap(psW[1], 0, 128, [[128, 8]], col0=127),
                            AF.Exp),
                            reads=[("psW", 1)], writes=["al16"], tag="al")
                        for ct in range(8):
                            scol = ct * CH + sb * 128
                            sch.op("vector", lambda e, ct=ct, scol=scol: e.tensor_mul(
                                rtf[:, ct * 128:(ct + 1) * 128],
                                rT[:, scol:scol + 128],
                                ea16[:, ct * 128:(ct + 1) * 128]),
                                reads=[("rT", ct), "ea16"], writes=[("rtf", ct)],
                                tag="rtf")
                            sch.op("vector", lambda e, ct=ct, scol=scol: e.tensor_mul(
                                ktf[:, ct * 128:(ct + 1) * 128],
                                kT[:, scol:scol + 128],
                                ebvf[:, ct * 128:(ct + 1) * 128]),
                                reads=[("kT", ct), "ebvf"], writes=[("ktf", ct)],
                                tag="ktf")
                            sch.op("vector", lambda e, ct=ct, scol=scol: e.tensor_scalar(
                                ruf[:, ct * 128:(ct + 1) * 128],
                                rT[:, scol:scol + 128],
                                u_col[:, ct:ct + 1], None, ALU.mult),
                                reads=[("rT", ct), "hcu_col"], writes=[("ruf", ct)],
                                tag="ruf")
                        sch.op("vector", lambda e, sb=sb: e.tensor_mul(
                            ktmf[:, :], ktm[:, sb * 1024:(sb + 1) * 1024], ebv[:, :]),
                            reads=[("ktm", sb, 0), ("ktm", sb, 1), "ebv"],
                            writes=["ktmf"], tag="ktmf")
                        for h in range(H):
                            p = h // 2
                            rw = (h % 2) * 64
                            pb = h % 2
                            t3b = t3b2[pb]
                            if pb == 0:
                                pP, pPk = psC[2][0:128, 0:128], ("psC", 2)
                                pR, pRk = psC[3][0:128, 0:128], ("psC", 3)
                            else:
                                pP, pPk = psW[1][0:128, 0:128], ("psW", 1)
                                pR, pRk = psW[1][0:128, 512:640], ("psW", 1, "b")
                            sch.op("tensor", lambda e, p=p, rw=rw, pP=pP: e.matmul(
                                pP,
                                ktf[rw:rw + 64, p * 128:(p + 1) * 128],
                                rtf[rw:rw + 64, p * 128:(p + 1) * 128],
                                start=True, stop=True),
                                reads=[("ktf", p), ("rtf", p)], writes=[pPk],
                                tag="pmm")
                            sch.op("tensor", lambda e, p=p, rw=rw, sb=sb, pR=pR: e.matmul(
                                pR,
                                ruf[rw:rw + 64, p * 128:(p + 1) * 128],
                                kT[rw:rw + 64, p * CH + sb * 128:
                                   p * CH + sb * 128 + 128],
                                start=True, stop=True),
                                reads=[("ruf", p), ("kT", p)], writes=[pRk],
                                tag="rkmm")
                            sch.op("vector", lambda e, pb=pb, pP=pP: e.tensor_mul(
                                p2b[pb][:, :], pP, u01s[:, :]),
                                reads=[pPk, "hcu01s"], writes=[("p2b", pb)],
                                tag="pmask")
                            sch.op("vector", lambda e, t3b=t3b, pR=pR: e.tensor_mul(
                                t3b[:, :], pR, ident[:, :]),
                                reads=[pRk, "hcident"], writes=[("t3b", pb)],
                                tag="dmask")
                            sch.op("vector", lambda e, pb=pb, t3b=t3b: e.tensor_add(
                                p2b[pb][:, :], p2b[pb][:, :], t3b[:, :]),
                                reads=[("p2b", pb), ("t3b", pb)], writes=[("p2b", pb)],
                                tag="paddd")
                            sch.op("tensor", lambda e, p=p, rw=rw: e.matmul(
                                psW[0][rw:rw + 64, p * 128:(p + 1) * 128],
                                S16[rw:rw + 64, p * 64:(p + 1) * 64],
                                rtf[rw:rw + 64, p * 128:(p + 1) * 128],
                                start=True, stop=False),
                                reads=["S16", ("rtf", p)], writes=[("psW", 0)],
                                tag="ointer")
                            sch.op("tensor", lambda e, h=h, p=p, rw=rw, sb=sb,
                                   pb=pb: e.matmul(
                                psW[0][rw:rw + 64, p * 128:(p + 1) * 128],
                                vtm[:, sb * 1024 + h * 64: sb * 1024 + h * 64 + 64],
                                p2b[pb][:, :],
                                start=False, stop=True),
                                reads=[("vtm", sb, (h * 64) // 512), ("p2b", pb)],
                                writes=[("psW", 0)], tag="ointra")
                            sch.op("tensor", lambda e, h=h, p=p, rw=rw, sb=sb: e.matmul(
                                psC[1][rw:rw + 64, p * 64:(p + 1) * 64],
                                ktmf[:, h * 64:(h + 1) * 64],
                                vtm[:, sb * 1024 + h * 64: sb * 1024 + h * 64 + 64],
                                start=True, stop=True),
                                reads=["ktmf", ("vtm", sb, (h * 64) // 512)],
                                writes=[("psC", 1)], tag="tmm")
                        for p in range(8):
                            sch.op("vector", lambda e, p=p: e.tensor_add(
                                S_all[:, p * 64:(p + 1) * 64],
                                S_all[:, p * 64:(p + 1) * 64],
                                psC[1][:, p * 64:(p + 1) * 64]),
                                reads=["S_all", ("psC", 1)], writes=["S_all"],
                                tag="sadd")
                            sch.op("vector", lambda e, p=p: e.tensor_scalar(
                                S_all[:, p * 64:(p + 1) * 64],
                                S_all[:, p * 64:(p + 1) * 64],
                                al16[:, p:p + 1], None, ALU.mult),
                                reads=["S_all", "al16"], writes=["S_all"], tag="smul")
                        sch.op("gpsimd", lambda e: e.tensor_copy(
                            out=S16[:, :], in_=S_all[:, :]),
                            reads=["S_all"], writes=["S16"], tag="s16c")
                        for ct in range(8):
                            sch.op("vector", lambda e, ct=ct: e.tensor_copy(
                                out=o16[:, ct * 128:(ct + 1) * 128],
                                in_=psW[0][:, ct * 128:(ct + 1) * 128]),
                                reads=[("psW", 0)], writes=[("o16", ct)], tag="oevac")
                            sch.op("gpsimd", lambda e, ct=ct: e.tensor_mul(
                                osq[:, ct * 128:(ct + 1) * 128],
                                o16[:, ct * 128:(ct + 1) * 128],
                                o16[:, ct * 128:(ct + 1) * 128]),
                                reads=[("o16", ct)], writes=[("osq", ct)], tag="osq")
                        for ct in range(8):
                            sch.op("tensor", lambda e, ct=ct: e.matmul(
                                psC[0][0:16, 0:128], hm16[:, ct * 16:(ct + 1) * 16],
                                o16[:, ct * 128:(ct + 1) * 128],
                                start=(ct == 0), stop=(ct == 7)),
                                reads=["hchm16", ("o16", ct)], writes=[("psC", 0)],
                                tag="osum")
                        for ct in range(8):
                            sch.op("tensor", lambda e, ct=ct: e.matmul(
                                psC[0][0:16, 128:256], hm16[:, ct * 16:(ct + 1) * 16],
                                osq[:, ct * 128:(ct + 1) * 128],
                                start=(ct == 0), stop=(ct == 7)),
                                reads=["hchm16", ("osq", ct)], writes=[("psC", 0)],
                                tag="osqs")
                        sch.op("vector", lambda e: e.tensor_scalar(
                            d16s[:, 0:128], psC[0][0:16, 0:128], 1.0 / 64.0, None,
                            ALU.mult),
                            reads=[("psC", 0)], writes=["gmu"], tag="gmu")
                        sch.op("vector", lambda e: e.tensor_mul(
                            d16s[:, 128:256], d16s[:, 0:128], d16s[:, 0:128]),
                            reads=["gmu"], writes=["gmu2"], tag="gmu2")
                        sch.op("vector", lambda e: e.scalar_tensor_tensor(
                            d16s[:, 128:256], psC[0][0:16, 128:256], 1.0 / 64.0,
                            d16s[:, 128:256], ALU.mult, ALU.subtract),
                            reads=[("psC", 0), "gmu2"], writes=["gvar"], tag="gvar")
                        sch.op("scalar", lambda e: e.activation(
                            d16s[:, 128:256], d16s[:, 128:256], AF.Ln,
                            bias=eps16[0:16, 0:1]),
                            reads=["gvar", "eps16"], writes=["gsd"], tag="gsd")
                        sch.op("scalar", lambda e: e.activation(
                            d16s[:, 256:384], d16s[:, 128:256], AF.Exp,
                            scale=-0.5),
                            reads=["gsd"], writes=["gA"], tag="gA")
                        sch.op("vector", lambda e: e.scalar_tensor_tensor(
                            d16s[:, 384:512], d16s[:, 0:128], -1.0, d16s[:, 256:384],
                            ALU.mult, ALU.mult),
                            reads=["gmu", "gA"], writes=["gC"], tag="gC")
                        sch.op("gpsimd", lambda e: e.tensor_copy(
                            out=gab16[0:16, 0:128], in_=d16s[:, 256:384]),
                            reads=["gA"], writes=["gA16"], tag="gA16")
                        sch.op("gpsimd", lambda e: e.tensor_copy(
                            out=gab16[0:16, 128:256], in_=d16s[:, 384:512]),
                            reads=["gC"], writes=["gC16"], tag="gC16")
                        for ct in range(8):
                            sch.op("tensor", lambda e, ct=ct: e.matmul(
                                psW[1][:, ct * 128:(ct + 1) * 128],
                                ind16p[0:16, ct * 128:(ct + 1) * 128],
                                gab16[0:16, 0:128],
                                start=True, stop=True),
                                reads=["hcind16p", "gA16"], writes=[("psW", 1)],
                                tag="gbA")
                        for ct in range(8):
                            po = ct % 2
                            sch.op("tensor", lambda e, ct=ct, po=po: e.matmul(
                                psC[2 + po][:, 0:128],
                                ind16p[0:16, ct * 128:(ct + 1) * 128],
                                gab16[0:16, 128:256],
                                start=True, stop=True),
                                reads=["hcind16p", "gC16"], writes=[("psC", 2 + po)],
                                tag="gbC")
                            sch.op("vector", lambda e, ct=ct, po=po: e.tensor_mul(
                                onrm[po][:, :], o16[:, ct * 128:(ct + 1) * 128],
                                psW[1][:, ct * 128:(ct + 1) * 128]),
                                reads=[("o16", ct), ("psW", 1)],
                                writes=[("onrm", po)], tag="gapA")
                            sch.op("vector", lambda e, po=po: e.tensor_add(
                                onrm[po][:, :], onrm[po][:, :], psC[2 + po][:, 0:128]),
                                reads=[("onrm", po), ("psC", 2 + po)],
                                writes=[("onrm", po)], tag="gapC")
                            sch.op("vector", lambda e, ct=ct, po=po: e.tensor_scalar(
                                onrm[po][:, :], onrm[po][:, :],
                                lncol["gx"][:, ct:ct + 1], lncol["bx"][:, ct:ct + 1],
                                ALU.mult, ALU.add),
                                reads=[("onrm", po), "hcln_gx", "hcln_bx"],
                                writes=[("onrm", po)], tag="gapgb")
                            acol = ct * CH + sb * 128
                            sch.op("vector", lambda e, po=po, acol=acol, ct=ct:
                                   e.tensor_mul(
                                       att16[:, acol:acol + 128], onrm[po][:, :],
                                       g16[:, acol:acol + 128]),
                                   reads=[("onrm", po), ("g16", ct)],
                                   writes=[("att16", ct, sb)], tag="attg")
                    for mt in range(KT):
                        pi = mt % 2
                        xi = mt % 2
                        sch.dma("sync", xl0t[xi][:, :],
                                dr_ap(xln0_d, mt * 128, 128, [[1, CH]],
                                      col0=s * T + c0),
                                reads=[("C0", "snk", s * T + (c0 // 512) * 512, mt)],
                                writes=[("xl0t", xi)], group=f"xl0{xi}", tag="xl0")
                        for k in range(KT):
                            sch.op("tensor", lambda e, mt=mt, k=k, pi=pi: e.matmul(
                                psC[pi][:, 0:CH],
                                wo16[:, k * 1024 + mt * 128: k * 1024 + mt * 128 + 128],
                                att16[:, k * CH:(k + 1) * CH],
                                start=(k == 0), stop=(k == KT - 1)),
                                reads=[("wwo", k)] + [("att16", k, sb2)
                                                 for sb2 in range(NSB)],
                                writes=[("psC", pi)], tag="womm")
                        sch.op("vector", lambda e, pi=pi, xi=xi: e.tensor_add(
                            xintt[:, :], psC[pi][:, 0:CH], xl0t[xi][:, :]),
                            reads=[("psC", pi), ("xl0t", xi)],
                            writes=["xintt"], tag="xint")
                        sch.dma("scalar",
                                dr_ap(xint_d, mt * 128, 128, [[1, CH]],
                                      col0=s * T + c0),
                                xintt[:, 0:CH],
                                reads=["xintt"], writes=[("xint_d", s, c0, mt)],
                                group="xint_st", tag="xintst")
                sch.barrier("endCatt")

    with ExitStack() as es_f:
        wkf16 = es_f.enter_context(nc.sbuf_tensor(mk.name("wkf16"),
                                                  [128, 8 * 3584], BF16))
        cast_stream(mk, es_f, wts["Wk_ffn"], 0, D, 0, 3584, wkf16, "wkf",
                    out_col_stride=3584)
        wvf16 = es_f.enter_context(nc.sbuf_tensor(mk.name("wvf16"),
                                                  [128, 28 * 1024], BF16))
        cast_stream(mk, es_f, wts["Wv_ffn"], 0, 3584, 0, 1024, wvf16, "wvf",
                    out_col_stride=1024)
        wrf16 = es_f.enter_context(nc.sbuf_tensor(mk.name("wrf16"),
                                                  [128, 8 * 1024], BF16))
        cast_stream(mk, es_f, wts["Wr_ffn"], 0, D, 0, 1024, wrf16, "wrf",
                    out_col_stride=1024)
        psF = [es_f.enter_context(nc.psum_tensor(mk.name("psF"), [128, 512], F32))
               for _ in range(2)]
        extF = {"ps_sum": (psF[0], ("psF", 0)),
                "ps_sq": (psF[1], ("psF", 1)),
                "ps_A": None, "ps_C": None}
        psA2 = es_f.enter_context(nc.psum_tensor(mk.name("psA2"), [128, 512], F32))
        psC2_ = es_f.enter_context(nc.psum_tensor(mk.name("psC2_"), [128, 512], F32))
        extF["ps_A"] = (psA2, "psA2k")
        extF["ps_C"] = (psC2_, "psC2k")
        lntf = alloc_ln_tail(mk, es_f, ext_psums=extF)
        zcol = es_f.enter_context(nc.sbuf_tensor(mk.name("zcol"), [128, 8], BF16))
        sch.op("vector", lambda e: e.memset(zcol[:, :], 0.0), writes=["zcol"])
        xf16g = es_f.enter_context(nc.sbuf_tensor(mk.name("xf16g"),
                                                  [128, 8 * 512], BF16))
        xfc = es_f.enter_context(nc.sbuf_tensor(mk.name("xfc"),
                                                [128, KT * (CH + 1)], BF16))
        xmix2 = es_f.enter_context(nc.sbuf_tensor(mk.name("xmix2"),
                                                  [128, KT * CH], BF16))
        kk16 = es_f.enter_context(nc.sbuf_tensor(mk.name("kk16"),
                                                 [128, 28 * CH], BF16))
        sg16 = es_f.enter_context(nc.sbuf_tensor(mk.name("sg16"),
                                                 [128, KT * CH], BF16))
        r16t = [es_f.enter_context(nc.sbuf_tensor(mk.name("r16t"), [128, CH], BF16))
                for _ in range(2)]
        xin2 = [es_f.enter_context(nc.sbuf_tensor(mk.name("xin2"), [128, CH], F32))
                for _ in range(2)]

        for s in range(B):
            soff = s * T + s
            sch.dma("scalar",
                    dr_ap(xf_d, 0, 128, [[128 * (TT + B), 8], [1, 1]], col0=soff),
                    zcol[:, 0:8],
                    reads=["zcol"], writes=[("xfz", s)], group="xfz_st", tag="xfz",
                    slow_ok=True)
            for g in range(NG):
                ltok = g * 512
                emit_ln_gen(mk, lntf, "C2",
                            [(xint_d, s * T + ltok,
                              [("xint_d", s, cc * CH, mt2) for cc in
                               range(ltok // CH, (ltok + 512) // CH)
                               for mt2 in range(KT)])],
                            [("sbuf", lambda mt: (
                                xf16g[:, mt * 512:(mt + 1) * 512],
                                ("xf16g", mt), []))],
                            gamma=lncol["g2"], beta=lncol["b2"],
                            gkeys=["hcln_g2", "hcln_b2"])
                sch.dma("scalar",
                        dr_ap(xf_d, 0, 128, [[128 * (TT + B), 8], [1, 512]],
                              col0=soff + 1 + ltok),
                        sb_ap(xf16g, 0, 128, [[512, 8], [1, 512]]),
                        reads=[("xf16g", mt2) for mt2 in range(KT)],
                        writes=[("xf_d", s, g)], group="xf_st", tag="xfst")
            for c in range(NCH):
                c0 = c * CH
                for j in range(KT):
                    sch.dma("sync", xfc[:, j * (CH + 1): (j + 1) * (CH + 1)],
                            dr_ap(xf_d, j * 128, 128, [[1, CH + 1]], col0=soff + c0),
                            reads=[("xf_d", s, g2) for g2 in range(NG)] +
                                  [("xfz", s)],
                            writes=[("xfc", j), ("chain", "xfc", j % 2)],
                            group=f"xfc{j % 2}", tag="xfl")
                for which, mcol, wt, wk_, nmt in (
                        ("k", "fk", wkf16, "wkf", 28), ("r", "fr", wrf16, "wrf", 8)):
                    for j in range(KT):
                        sch.op("vector", lambda e, j=j: e.tensor_sub(
                            xmix2[:, j * CH:(j + 1) * CH],
                            xfc[:, j * (CH + 1): j * (CH + 1) + CH],
                            xfc[:, j * (CH + 1) + 1: j * (CH + 1) + 1 + CH]),
                            reads=[("xfc", j)], writes=[("xmix2", j)], tag="dxf")
                        sch.op("vector", lambda e, j=j, mcol=mcol: e.scalar_tensor_tensor(
                            xmix2[:, j * CH:(j + 1) * CH],
                            xmix2[:, j * CH:(j + 1) * CH], maa[mcol][:, j:j + 1],
                            xfc[:, j * (CH + 1) + 1: j * (CH + 1) + 1 + CH],
                            ALU.mult, ALU.add),
                            reads=[("xmix2", j), "hcmaa_" + mcol, ("xfc", j)],
                            writes=[("xmix2", j)], tag="fmix")
                    for mt in range(nmt):
                        pi = mt % 2
                        for k in range(KT):
                            sch.op("tensor", lambda e, mt=mt, k=k, pi=pi, wt=wt,
                                   nmt=nmt: e.matmul(
                                psF[pi][:, 0:CH],
                                wt[:, k * (nmt * 128) + mt * 128:
                                   k * (nmt * 128) + mt * 128 + 128],
                                xmix2[:, k * CH:(k + 1) * CH],
                                start=(k == 0), stop=(k == KT - 1)),
                                reads=[(wk_, k)] + [("xmix2", kk) for kk in range(KT)],
                                writes=[("psF", pi)], tag="ffmm")
                        if which == "k":
                            sch.op("scalar", lambda e, pi=pi: e.activation(
                                r16t[pi][:, :], psF[pi][:, 0:CH], AF.Relu),
                                reads=[("psF", pi)], writes=[("r16t", pi)],
                                tag="frelu")
                            sch.op("gpsimd", lambda e, mt=mt, pi=pi: e.tensor_mul(
                                kk16[:, mt * CH:(mt + 1) * CH],
                                r16t[pi][:, :], r16t[pi][:, :]),
                                reads=[("r16t", pi)], writes=[("kk16", mt)],
                                tag="fsq")
                        else:
                            sch.op("scalar", lambda e, mt=mt, pi=pi: e.activation(
                                sg16[:, mt * CH:(mt + 1) * CH], psF[pi][:, 0:CH],
                                AF.Sigmoid),
                                reads=[("psF", pi)], writes=[("sg16", mt)],
                                tag="fsig")
                for mt in range(KT):
                    pi = mt % 2
                    for kk in range(28):
                        sch.op("tensor", lambda e, mt=mt, kk=kk, pi=pi: e.matmul(
                            psF[pi][:, 0:CH],
                            wvf16[:, kk * 1024 + mt * 128: kk * 1024 + mt * 128 + 128],
                            kk16[:, kk * CH:(kk + 1) * CH],
                            start=(kk == 0), stop=(kk == 27)),
                            reads=[("wvf", kk), ("kk16", kk)], writes=[("psF", pi)],
                            tag="fdmm")
                    xoff = c0 % 512
                    sch.dma("sync", xin2[pi][:, 0:CH],
                            dr_ap(xint_d, mt * 128, 128, [[1, CH]], col0=s * T + c0),
                            reads=[("xint_d", s, c0, mt)],
                            writes=[("xin2", pi)], group=f"xin2{pi}", tag="xi2")
                    sch.op("vector", lambda e, mt=mt, pi=pi, xoff=xoff: e.tensor_mul(
                        lntf["xsum"][:, mt * 512 + xoff: mt * 512 + xoff + CH],
                        psF[pi][:, 0:CH], sg16[:, mt * CH:(mt + 1) * CH]),
                        reads=[("psF", pi), ("sg16", mt)],
                        writes=[(lntf["kid"], "pre", mt)], tag="fgate")
                    sch.op("vector", lambda e, mt=mt, pi=pi, xoff=xoff: e.tensor_add(
                        lntf["xsum"][:, mt * 512 + xoff: mt * 512 + xoff + CH],
                        lntf["xsum"][:, mt * 512 + xoff: mt * 512 + xoff + CH],
                        xin2[pi][:, 0:CH]),
                        reads=[(lntf["kid"], "pre", mt), ("xin2", pi)],
                        writes=[(lntf["kid"], "pre", mt)], tag="faddint")
                if c % 2 == 1:
                    base = (c - 1) * CH
                    emit_ln_gen(mk, lntf, "C3",
                                [(xin, s * T + base, [])],
                                [("dram", x3_out, s * T + base)],
                                pre_filled=True)
        sch.barrier("endCffn")


def alloc_ln_tail(mk, es2, ext_psums=None, need_stage=True, need_xsum=True):
    nc = mk.nc
    d = {}
    d["xstg"] = ([es2.enter_context(nc.sbuf_tensor(mk.name("xstg"), [128, 512], F32))
                  for _ in range(2)] if need_stage else None)
    d["xsum"] = (es2.enter_context(nc.sbuf_tensor(mk.name("xsum"), [128, 8 * 512], F32))
                 if need_xsum else None)
    d["xs16"] = [es2.enter_context(nc.sbuf_tensor(mk.name("xs16"), [128, 512], BF16))
                 for _ in range(2)]
    d["sqr"] = [es2.enter_context(nc.sbuf_tensor(mk.name("sqr"), [128, 512], BF16))
                for _ in range(2)]
    d["x1r"] = [es2.enter_context(nc.sbuf_tensor(mk.name("x1r"), [128, 512], F32))
                for _ in range(2)]
    for nm in ("st_mu", "st_a", "st_t", "st_c"):
        d[nm] = es2.enter_context(nc.sbuf_tensor(mk.name(nm), [1, 512], F32))
    if ext_psums is None:
        for nm, shp in (("ps_sum", [1, 512]), ("ps_sq", [1, 512]),
                        ("ps_A", [128, 512]), ("ps_C", [128, 512])):
            t = es2.enter_context(nc.psum_tensor(mk.name(nm), shp, F32))
            d[nm] = t
            d["k_" + nm] = mk.name("k" + nm)
    else:
        for nm in ("ps_sum", "ps_sq", "ps_A", "ps_C"):
            ap, key = ext_psums[nm]
            d[nm] = ap
            d["k_" + nm] = key
    d["o128"] = es2.enter_context(nc.sbuf_tensor(mk.name("o128"), [128, 1], BF16))
    d["o1r"] = es2.enter_context(nc.sbuf_tensor(mk.name("o1r"), [1, 128], F32))
    d["eps"] = es2.enter_context(nc.sbuf_tensor(mk.name("epsln"), [1, 1], F32))
    sch = mk.sch
    d["kid"] = mk.name("lnid")
    ck = mk.name("Lc")
    d["k_o128"] = ck + "o128"
    d["k_o1r"] = ck + "o1r"
    d["k_eps"] = ck + "eps"
    sch.op("vector", lambda e: e.memset(d["o128"][:, :], 1.0), writes=[d["k_o128"]])
    sch.op("vector", lambda e: e.memset(d["o1r"][0:1, :], 1.0), writes=[d["k_o1r"]])
    sch.op("vector", lambda e: e.memset(d["eps"][0:1, 0:1], 1e-5), writes=[d["k_eps"]])
    return d


def emit_ln_tail(mk, d, xres_d, xout_d, tok0, name="L", xres_keys=(),
                 xsum_t=None, xsum_base=0, pre_keys=None):
    sch, nc = mk.sch, mk.nc
    KT = 8
    TT = mk.TT
    xsum0 = xsum_t if xsum_t is not None else d["xsum"]

    class _XV:
        def __getitem__(self, idx):
            p, c = idx
            return xsum0[p, c.start + xsum_base: c.stop + xsum_base]
    xsum = _XV()
    if pre_keys is None:
        pre_keys = [pre_keys[mt] for mt in range(KT)]
    for mt in range(KT):
        xi = mt % 2
        sch.dma("sync", d["xstg"][xi][:, :],
                dr_ap(xres_d, mt * 128, 128, [[1, 512]], col0=tok0),
                reads=list(xres_keys),
                writes=[(name, "xstg", xi)], group=f"{name}xstg{xi}", tag="xld")
        sch.op("vector", lambda e, mt=mt, xi=xi: e.tensor_add(
            xsum[:, mt * 512:(mt + 1) * 512], xsum[:, mt * 512:(mt + 1) * 512],
            d["xstg"][xi][:, :]),
            reads=[pre_keys[mt], (name, "xstg", xi)],
            writes=[pre_keys[mt]], tag="xsum")
        sch.op("gpsimd", lambda e, mt=mt, xi=xi: e.tensor_copy(
            out=d["xs16"][xi][:, :], in_=xsum[:, mt * 512:(mt + 1) * 512]),
            reads=[pre_keys[mt]], writes=[(name, "xs16", xi)], tag="xs16")
        sch.op("gpsimd", lambda e, mt=mt, xi=xi: e.tensor_mul(
            d["sqr"][xi][:, :], d["xs16"][xi][:, :], d["xs16"][xi][:, :]),
            reads=[(name, "xs16", xi)], writes=[(name, "sqr", xi)], tag="xsq")
        sch.op("tensor", lambda e, mt=mt, xi=xi: e.matmul(
            d["ps_sum"][0:1, :], d["o128"][:, 0:1], d["xs16"][xi][:, :],
            start=(mt == 0), stop=(mt == KT - 1)),
            reads=[d["k_o128"], (name, "xs16", xi)], writes=[d["k_ps_sum"]], tag="summ")
        sch.op("tensor", lambda e, mt=mt, xi=xi: e.matmul(
            d["ps_sq"][0:1, :], d["o128"][:, 0:1], d["sqr"][xi][:, :],
            start=(mt == 0), stop=(mt == KT - 1)),
            reads=[d["k_o128"], (name, "sqr", xi)], writes=[d["k_ps_sq"]], tag="sqm")
    sch.op("vector", lambda e: e.tensor_scalar(
        d["st_mu"][0:1, :], d["ps_sum"][0:1, :], 1.0 / 1024.0, None, ALU.mult),
        reads=[d["k_ps_sum"]], writes=[(name, "st_mu")], tag="mu")
    sch.op("vector", lambda e: e.tensor_mul(
        d["st_t"][0:1, :], d["st_mu"][0:1, :], d["st_mu"][0:1, :]),
        reads=[(name, "st_mu")], writes=[(name, "st_mu2")], tag="mu2")
    sch.op("vector", lambda e: e.scalar_tensor_tensor(
        d["st_t"][0:1, :], d["ps_sq"][0:1, :], 1.0 / 1024.0, d["st_t"][0:1, :],
        ALU.mult, ALU.subtract),
        reads=[d["k_ps_sq"], (name, "st_mu2")], writes=[(name, "st_var")], tag="var")
    sch.op("scalar", lambda e: e.activation(
        d["st_t"][0:1, :], d["st_t"][0:1, :], AF.Sqrt, bias=d["eps"][0:1, 0:1]),
        reads=[(name, "st_var"), d["k_eps"]], writes=[(name, "st_sd")], tag="sd")
    sch.op("vector", lambda e: e.reciprocal(d["st_a"][0:1, :], d["st_t"][0:1, :]),
           reads=[(name, "st_sd")], writes=[(name, "st_a")], tag="rstd")
    sch.op("vector", lambda e: e.scalar_tensor_tensor(
        d["st_c"][0:1, :], d["st_mu"][0:1, :], -1.0, d["st_a"][0:1, :],
        ALU.mult, ALU.mult),
        reads=[(name, "st_mu"), (name, "st_a")], writes=[(name, "st_c")], tag="stc")
    sch.op("tensor", lambda e: e.matmul(
        d["ps_A"][:, :], d["o1r"][0:1, :], d["st_a"][0:1, :], start=True, stop=True),
        reads=[d["k_o1r"], (name, "st_a")], writes=[d["k_ps_A"]], tag="bA")
    sch.op("tensor", lambda e: e.matmul(
        d["ps_C"][:, :], d["o1r"][0:1, :], d["st_c"][0:1, :], start=True, stop=True),
        reads=[d["k_o1r"], (name, "st_c")], writes=[d["k_ps_C"]], tag="bC")
    for mt in range(KT):
        xi = mt % 2
        sch.op("vector", lambda e, mt=mt, xi=xi: e.tensor_mul(
            d["x1r"][xi][:, :], xsum[:, mt * 512:(mt + 1) * 512], d["ps_A"][:, :]),
            reads=[pre_keys[mt], d["k_ps_A"]],
            writes=[(name, "x1r", xi)], tag="lnA")
        sch.op("vector", lambda e, mt=mt, xi=xi: e.tensor_add(
            d["x1r"][xi][:, :], d["x1r"][xi][:, :], d["ps_C"][:, :]),
            reads=[(name, "x1r", xi), d["k_ps_C"]],
            writes=[(name, "x1r", xi)], tag="lnC")
        sch.dma("scalar",
                dr_ap(xout_d, mt * 128, 128, [[1, 512]], col0=tok0),
                d["x1r"][xi][:, :],
                reads=[(name, "x1r", xi)],
                writes=[("xout", tok0, mt)], group=f"{name}xo{xi}", tag="xoutst")


def emit_phaseD(mk, es, xin, rwT, wg_d, wu_d, wd_d, swg_d, swu_d, swd_d,
                host, x4_out):
    nc, sch = mk.nc, mk.sch
    TT = mk.TT
    D, HID, NE = 1024, 4096, 7
    KT = 8
    TG = min(1024, TT)
    NTG = TT // TG

    c_d = nc.dram_tensor("c_d", [NE, TT], BF16).ap()

    ident16 = es.enter_context(nc.sbuf_tensor(mk.name("identD"), [128, 128], BF16))
    sch.dma("gpsimd", ident16[:, :], host["ident"],
            writes=["identD", ("chain", "hcD")], group="hcD")
    ones1b = es.enter_context(nc.sbuf_tensor(mk.name("ones1b"), [1, 128], BF16))
    sch.dma("gpsimd", ones1b[0:1, :], host["ones1"],
            writes=["ones1b", ("chain", "hcD")], group="hcD")
    rw16 = es.enter_context(nc.sbuf_tensor("rw16", [128, KT * 7], BF16))
    cast_stream(mk, es, rwT, 0, D, 0, 7, rw16, "rw16", out_col_stride=7)

    with ExitStack() as es_e:
        x16 = es_e.enter_context(nc.sbuf_tensor("x16D", [128, KT * TG], BF16))
        mixed = es_e.enter_context(nc.sbuf_tensor("mixedD", [128, KT * TG], F32))
        h2 = es_e.enter_context(nc.sbuf_tensor("h2D", [128, 16 * TG], BF16))
        wbuf = es_e.enter_context(nc.sbuf_tensor("wbufD", [128, 32 * 1024], BF16))
        wdbuf = es_e.enter_context(nc.sbuf_tensor("wdbufD", [128, 16 * 1024], BF16))
        c_row = es_e.enter_context(nc.sbuf_tensor("c_rowD", [1, TG], BF16))
        c_brd = es_e.enter_context(nc.sbuf_tensor("c_brdD", [128, TG], BF16))
        gtmp = [es_e.enter_context(nc.sbuf_tensor(mk.name("gtmpD"), [128, 512], BF16))
                for _ in range(2)]
        u16 = [es_e.enter_context(nc.sbuf_tensor(mk.name("u16D"), [128, 512], BF16))
               for _ in range(2)]
        lg = es_e.enter_context(nc.sbuf_tensor("lgD", [128, 7], F32))
        lg2 = es_e.enter_context(nc.sbuf_tensor("lg2D", [128, 7], F32))
        msk1 = es_e.enter_context(nc.sbuf_tensor("msk1D", [128, 7], F32))
        msk2 = es_e.enter_context(nc.sbuf_tensor("msk2D", [128, 7], F32))
        m1c = es_e.enter_context(nc.sbuf_tensor("m1cD", [128, 1], F32))
        m2c = es_e.enter_context(nc.sbuf_tensor("m2cD", [128, 1], F32))
        w2c = es_e.enter_context(nc.sbuf_tensor("w2cD", [128, 1], F32))
        cmb = es_e.enter_context(nc.sbuf_tensor("cmbD", [128, 7], F32))
        cmb16 = es_e.enter_context(nc.sbuf_tensor("cmb16D", [128, 7], BF16))
        cstg = es_e.enter_context(nc.sbuf_tensor("cstgD", [1, 7 * 128], BF16))
        ps = [es_e.enter_context(nc.psum_tensor(mk.name("psD"), [128, 512], F32))
              for _ in range(4)]
        lnt = alloc_ln_tail(mk, es_e, need_xsum=False)

        for tgi in range(NTG):
            tg0 = tgi * TG
            for k in range(KT):
                sch.dma("gpsimd", x16[:, k * TG:(k + 1) * TG],
                        dr_ap(xin, k * 128, 128, [[1, TG]], col0=tg0),
                        writes=[("x16", k), ("chain", "x16D", k % 4)],
                        group=f"x16D_{k % 4}", tag="xc")
            for ttl in range(TG // 128):
                tt = tg0 // 128 + ttl
                pi = ttl % 2
                for k in range(KT):
                    sch.op("tensor", lambda e, ttl=ttl, k=k, pi=pi: e.matmul(
                        ps[pi][:, 0:7],
                        x16[:, k * TG + ttl * 128: k * TG + ttl * 128 + 128],
                        rw16[:, k * 7:(k + 1) * 7],
                        start=(k == 0), stop=(k == KT - 1)),
                        reads=[("x16", k), ("rw16", k)], writes=[("ps", pi)], tag="rlmm")
                sch.op("vector", lambda e, pi=pi: e.tensor_copy(lg[:, :], ps[pi][:, 0:7]),
                       reads=[("ps", pi)], writes=["lg"], tag="lgc")
                sch.op("vector", lambda e: e.reduce_max(m1c[:, :], lg[:, :], axis=AX.X),
                       reads=["lg"], writes=["m1c"], tag="m1")
                sch.op("vector", lambda e: e.tensor_scalar(
                    msk1[:, :], lg[:, :], m1c[:, 0:1], None, ALU.is_ge),
                    reads=["lg", "m1c"], writes=["msk1"], tag="msk1")
                sch.op("vector", lambda e: e.scalar_tensor_tensor(
                    lg2[:, :], msk1[:, :], -1e30, lg[:, :], ALU.mult, ALU.add),
                    reads=["msk1", "lg"], writes=["lg2"], tag="lg2")
                sch.op("vector", lambda e: e.reduce_max(m2c[:, :], lg2[:, :], axis=AX.X),
                       reads=["lg2"], writes=["m2c"], tag="m2")
                sch.op("vector", lambda e: e.tensor_scalar(
                    msk2[:, :], lg2[:, :], m2c[:, 0:1], None, ALU.is_ge),
                    reads=["lg2", "m2c"], writes=["msk2"], tag="msk2")
                sch.op("vector", lambda e: e.tensor_sub(m2c[:, :], m2c[:, :], m1c[:, :]),
                       reads=["m2c", "m1c"], writes=["dm"], tag="dm")
                sch.op("scalar", lambda e: e.activation(w2c[:, :], m2c[:, :], AF.Sigmoid),
                       reads=["dm"], writes=["w2c"], tag="w2")
                sch.op("vector", lambda e: e.tensor_sub(cmb[:, :], msk2[:, :], msk1[:, :]),
                       reads=["msk2", "msk1"], writes=["cmb0"], tag="cmb0")
                sch.op("vector", lambda e: e.scalar_tensor_tensor(
                    cmb[:, :], cmb[:, :], w2c[:, 0:1], msk1[:, :], ALU.mult, ALU.add),
                    reads=["cmb0", "w2c", "msk1"], writes=["cmb"], tag="cmb")
                sch.op("gpsimd", lambda e: e.tensor_copy(cmb16[:, :], cmb[:, :]),
                       reads=["cmb"], writes=["cmb16"], tag="cmb16")
                for ee in range(NE):
                    bnk = 2 + ee // 4
                    off = (ee % 4) * 128
                    sch.op("tensor", lambda e, ee=ee, bnk=bnk, off=off: e.matmul(
                        ps[bnk][0:1, off:off + 128], cmb16[:, ee:ee + 1], ident16[:, :],
                        start=True, stop=True),
                        reads=["cmb16", "identD"], writes=[("ps", bnk)], tag="ctr")
                sch.op("vector", lambda e: e.tensor_copy(cstg[0:1, 0:512], ps[2][0:1, :]),
                       reads=[("ps", 2)], writes=["cstg0"], tag="cst0")
                sch.op("vector", lambda e: e.tensor_copy(cstg[0:1, 512:896], ps[3][0:1, 0:384]),
                       reads=[("ps", 3)], writes=["cstg1"], tag="cst1")
                sch.dma("scalar",
                        dr_ap(c_d, 0, NE, [[1, 128]], col0=tt * 128),
                        sb_ap(cstg, 0, 1, [[128, NE], [1, 128]]),
                        reads=["cstg0", "cstg1"], writes=[("c_d", tt)], group="c_st",
                        tag="cdst")
            for e_i in range(8):
                shared = e_i == 7
                if shared:
                    sch.op("vector", lambda e: e.memset(c_brd[:, :], 1.0),
                           reads=[], writes=["c_brd"], tag="cb1")
                else:
                    sch.dma("sync", c_row[0:1, 0:TG],
                            dr_ap(c_d, e_i, 1, [[1, TG]], col0=tg0),
                            reads=[("c_d", t2) for t2 in
                                   range(tg0 // 128, (tg0 + TG) // 128)],
                            writes=["c_row"], group="c_row_ld", tag="crl")
                    for tgb in range(TG // 512):
                        sch.op("tensor", lambda e, tgb=tgb: e.matmul(
                            ps[0][:, :], ones1b[0:1, :],
                            c_row[0:1, tgb * 512:(tgb + 1) * 512],
                            start=True, stop=True),
                            reads=["ones1b", "c_row"], writes=[("ps", 0)], tag="cbmm")
                        sch.op("scalar", lambda e, tgb=tgb: e.copy(
                            c_brd[:, tgb * 512:(tgb + 1) * 512], ps[0][:, :]),
                            reads=[("ps", 0)], writes=["c_brd"], tag="cbc")
                for half in range(2):
                    hid0 = half * 2048
                    for k in range(KT):
                        if shared:
                            gap, uap, ero = swg_d, swu_d, 0
                        else:
                            gap, uap, ero = wg_d, wu_d, e_i * D
                        sch.dma("gpsimd", wbuf[:, k * 2048: k * 2048 + 2048],
                                dr_ap(gap, ero + k * 128, 128, [[1, 2048]], col0=hid0),
                                writes=[("wbuf", 2 * k), ("wbuf", 2 * k + 1),
                                        ("chain", "wg", k % 4)],
                                group=f"wg_{k % 4}", tag="wgl")
                        sch.dma("gpsimd", wbuf[:, 16384 + k * 2048: 16384 + k * 2048 + 2048],
                                dr_ap(uap, ero + k * 128, 128, [[1, 2048]], col0=hid0),
                                writes=[("wbuf", 16 + 2 * k), ("wbuf", 16 + 2 * k + 1),
                                        ("chain", "wu", k % 4)],
                                group=f"wu_{k % 4}", tag="wul")
                    for m in range(16):
                        for tgb in range(TG // 512):
                            pi = (m * (TG // 512) + tgb) % 2
                            bi = pi
                            for k in range(KT):
                                sch.op("tensor", lambda e, m=m, tgb=tgb, k=k, pi=pi: e.matmul(
                                    ps[pi][:, :],
                                    wbuf[:, k * 2048 + m * 128: k * 2048 + m * 128 + 128],
                                    x16[:, k * TG + tgb * 512: k * TG + tgb * 512 + 512],
                                    start=(k == 0), stop=(k == KT - 1)),
                                    reads=[("wbuf", 2 * k + m // 8), ("x16", k)],
                                    writes=[("ps", pi)], tag="gmm")
                            for k in range(KT):
                                sch.op("tensor", lambda e, m=m, tgb=tgb, k=k, pi=pi: e.matmul(
                                    ps[2 + pi][:, :],
                                    wbuf[:, 16384 + k * 2048 + m * 128:
                                         16384 + k * 2048 + m * 128 + 128],
                                    x16[:, k * TG + tgb * 512: k * TG + tgb * 512 + 512],
                                    start=(k == 0), stop=(k == KT - 1)),
                                    reads=[("wbuf", 16 + 2 * k + m // 8), ("x16", k)],
                                    writes=[("ps", 2 + pi)], tag="umm")
                            sch.op("scalar", lambda e, pi=pi, bi=bi: e.activation(
                                gtmp[bi][:, :], ps[pi][:, :], AF.Silu),
                                reads=[("ps", pi)], writes=[("gtmp", bi)], tag="silu")
                            sch.op("scalar", lambda e, pi=pi, bi=bi: e.copy(
                                u16[bi][:, :], ps[2 + pi][:, :]),
                                reads=[("ps", 2 + pi)], writes=[("u16", bi)], tag="ucp")
                            sch.op("vector", lambda e, bi=bi: e.tensor_mul(
                                gtmp[bi][:, :], u16[bi][:, :], gtmp[bi][:, :]),
                                reads=[("u16", bi), ("gtmp", bi)],
                                writes=[("gtmp", bi)], tag="hmul")
                            hcol = m * TG + tgb * 512
                            sch.op("vector", lambda e, bi=bi, hcol=hcol, tgb=tgb: e.tensor_mul(
                                h2[:, hcol:hcol + 512], gtmp[bi][:, :],
                                c_brd[:, tgb * 512:(tgb + 1) * 512]),
                                reads=[("gtmp", bi), "c_brd"],
                                writes=[("h2", m, tgb)], tag="hscl")
                    for kk in range(16):
                        if shared:
                            dap, ero = swd_d, 0
                        else:
                            dap, ero = wd_d, e_i * HID
                        sch.dma("gpsimd", wdbuf[:, kk * 1024: kk * 1024 + 1024],
                                dr_ap(dap, ero + hid0 + kk * 128, 128, [[1, 1024]]),
                                writes=[("wdb", kk), ("chain", "wd", kk % 4)],
                                group=f"wd_{kk % 4}", tag="wdl")
                    for m in range(KT):
                        for tgb in range(TG // 512):
                            pi = (m * (TG // 512) + tgb) % 2
                            for kk in range(16):
                                sch.op("tensor", lambda e, m=m, tgb=tgb, kk=kk, pi=pi: e.matmul(
                                    ps[pi][:, :],
                                    wdbuf[:, kk * 1024 + m * 128: kk * 1024 + m * 128 + 128],
                                    h2[:, kk * TG + tgb * 512: kk * TG + tgb * 512 + 512],
                                    start=(kk == 0), stop=(kk == 15)),
                                    reads=[("wdb", kk), ("h2", kk, tgb)],
                                    writes=[("ps", pi)], tag="dmm")
                            mcol = tgb * 4096 + m * 512
                            if e_i == 0 and half == 0:
                                sch.op("vector", lambda e, mcol=mcol, pi=pi: e.tensor_copy(
                                    mixed[:, mcol:mcol + 512], ps[pi][:, :]),
                                    reads=[("ps", pi)], writes=[("mixed", m, tgb)],
                                    tag="mxc")
                            else:
                                sch.op("vector", lambda e, mcol=mcol, pi=pi: e.tensor_add(
                                    mixed[:, mcol:mcol + 512],
                                    mixed[:, mcol:mcol + 512], ps[pi][:, :]),
                                    reads=[("mixed", m, tgb), ("ps", pi)],
                                    writes=[("mixed", m, tgb)], tag="mxa")
            for tgb in range(TG // 512):
                emit_ln_tail(mk, lnt, xin, x4_out, tg0 + tgb * 512, name="Dln",
                             xsum_t=mixed, xsum_base=tgb * 4096,
                             pre_keys=[("mixed", mt, tgb) for mt in range(KT)])
        sch.barrier("endexperts")


def phaseA_host_consts(q_scale, k_scale, bproj):
    hm16 = np.zeros((128, 8 * 16), np.float32)
    ind16c = np.zeros((16, 8 * 128), np.float32)
    ind16p = np.zeros((16, 8 * 128), np.float32)
    for j in range(8):
        for p in range(128):
            h = (j * 128 + p) // 64
            hm16[p, j * 16 + h] = 1.0
            ind16c[h, j * 128 + p] = 0.125
            ind16p[h, j * 128 + p] = 1.0
    ones1 = np.ones((1, 128), np.float32)
    qs_col = np.ascontiguousarray(np.tile(q_scale, 16).reshape(8, 128).T)
    ks_col = np.ascontiguousarray(np.tile(k_scale, 16).reshape(8, 128).T)
    bproj_col = np.ascontiguousarray(bproj.reshape(8, 128).T)
    return dict(hm16=hm16, ind16c=ind16c, ind16p=ind16p, ones1=ones1,
                qs_col=qs_col, ks_col=ks_col, bproj_col=bproj_col)


PHASEA_HOST_SHAPES = {"hm16": [128, 128], "ind16c": [16, 1024], "ind16p": [16, 1024],
                      "ones1": [1, 128], "qs_col": [128, 8], "ks_col": [128, 8],
                      "bproj_col": [128, 8]}


PHASEB_HOST_SHAPES = {
    "hm8": [128, 32], "ind8": [8, 512], "prot": [128, 128],
    "csq": [128, "T"], "ssq": [128, "T"], "csk": [128, "T"], "ssk": [128, "T"],
    "mcur": [128, 2048], "mprev": [128, 2048], "ones1f": [1, 128],
    "qs_col": [128, 8], "ks_col": [128, 8], "lnBg": [128, 16], "lnBb": [128, 16],
    "b1a": [128, 44], "b1g": [128, 44], "b2c": [128, 16],
    "identB": [128, 128], "o128B": [128, 1],
}

PHASEB_WTS = {"wqkv": [2 * 1024, 1536], "wo": [2 * 512, 1024],
              "w1": [2 * 1024, 5460], "w2": [2 * 2730, 1024]}


def phaseB_host_consts(ins, T=2048):
    DIM_HEAD, WINDOW = 64, 512
    hm8 = np.zeros((128, 32), np.float32)
    ind8 = np.zeros((8, 512), np.float32)
    for ti in range(4):
        for p in range(128):
            h = (ti * 128 + p) // 64
            hm8[p, ti * 8 + h] = 1.0
            ind8[h, ti * 128 + p] = 1.0
    prot = np.zeros((128, 128), np.float32)
    for b in range(2):
        for c in range(32):
            prot[b * 64 + c + 32, b * 64 + c] = -1.0
            prot[b * 64 + c, b * 64 + c + 32] = 1.0
    half_idx = np.arange(0, DIM_HEAD, 2, dtype=np.float32)
    inv_freq = 1.0 / 10000.0 ** (half_idx / DIM_HEAD)
    t = np.arange(T, dtype=np.float32)
    freqs = t[:, None] * inv_freq[None, :]
    freqs = np.concatenate([freqs, freqs], axis=-1)
    base = (half_idx + 0.4 * DIM_HEAD) / (1.4 * DIM_HEAD)
    power = (t - T // 2) / float(WINDOW // 2)
    sc = base[None, :] ** power[:, None]
    sc = np.concatenate([sc, sc], axis=-1)
    cos, sin = np.cos(freqs), np.sin(freqs)
    def tile2(a):
        return np.ascontiguousarray(np.tile(a.T, (2, 1)).astype(np.float32))
    csq = tile2(8.0 * cos * sc)
    ssq = tile2(8.0 * sin * sc)
    csk = tile2(cos / sc)
    ssk = tile2(sin / sc)
    mcur = np.zeros((128, 2048), np.float32)
    mprev = np.zeros((128, 2048), np.float32)
    for ci in range(4):
        for kr in range(128):
            for q in range(512):
                if ci * 128 + kr <= q:
                    mcur[kr, ci * 512 + q] = 1.0
                if ci * 128 + kr > q:
                    mprev[kr, ci * 512 + q] = 1.0
    qs = np.asarray(ins["L0_ffn_q_scale"], np.float32)
    ks = np.asarray(ins["L0_ffn_k_scale"], np.float32)
    qs_col = np.zeros((128, 8), np.float32)
    ks_col = np.zeros((128, 8), np.float32)
    for l in range(2):
        qs_col[:, l * 4: l * 4 + 4] = np.tile(qs[l], 2).reshape(1, 128).T.repeat(4, 1)
        ks_col[:, l * 4: l * 4 + 4] = np.tile(ks[l], 2).reshape(1, 128).T.repeat(4, 1)
    lnBg = np.concatenate([col8(np.asarray(ins["L0_ffn_ln_g"], np.float32)[l])
                           for l in range(2)], axis=1)
    lnBb = np.concatenate([col8(np.asarray(ins["L0_ffn_ln_b"], np.float32)[l])
                           for l in range(2)], axis=1)
    b1 = np.asarray(ins["L0_ffn_b1"], np.float32)
    b2 = np.asarray(ins["L0_ffn_b2"], np.float32)
    b1a = np.zeros((128, 44), np.float32)
    b1g = np.zeros((128, 44), np.float32)
    for l in range(2):
        for i in range(22):
            n = 128 if i < 21 else 42
            b1a[:n, l * 22 + i] = b1[l, i * 128: i * 128 + n]
            b1g[:n, l * 22 + i] = b1[l, 2730 + i * 128: 2730 + i * 128 + n]
    b2c = np.concatenate([col8(b2[l]) for l in range(2)], axis=1)
    return dict(hm8=hm8, ind8=ind8, prot=prot, csq=csq, ssq=ssq, csk=csk, ssk=ssk,
                mcur=mcur, mprev=mprev, ones1f=np.ones((1, 128), np.float32),
                qs_col=qs_col, ks_col=ks_col, lnBg=lnBg, lnBb=lnBb,
                b1a=b1a, b1g=b1g, b2c=b2c,
                identB=np.eye(128, dtype=np.float32),
                o128B=np.ones((128, 1), np.float32))


def build_phaseB(B=2, T=2048):
    nc = bass.Bass("TRN2", target_bir_lowering=False)
    TT = B * T
    xin = nc.dram_tensor("xin", [1024, TT], F32, kind="ExternalInput").ap()
    wts = {n: nc.dram_tensor("w_" + n, s, F32, kind="ExternalInput").ap()
           for n, s in PHASEB_WTS.items()}
    host = {n: nc.dram_tensor("h_" + n,
                              [d if d != "T" else T for d in s], F32,
                              kind="ExternalInput").ap()
            for n, s in PHASEB_HOST_SHAPES.items()}
    x2_out = nc.dram_tensor("x2", [1024, TT], F32, kind="ExternalOutput").ap()
    sch = Sched(nc)
    mk = MK(nc, sch, B, T)
    with ExitStack() as es:
        emit_phaseB(mk, es, xin, wts, host, x2_out)
        sch.barrier("final")
        sch.emit(es)
    return nc


def phaseB_in_map(inputs, xin, T=2048):
    m = {"xin": xin,
         "w_wqkv": np.ascontiguousarray(np.asarray(inputs["L0_ffn_wqkv"], np.float32).reshape(2 * 1024, 1536)),
         "w_wo": np.ascontiguousarray(np.asarray(inputs["L0_ffn_wo"], np.float32).reshape(2 * 512, 1024)),
         "w_w1": np.ascontiguousarray(np.asarray(inputs["L0_ffn_w1"], np.float32).reshape(2 * 1024, 5460)),
         "w_w2": np.ascontiguousarray(np.asarray(inputs["L0_ffn_w2"], np.float32).reshape(2 * 2730, 1024))}
    for n, v in phaseB_host_consts(inputs, T).items():
        m["h_" + n] = v
    return m


def col8(v):
    return np.ascontiguousarray(np.asarray(v, np.float32).reshape(8, 128).T)


def phaseC_host_consts(ins):
    hm16 = np.zeros((128, 8 * 16), np.float32)
    ind16p = np.zeros((16, 8 * 128), np.float32)
    for j in range(8):
        for p in range(128):
            h = (j * 128 + p) // 64
            hm16[p, j * 16 + h] = 1.0
            ind16p[h, j * 128 + p] = 1.0
    tri = np.ones((128, 128), np.float32)
    d = {
        "hm16": hm16, "ind16p": ind16p,
        "u_col": col8(np.asarray(ins["L1_attn_time_faaaa"], np.float32).reshape(-1)),
        "ident": np.eye(128, dtype=np.float32),
        "u_inc": -np.triu(tri, 0),
        "u_str": -np.triu(tri, 1),
        "u01s": np.triu(tri, 1),
        "tdrow": np.asarray(ins["L1_attn_time_decay"], np.float32).reshape(1, 1024),
    }
    for nm in ("x", "w", "k", "v", "r", "g"):
        d["maa_" + nm] = col8(ins["L1_attn_time_maa_" + nm])
    d["maa_fk"] = col8(ins["L1_attn_ffn_maa_k"])
    d["maa_fr"] = col8(ins["L1_attn_ffn_maa_r"])
    for src_, dst in (("ln0_g", "ln_g0"), ("ln0_b", "ln_b0"), ("ln1_g", "ln_g1"),
                      ("ln1_b", "ln_b1"), ("ln2_g", "ln_g2"), ("ln2_b", "ln_b2"),
                      ("lnx_g", "ln_gx"), ("lnx_b", "ln_bx")):
        d[dst] = col8(ins["L1_attn_" + src_])
    return d


PHASEC_HOST_SHAPES = {
    "hm16": [128, 128], "ind16p": [16, 1024], "u_col": [128, 8],
    "ident": [128, 128], "u_inc": [128, 128], "u_str": [128, 128],
    "u01s": [128, 128], "tdrow": [1, 1024],
    **{"maa_" + nm: [128, 8] for nm in ("x", "w", "k", "v", "r", "g", "fk", "fr")},
    **{"ln_" + nm: [128, 8] for nm in ("g0", "b0", "g1", "b1", "g2", "b2", "gx", "bx")},
}

PHASEC_WTS = {"Wr": [1024, 1024], "Wk": [1024, 1024], "Wv": [1024, 1024],
              "Wg": [1024, 1024], "Wo": [1024, 1024], "tm_w1": [1024, 160],
              "tm_w2": [160, 1024], "td_w1": [1024, 64], "td_w2": [64, 1024],
              "Wk_ffn": [1024, 3584], "Wv_ffn": [3584, 1024],
              "Wr_ffn": [1024, 1024]}


def build_phaseC(B=2, T=2048):
    nc = bass.Bass("TRN2", target_bir_lowering=False)
    TT = B * T
    xin = nc.dram_tensor("xin", [1024, TT], F32, kind="ExternalInput").ap()
    wts = {n: nc.dram_tensor("w_" + n, s, F32, kind="ExternalInput").ap()
           for n, s in PHASEC_WTS.items()}
    host = {n: nc.dram_tensor("h_" + n, s, F32, kind="ExternalInput").ap()
            for n, s in PHASEC_HOST_SHAPES.items()}
    x3_out = nc.dram_tensor("x3", [1024, TT], F32, kind="ExternalOutput").ap()
    sch = Sched(nc)
    mk = MK(nc, sch, B, T)
    with ExitStack() as es:
        emit_phaseC(mk, es, xin, wts, host, x3_out)
        sch.barrier("final")
        sch.emit(es)
    return nc


def phaseC_in_map(inputs, xin):
    m = {"xin": xin}
    for n in PHASEC_WTS:
        v = np.asarray(inputs["L1_attn_" + n], np.float32)
        if n == "tm_w2":
            v = v.reshape(160, 1024)
        m["w_" + n] = np.ascontiguousarray(v)
    for n, v in phaseC_host_consts(inputs).items():
        m["h_" + n] = v
    return m


PHASED_HOST = {"ident": [128, 128], "ones1": [1, 128]}


def phaseD_host_consts():
    return {"ident": np.eye(128, dtype=np.float32),
            "ones1": np.ones((1, 128), np.float32)}


def build_phaseD(B=2, T=2048):
    nc = bass.Bass("TRN2", target_bir_lowering=False)
    TT = B * T
    xin = nc.dram_tensor("xin", [1024, TT], F32, kind="ExternalInput").ap()
    rwT = nc.dram_tensor("rwT", [1024, 7], F32, kind="ExternalInput").ap()
    wg = nc.dram_tensor("wg", [7 * 1024, 4096], F32, kind="ExternalInput").ap()
    wu = nc.dram_tensor("wu", [7 * 1024, 4096], F32, kind="ExternalInput").ap()
    wd = nc.dram_tensor("wd", [7 * 4096, 1024], F32, kind="ExternalInput").ap()
    swg = nc.dram_tensor("swg", [1024, 4096], F32, kind="ExternalInput").ap()
    swu = nc.dram_tensor("swu", [1024, 4096], F32, kind="ExternalInput").ap()
    swd = nc.dram_tensor("swd", [4096, 1024], F32, kind="ExternalInput").ap()
    host = {n: nc.dram_tensor("h_" + n, s, F32, kind="ExternalInput").ap()
            for n, s in PHASED_HOST.items()}
    x4_out = nc.dram_tensor("x4", [1024, TT], F32, kind="ExternalOutput").ap()
    sch = Sched(nc)
    mk = MK(nc, sch, B, T)
    with ExitStack() as es:
        emit_phaseD(mk, es, xin, rwT, wg, wu, wd, swg, swu, swd, host, x4_out)
        sch.barrier("final")
        sch.emit(es)
    return nc


def phaseD_in_map(inputs, xin):
    return {"xin": xin,
            "rwT": np.ascontiguousarray(np.asarray(inputs["L1_ffn_router_w"], np.float32).T),
            "wg": np.ascontiguousarray(np.asarray(inputs["L1_ffn_w_gate"], np.float32).reshape(7 * 1024, 4096)),
            "wu": np.ascontiguousarray(np.asarray(inputs["L1_ffn_w_up"], np.float32).reshape(7 * 1024, 4096)),
            "wd": np.ascontiguousarray(np.asarray(inputs["L1_ffn_w_down"], np.float32).reshape(7 * 4096, 1024)),
            "swg": np.ascontiguousarray(np.asarray(inputs["L1_ffn_sw_gate"], np.float32).reshape(1024, 4096)),
            "swu": np.ascontiguousarray(np.asarray(inputs["L1_ffn_sw_up"], np.float32).reshape(1024, 4096)),
            "swd": np.ascontiguousarray(np.asarray(inputs["L1_ffn_sw_down"], np.float32).reshape(4096, 1024)),
            **{"h_" + n: v for n, v in phaseD_host_consts().items()}}


def build_phaseA(B=2, T=2048):
    nc = bass.Bass("TRN2", target_bir_lowering=False)
    TT = B * T
    xin = nc.dram_tensor("xin", [1024, TT], F32, kind="ExternalInput").ap()
    Wqkv = nc.dram_tensor("Wqkv", [1024, 3072], F32, kind="ExternalInput").ap()
    Wproj = nc.dram_tensor("Wproj", [1024, 1024], F32, kind="ExternalInput").ap()
    host = {n: nc.dram_tensor(n, s, F32, kind="ExternalInput").ap()
            for n, s in PHASEA_HOST_SHAPES.items()}
    x1_out = nc.dram_tensor("x1", [1024, TT], F32, kind="ExternalOutput").ap()

    sch = Sched(nc)
    mk = MK(nc, sch, B, T)
    with ExitStack() as es:
        emit_phaseA(mk, es, xin, Wqkv, Wproj, host, x1_out)
        sch.barrier("final")
        sch.emit(es)
    return nc


def build_full(B=2, T=2048):
    nc = bass.Bass("TRN2", target_bir_lowering=False)
    TT = B * T
    x0 = nc.dram_tensor("x0", [1024, TT], F32, kind="ExternalInput").ap()
    x4 = nc.dram_tensor("x4", [1024, TT], F32, kind="ExternalOutput").ap()
    x1_d = nc.dram_tensor("x1_d", [1024, TT], F32).ap()
    x2_d = nc.dram_tensor("x2_d", [1024, TT], F32).ap()
    x3_d = nc.dram_tensor("x3_d", [1024, TT], F32).ap()

    def din(name, shape):
        return nc.dram_tensor(name, shape, F32, kind="ExternalInput").ap()

    aw = {"Wqkv": din("A_Wqkv", [1024, 3072]), "Wproj": din("A_Wproj", [1024, 1024])}
    ah = {n: din("A_h_" + n, s) for n, s in PHASEA_HOST_SHAPES.items()}
    bw = {n: din("B_w_" + n, s) for n, s in PHASEB_WTS.items()}
    bh = {n: din("B_h_" + n, [d if d != "T" else T for d in s])
          for n, s in PHASEB_HOST_SHAPES.items()}
    cw = {n: din("C_w_" + n, s) for n, s in PHASEC_WTS.items()}
    ch = {n: din("C_h_" + n, s) for n, s in PHASEC_HOST_SHAPES.items()}
    dw = {n: din("D_w_" + n, s) for n, s in
          (("rwT", [1024, 7]), ("wg", [7 * 1024, 4096]), ("wu", [7 * 1024, 4096]),
           ("wd", [7 * 4096, 1024]), ("swg", [1024, 4096]), ("swu", [1024, 4096]),
           ("swd", [4096, 1024]))}
    dh = {n: din("D_h_" + n, s) for n, s in PHASED_HOST.items()}

    sch = Sched(nc)
    mk = MK(nc, sch, B, T)
    with ExitStack() as es_all:
        with ExitStack() as esA:
            emit_phaseA(mk, esA, x0, aw["Wqkv"], aw["Wproj"], ah, x1_d)
            sch.barrier("endA")
        sch.new_epoch()
        with ExitStack() as esB:
            emit_phaseB(mk, esB, x1_d, bw, bh, x2_d)
            sch.barrier("endB2")
        sch.new_epoch()
        with ExitStack() as esC:
            emit_phaseC(mk, esC, x2_d, cw, ch, x3_d)
            sch.barrier("endC2")
        sch.new_epoch()
        with ExitStack() as esD:
            emit_phaseD(mk, esD, x3_d, dw["rwT"], dw["wg"], dw["wu"], dw["wd"],
                        dw["swg"], dw["swu"], dw["swd"], dh, x4)
            sch.barrier("final")
        sch.emit(es_all)
    return nc


def full_in_map(inputs, xshard, T=2048):
    m = {"x0": xshard,
         "A_Wqkv": np.ascontiguousarray(np.asarray(inputs["L0_attn_Wqkv"], np.float32)),
         "A_Wproj": np.ascontiguousarray(np.asarray(inputs["L0_attn_Wproj"], np.float32))}
    for n, v in phaseA_host_consts(
            np.asarray(inputs["L0_attn_q_scale"], np.float32),
            np.asarray(inputs["L0_attn_k_scale"], np.float32),
            np.asarray(inputs["L0_attn_bproj"], np.float32)).items():
        m["A_h_" + n] = v
    bm = phaseB_in_map(inputs, None, T)
    for k, v in bm.items():
        if k == "xin":
            continue
        m["B_" + k] = v
    cm = phaseC_in_map(inputs, None)
    for k, v in cm.items():
        if k == "xin":
            continue
        m["C_" + k] = v
    dm = phaseD_in_map(inputs, None)
    for k, v in dm.items():
        if k == "xin":
            continue
        if not k.startswith("h_"):
            m["D_w_" + k] = v
        else:
            m["D_" + k] = v
    return m


_CACHE = {}


def kernel(**inputs):
    B_full, T, D = 16, 2048, 1024
    n_cores = 8
    Bc = B_full // n_cores

    if "F" not in _CACHE:
        _CACHE["F"] = build_full(Bc, T)

    x = np.ascontiguousarray(np.asarray(inputs["L0_attn_x"], np.float32))
    in_maps = []
    for c in range(n_cores):
        xs = np.ascontiguousarray(x[c * Bc:(c + 1) * Bc].reshape(Bc * T, D).T)
        in_maps.append(full_in_map(inputs, xs, T))
    res = run_bass_kernel_spmd(_CACHE["F"], in_maps, core_ids=list(range(n_cores)))
    outs = [np.asarray(res.results[c]["x4"]).T.reshape(Bc, T, D)
            for c in range(n_cores)]
    return np.concatenate(outs, axis=0)


# version 13
# speedup vs baseline: 1.0467x; 1.0467x over previous
from contextlib import ExitStack

import numpy as np
import ml_dtypes
import concourse.bass as bass
import concourse.mybir as mybir
from concourse.bass_utils import run_bass_kernel_spmd

F32 = mybir.dt.float32
BF16 = mybir.dt.bfloat16
AF = mybir.ActivationFunctionType
ALU = mybir.AluOpType
AX = mybir.AxisListType

ENGS = ("sync", "tensor", "vector", "scalar", "gpsimd")


def sb_ap(t, p0, np_, dims, col0=0):
    w = t.shape[-1]
    return bass.AP(t, p0 * w + col0, [[w, np_]] + [list(d) for d in dims])


def dr_ap(t_ap, row0, nrow, dims, col0=0):
    w = t_ap.tensor.shape[-1]
    return bass.AP(t_ap.tensor, t_ap.offset + row0 * w + col0,
                   [[w, nrow]] + [list(d) for d in dims])


class _Cap:

    def __init__(self):
        self.calls = []

    def __getattr__(self, name):
        def m(*a, **k):
            self.calls.append((name, a, k))
            return None
        return m

    def __call__(self, engine):
        name, a, k = self.calls[0]
        return getattr(engine, name)(*a, **k)


class Sched:

    def __init__(self, nc):
        self.nc = nc
        self.ops = {e: [] for e in ENGS}
        self.sem_cnt = {}
        self.sem_order = []
        self.writer = {}
        self.readers = {}
        self.n_ops = 0
        self.epoch = 0

    def new_epoch(self):
        self.barrier(f"epoch{self.epoch}")
        self.epoch += 1

    def _touch_sem(self, name):
        if name not in self.sem_cnt:
            self.sem_cnt[name] = 0
            self.sem_order.append(name)

    def op(self, eng, fn, reads=(), writes=(), dma=None, tag=""):
        cap = _Cap()
        fn(cap)
        assert len(cap.calls) == 1, f"op must emit exactly 1 instruction, got {len(cap.calls)} ({tag})"
        fn = cap
        semname = ((self.epoch, "D_" + dma) if dma is not None else eng)
        inc = 16 if dma is not None else 1
        self._touch_sem(semname)
        if dma is not None:
            cls = "sw" if eng == "gpsimd" else "hw"
            prev = getattr(self, "sem_qcls", None)
            if prev is None:
                self.sem_qcls = {}
            old = self.sem_qcls.get(semname)
            assert old in (None, cls), f"group {dma} used from both queue classes"
            self.sem_qcls[semname] = cls
        deps = {}

        def add(tok):
            if tok is None:
                return
            sn, v = tok
            if deps.get(sn, 0) < v:
                deps[sn] = v

        for b in reads:
            add(self.writer.get(b))
        for b in writes:
            add(self.writer.get(b))
            for r in self.readers.get(b, ()):
                add(r)
        if eng == "tensor" and dma is None:
            deps.pop("tensor", None)
        val = self.sem_cnt[semname] + inc
        self.sem_cnt[semname] = val
        self.ops[eng].append((fn, sorted(deps.items(), key=lambda kv: str(kv[0])), semname, inc, tag, self.epoch))
        self.n_ops += 1
        tok = (semname, val)
        for b in reads:
            self.readers.setdefault(b, []).append(tok)
        for b in writes:
            self.writer[b] = tok
            self.readers[b] = []
        return tok

    def dma(self, queue, out, in_, reads=(), writes=(), group=None, tag="",
            slow_ok=False):
        assert group is not None
        return self.op(queue,
                       lambda e: e.dma_start(out=out, in_=in_,
                                             allow_slow_non_contiguous=slow_ok),
                       reads=reads, writes=writes, dma=group, tag=tag)

    def barrier(self, tag="barrier"):
        snap = dict(self.sem_cnt)
        deps = sorted(snap.items(), key=lambda kv: str(kv[0]))
        for eng in ENGS:
            self.ops[eng].append((None, deps, None, 0, tag, self.epoch))

    def emit(self, exitstack):
        nc = self.nc
        eng_names = [n for n in self.sem_order if not isinstance(n, tuple)]
        phys = {}
        for i, n in enumerate(eng_names):
            phys[n] = i
        npool_base = len(eng_names)
        epochs = sorted({n[0] for n in self.sem_order if isinstance(n, tuple)})
        qcls = getattr(self, "sem_qcls", {})
        pools = {"sw": [], "hw": []}
        base = {}
        slot_of = {}
        for ep in epochs:
            names = [n for n in self.sem_order
                     if isinstance(n, tuple) and n[0] == ep]
            per_cls = {"sw": 0, "hw": 0}
            for n in names:
                c = qcls.get(n, "hw")
                j = per_cls[c]
                per_cls[c] += 1
                pool = pools[c]
                while j >= len(pool):
                    pool.append(0)
                slot_of[n] = (c, j)
                base[n] = pool[j]
                pool[j] += self.sem_cnt[n]
        for n, (c, j) in slot_of.items():
            off_c = 0 if c == "sw" else len(pools["sw"])
            phys[n] = npool_base + off_c + j
        nphys = npool_base + len(pools["sw"]) + len(pools["hw"])
        assert nphys <= 100, f"too many physical sems: {nphys}"
        handles = [exitstack.enter_context(nc.semaphore(f"s{i}"))
                   for i in range(nphys)]
        sems = {n: handles[p] for n, p in phys.items()}
        off = {n: base.get(n, 0) for n in self.sem_order}
        self._off = off
        block = exitstack.enter_context(nc.Block())
        for eng in ENGS:
            recs = self.ops[eng]
            if not recs:
                continue

            def body(engine, recs=recs):
                es_lp = ExitStack()
                es_lp.enter_context(nc.allow_low_precision(
                    reason="deliberate bf16 production of matmul operands"))
                vclock = {}
                cur_ep = -1
                cur_sid = None
                for fn, deps, semname, inc, tag, ep in recs:
                    if ep != cur_ep and eng == "sync":
                        if cur_sid is not None:
                            nc.leave_named_scope(f"ph{cur_ep}", cur_sid, False)
                        cur_sid, _ = nc.enter_named_scope(f"ph{ep}", False)
                        cur_ep = ep
                    elif ep != cur_ep:
                        cur_ep = ep
                    for sn, v in deps:
                        if vclock.get(sn, 0) < v:
                            engine.wait_ge(sems[sn], off[sn] + v)
                            vclock[sn] = v
                    if fn is None:
                        continue
                    try:
                        instr = fn(engine)
                    except Exception as ex:
                        raise RuntimeError(
                            f"emit failed for op tag={tag} eng={eng} "
                            f"call={fn.calls[0][0]}") from ex
                    instr.then_inc(sems[semname], inc)
                if cur_sid is not None:
                    nc.leave_named_scope(f"ph{cur_ep}", cur_sid, False)
                es_lp.close()

            getattr(block, eng)(body)
        return sems


class MK:
    def __init__(self, nc, sch, B, T):
        self.nc = nc
        self.sch = sch
        self.B = B
        self.T = T
        self.TT = B * T
        self.uid = 0

    def name(self, base):
        self.uid += 1
        return f"{base}_{self.uid}"


def cast_stream(mk, es, dram_ap, row0, D, col0, C, out16, out_key,
                queue="gpsimd", cast_eng=None, out_col_stride=None):
    nc, sch = mk.nc, mk.sch
    KT = D // 128
    ocs = out_col_stride or C
    gname = mk.name("cs")
    for j in range(KT):
        sch.dma("gpsimd", out16[:, j * ocs: j * ocs + C],
                dr_ap(dram_ap, row0 + j * 128, 128, [[1, C]], col0),
                writes=[(out_key, j), ("chain", gname, j % 2)],
                group=f"{gname}_{j % 2}", tag="wload")


def emit_phaseA(mk, es, xin, Wqkv, Wproj, host, x1_out):
    nc, sch = mk.nc, mk.sch
    B, T, TT = mk.B, mk.T, mk.TT
    D, H = 1024, 16
    KT = 8
    NG = T // 512
    QT = T // 128
    QPW = min(1024, T)
    NQP = T // QPW
    EPS = 1e-8

    qh_d = nc.dram_tensor("qh_d", [D, T], BF16).ap()
    kh_d = nc.dram_tensor("kh_d", [D, T], BF16).ap()
    mrow_d = nc.dram_tensor("mrow_d", [H, T], BF16).ap()

    def load_const(name, rows, cols, cast16=False):
        ap = host[name]
        if cast16:
            t16 = es.enter_context(nc.sbuf_tensor(mk.name(name + "16"), [rows, cols], BF16))
            sch.dma("gpsimd", t16[0:rows, 0:cols], ap,
                    writes=[name, ("chain", "hcA")], group="hcA")
            return t16
        t32 = es.enter_context(nc.sbuf_tensor(mk.name(name), [rows, cols], F32))
        sch.dma("sync", t32[0:rows, 0:cols], ap,
                writes=[name + "32", ("chain", "hcA2")], group="hcA2")
        return t32

    hm16 = load_const("hm16", 128, 8 * 16, cast16=True)
    ind16c = load_const("ind16c", 16, 8 * 128, cast16=True)
    ind16p = load_const("ind16p", 16, 8 * 128, cast16=True)
    ones1_32 = load_const("ones1", 1, 128)
    qs_col = load_const("qs_col", 128, 8)
    ks_col = load_const("ks_col", 128, 8)
    bproj_col = load_const("bproj_col", 128, 8)


    epsA2 = es.enter_context(nc.sbuf_tensor("epsA2", [16, 1], F32))
    ln8A = es.enter_context(nc.sbuf_tensor("ln8A", [16, 1], F32))
    sch.op("vector", lambda e: e.memset(epsA2[0:16, 0:1], 6.4e-15),
           writes=["epsA2c"], tag="epsA2c")
    sch.op("vector", lambda e: e.memset(ln8A[0:16, 0:1], 2.0794415416798357),
           writes=["ln8Ac"], tag="ln8Ac")

    wqkv16 = es.enter_context(nc.sbuf_tensor("wqkv16", [128, KT * 3072], BF16))
    cast_stream(mk, es, Wqkv, 0, D, 0, 3072, wqkv16, "wqkv16", out_col_stride=3072)
    wproj16 = es.enter_context(nc.sbuf_tensor("wproj16", [128, KT * 1024], BF16))
    cast_stream(mk, es, Wproj, 0, D, 0, 1024, wproj16, "wproj16", out_col_stride=1024)

    VBLK = H * 65

    for s in range(B):
        with ExitStack() as es_s:
            v_sb = es_s.enter_context(nc.sbuf_tensor(mk.name("v_sb"), [128, QT * VBLK], BF16))
            sch.op("vector", lambda e: e.memset(
                sb_ap(v_sb, 0, 128, [[VBLK, QT], [65, H]], col0=64), 1.0),
                writes=["v_ones"], tag="v_ones")
            obuf = es_s.enter_context(nc.sbuf_tensor(mk.name("obuf"), [128, KT * T], BF16))
            obuf_keys = []

            with ExitStack() as es_1:
                x16r = [es_1.enter_context(nc.sbuf_tensor(mk.name("x16r"),
                                                          [128, KT * 512], BF16))
                        for _ in range(2)]
                qkraw = es_1.enter_context(nc.sbuf_tensor(mk.name("qkraw"), [128, 16 * 512], BF16))
                sqr = [es_1.enter_context(nc.sbuf_tensor(mk.name("sqr"), [128, 512], BF16))
                       for _ in range(2)]
                fq_sb = es_1.enter_context(nc.sbuf_tensor(mk.name("fq_sb"), [16, T], BF16))
                fk_sb = es_1.enter_context(nc.sbuf_tensor(mk.name("fk_sb"), [16, T], BF16))
                qn_sb = es_1.enter_context(nc.sbuf_tensor(mk.name("qn_sb"), [16, T], F32))
                kn_sb = es_1.enter_context(nc.sbuf_tensor(mk.name("kn_sb"), [16, T], F32))
                knorm = es_1.enter_context(nc.sbuf_tensor(mk.name("knorm"), [16, 1], F32))
                mneg16 = es_1.enter_context(nc.sbuf_tensor(mk.name("mneg16"), [16, T], BF16))
                sc1 = es_1.enter_context(nc.sbuf_tensor(mk.name("sc1"), [16, 512], F32))
                sc2 = sc1
                qstage = es_1.enter_context(nc.sbuf_tensor(mk.name("qstage"), [128, 8 * 512], BF16))
                kstage = qstage

                ps_mm = [es_1.enter_context(nc.psum_tensor(mk.name("ps_mm"),
                                                           [128, 512], F32))
                         for _ in range(2)]
                ps_ss = [es_1.enter_context(nc.psum_tensor(mk.name("ps_ss"),
                                                           [16, 512], F32))
                         for _ in range(4)]
                ps_fb = [es_1.enter_context(nc.psum_tensor(mk.name("ps_fb"),
                                                           [128, 512], F32))
                        for _ in range(2)]

                for g in range(NG):
                    tok0 = s * T + g * 512
                    xr = g % 2
                    for k in range(KT):
                        sch.dma("gpsimd", x16r[xr][:, k * 512:(k + 1) * 512],
                                dr_ap(xin, k * 128, 128, [[1, 512]], col0=tok0),
                                reads=[], writes=[("x16r", xr, k)],
                                group=f"x16r{xr}_{k}", tag="xcast")
                    for ttl in range(4):
                        tt = g * 4 + ttl
                        for vc in range(2):
                            pi = (ttl * 2 + vc) % 2
                            pkey = ("ps_mm", pi)
                            for k in range(KT):
                                sch.op("tensor", lambda e, xr=xr, ttl=ttl, vc=vc, k=k, pi=pi: e.matmul(
                                    ps_mm[pi][:, :],
                                    x16r[xr][:, k * 512 + ttl * 128: k * 512 + ttl * 128 + 128],
                                    wqkv16[:, k * 3072 + 2048 + vc * 512:
                                           k * 3072 + 2048 + vc * 512 + 512],
                                    start=(k == 0), stop=(k == KT - 1)),
                                    reads=[("x16r", xr, k), ("wqkv16", k)],
                                    writes=[pkey], tag="vmm")
                            base = tt * VBLK + vc * 8 * 65
                            sch.op("vector", lambda e, base=base, pi=pi: e.tensor_copy(
                                out=sb_ap(v_sb, 0, 128, [[65, 8], [1, 64]], col0=base),
                                in_=ps_mm[pi][:, :]),
                                reads=[pkey, "v_ones"],
                                writes=[("v_sb", tt, vc)], tag="vevac")
                    for mt in range(16):
                        pi = mt % 2
                        pkey = ("ps_mm", pi)
                        isq = mt < 8
                        j = mt % 8
                        for k in range(KT):
                            sch.op("tensor", lambda e, xr=xr, mt=mt, k=k, pi=pi: e.matmul(
                                ps_mm[pi][:, :],
                                wqkv16[:, k * 3072 + mt * 128: k * 3072 + mt * 128 + 128],
                                x16r[xr][:, k * 512:(k + 1) * 512],
                                start=(k == 0), stop=(k == KT - 1)),
                                reads=[("wqkv16", k), ("x16r", xr, k)],
                                writes=[pkey], tag="qkmm")
                        sch.op("vector", lambda e, mt=mt, pi=pi: e.tensor_copy(
                            out=qkraw[:, mt * 512:(mt + 1) * 512], in_=ps_mm[pi][:, :]),
                            reads=[pkey], writes=[("qkraw", mt)], tag="qkevac")
                        sch.op("gpsimd", lambda e, mt=mt, pi=pi: e.tensor_mul(
                            sqr[pi][:, :],
                            qkraw[:, mt * 512:(mt + 1) * 512],
                            qkraw[:, mt * 512:(mt + 1) * 512]),
                            reads=[("qkraw", mt)], writes=[("sqr", pi)], tag="qksq")
                        wu = 0 if isq else 2
                        sch.op("tensor", lambda e, wu=wu, j=j, pi=pi: e.matmul(
                            ps_ss[wu][0:16, :], hm16[:, j * 16:(j + 1) * 16],
                            sqr[pi][:, :], start=(j == 0), stop=(j == 7)),
                            reads=["hm16", ("sqr", pi)],
                            writes=[("ps_ss", wu)], tag="ssmm")
                    gsl = slice(g * 512, (g + 1) * 512)
                    sch.op("scalar", lambda e: e.activation(
                        sc1[:, :], ps_ss[0][0:16, :], AF.Ln, bias=epsA2[0:16, 0:1]),
                        reads=[("ps_ss", 0), "epsA2c"], writes=["sc1"], tag="fq1")
                    sch.op("scalar", lambda e, gsl=gsl: e.activation(
                        fq_sb[:, gsl], sc1[:, :], AF.Exp, scale=-0.5,
                        bias=ln8A[0:16, 0:1]),
                        reads=["sc1", "ln8Ac"], writes=[("fq_sb", g)], tag="fq2")
                    sch.op("scalar", lambda e: e.activation(
                        sc2[:, :], ps_ss[2][0:16, :], AF.Ln, bias=epsA2[0:16, 0:1]),
                        reads=[("ps_ss", 2), "epsA2c"], writes=["sc1"], tag="fk1")
                    sch.op("scalar", lambda e, gsl=gsl: e.activation(
                        fk_sb[:, gsl], sc2[:, :], AF.Exp, scale=-0.5,
                        bias=ln8A[0:16, 0:1]),
                        reads=["sc1", "ln8Ac"], writes=[("fk_sb", g)], tag="fk2")
                    for mt in list(range(8)) + ["QDMA"] + list(range(8, 16)):
                        if mt == "QDMA":
                            sch.dma("scalar",
                                    dr_ap(qh_d, 0, 128, [[128 * T, 8], [1, 512]], col0=g * 512),
                                    sb_ap(qstage, 0, 128, [[512, 8], [1, 512]]),
                                    reads=[("qstage", jj) for jj in range(8)],
                                    writes=[("qh_d", g)], group="qh_st", tag="qst")
                            continue
                        isq = mt < 8
                        j = mt % 8
                        pi = mt % 2
                        ind = ind16c if isq else ind16p
                        indk = "ind16c" if isq else "ind16p"
                        fsb = fq_sb if isq else fk_sb
                        fkey = ("fq_sb", g) if isq else ("fk_sb", g)
                        scol = qs_col if isq else ks_col
                        sckey = "qs_col32" if isq else "ks_col32"
                        stgb = qstage
                        stgkey = ("qstage", j) if isq else ("kstage", j)
                        stgkey2 = ("kstage", j) if isq else ("qstage", j)
                        sch.op("tensor", lambda e, ind=ind, j=j, fsb=fsb, gsl=gsl, pi=pi: e.matmul(
                            ps_fb[pi][:, :], ind[0:16, j * 128:(j + 1) * 128], fsb[:, gsl],
                            start=True, stop=True),
                            reads=[indk, fkey], writes=[("ps_fb", pi)], tag="fbmm")
                        sch.op("vector", lambda e, stgb=stgb, j=j, scol=scol, mt=mt, pi=pi:
                               e.scalar_tensor_tensor(
                                   stgb[:, j * 512:(j + 1) * 512], ps_fb[pi][:, :],
                                   scol[:, j:j + 1], qkraw[:, mt * 512:(mt + 1) * 512],
                                   ALU.mult, ALU.mult),
                               reads=[("ps_fb", pi), sckey, ("qkraw", mt)],
                               writes=[stgkey, stgkey2], tag="qhat")
                    sch.dma("scalar",
                            dr_ap(kh_d, 0, 128, [[128 * T, 8], [1, 512]], col0=g * 512),
                            sb_ap(kstage, 0, 128, [[512, 8], [1, 512]]),
                            reads=[("kstage", j) for j in range(8)],
                            writes=[("kh_d", g)], group="kh_st", tag="kst")
                sch.barrier("endA1")

            with ExitStack() as es_2:
                qaug = [es_2.enter_context(nc.sbuf_tensor(mk.name("qaug"), [128, T], BF16))
                        for _ in range(2)]
                kaug = [es_2.enter_context(nc.sbuf_tensor(mk.name("kaug"), [128, T], BF16))
                        for _ in range(2)]
                for i in range(2):
                    sch.op("vector", lambda e, i=i: e.memset(kaug[i][64:65, :], 1.0),
                           writes=[("kaug_ones", i)], tag="kones")
                    sch.op("vector", lambda e, i=i: e.memset(qaug[i][64:65, :], 0.0),
                           writes=[("qaug_m", i)], tag="qmz")
                pbuf = es_2.enter_context(nc.sbuf_tensor(mk.name("pbuf"), [128, QT * QPW], BF16))
                rden = es_2.enter_context(nc.sbuf_tensor(mk.name("rden"), [1, 512], F32))
                rdenb = es_2.enter_context(nc.sbuf_tensor(mk.name("rdenb"), [64, 512], F32))
                ps_s2 = [es_2.enter_context(nc.psum_tensor(mk.name("ps_s2"),
                                                           [128, QPW], F32))
                         for _ in range(2)]
                ps_o = [es_2.enter_context(nc.psum_tensor(mk.name("ps_o"), [65, 512], F32))
                        for _ in range(3)]
                ps_r = es_2.enter_context(nc.psum_tensor(mk.name("ps_r"), [64, 512], F32))
                for h in range(H):
                    bi = h % 2
                    sch.dma("sync", qaug[bi][0:64, 0:T],
                            qh_d[h * 64:(h + 1) * 64, :],
                            reads=[("qh_d", g) for g in range(NG)],
                            writes=[("qaug", bi)], group=f"qaug{bi}", tag="qld")
                    sch.dma("sync", kaug[bi][0:64, 0:T],
                            kh_d[h * 64:(h + 1) * 64, :],
                            reads=[("kh_d", g) for g in range(NG)],
                            writes=[("kaug", bi)], group=f"kaug{bi}", tag="kld")
                    for qp in range(NQP):
                        for kt in range(QT):
                            pi = kt % 2
                            for qq in range(QPW // 512):
                                sch.op("tensor", lambda e, kt=kt, qp=qp, bi=bi, pi=pi, qq=qq: e.matmul(
                                    ps_s2[pi][:, qq * 512:(qq + 1) * 512],
                                    kaug[bi][0:65, kt * 128:(kt + 1) * 128],
                                    qaug[bi][0:65, qp * QPW + qq * 512:
                                             qp * QPW + qq * 512 + 512],
                                    start=True, stop=True),
                                    reads=[("kaug", bi), ("kaug_ones", bi),
                                           ("qaug", bi), ("qaug_m", bi)],
                                    writes=[("ps_s2", pi)], tag="s2mm")
                            sch.op("scalar", lambda e, kt=kt, pi=pi: e.activation(
                                pbuf[:, kt * QPW:(kt + 1) * QPW], ps_s2[pi][:, :], AF.Exp),
                                reads=[("ps_s2", pi)], writes=[("pbuf", kt)], tag="exp")
                        for qh_i in range(QPW // 512):
                            q0 = qp * QPW + qh_i * 512
                            po = (h * NQP * (QPW // 512)
                                  + qp * (QPW // 512) + qh_i) % 3
                            for kt in range(QT):
                                sch.op("tensor", lambda e, kt=kt, h=h, qh_i=qh_i, po=po: e.matmul(
                                    ps_o[po][0:65, :],
                                    v_sb[:, kt * VBLK + h * 65: kt * VBLK + h * 65 + 65],
                                    pbuf[:, kt * QPW + qh_i * 512:
                                         kt * QPW + qh_i * 512 + 512],
                                    start=(kt == 0), stop=(kt == QT - 1)),
                                    reads=[("v_sb", kt, h // 8), "v_ones", ("pbuf", kt)],
                                    writes=[("ps_o", po)], tag="pvmm")
                            sch.op("vector", lambda e, po=po: e.reciprocal(
                                rden[0:1, :], ps_o[po][64:65, :]),
                                reads=[("ps_o", po)], writes=["rden"], tag="rden")
                            sch.op("tensor", lambda e: e.matmul(
                                ps_r[0:64, :], ones1_32[0:1, 0:64], rden[0:1, :],
                                start=True, stop=True),
                                reads=["ones132", "rden"], writes=["ps_r"], tag="rbmm")
                            sch.op("scalar", lambda e: e.copy(rdenb[:, :], ps_r[0:64, :]),
                                   reads=["ps_r"], writes=["rdenb"], tag="rbc")
                            ocol = (h // 2) * T + q0
                            orow = (h % 2) * 64
                            okey = ("obuf", h, q0)
                            obuf_keys.append(okey)
                            sch.op("vector", lambda e, ocol=ocol, orow=orow, po=po: e.tensor_mul(
                                obuf[orow:orow + 64, ocol:ocol + 512],
                                ps_o[po][0:64, :], rdenb[:, :]),
                                reads=[("ps_o", po), "rdenb"],
                                writes=[okey], tag="onorm")
                sch.barrier("endattn")

            emit_proj_residual_ln(mk, s, obuf, obuf_keys, wproj16,
                                  [("wproj16", k) for k in range(KT)],
                                  bproj_col, "bproj_col32", xin, x1_out)
            sch.barrier("endproj")


def emit_proj_residual_ln(mk, s, act16, act_keys, w16, w_keys, bias_col, biaskey,
                          xres_d, xout_d):
    nc, sch = mk.nc, mk.sch
    T, TT = mk.T, mk.TT
    KT = 8
    NG = T // 512
    with ExitStack() as es2:
        xstg = [es2.enter_context(nc.sbuf_tensor(mk.name("xstg"), [128, 512], F32))
                for _ in range(2)]
        xsum = es2.enter_context(nc.sbuf_tensor(mk.name("xsum"), [128, 8 * 512], F32))
        xs16 = es2.enter_context(nc.sbuf_tensor(mk.name("xs16"), [128, 8 * 512], BF16))
        xsq16 = es2.enter_context(nc.sbuf_tensor(mk.name("xsq16"), [128, 8 * 512], BF16))
        x1r = es2.enter_context(nc.sbuf_tensor(mk.name("x1r"), [128, 8 * 512], F32))
        st_mu = es2.enter_context(nc.sbuf_tensor(mk.name("st_mu"), [1, 512], F32))
        st_a = es2.enter_context(nc.sbuf_tensor(mk.name("st_a"), [1, 512], F32))
        st_t = es2.enter_context(nc.sbuf_tensor(mk.name("st_t"), [1, 512], F32))
        st_c = es2.enter_context(nc.sbuf_tensor(mk.name("st_c"), [1, 512], F32))
        ps_p = [es2.enter_context(nc.psum_tensor(mk.name("ps_p"), [128, 512], F32))
                for _ in range(2)]
        ps_sum = es2.enter_context(nc.psum_tensor(mk.name("ps_sum"), [1, 512], F32))
        ps_sq = es2.enter_context(nc.psum_tensor(mk.name("ps_sq"), [1, 512], F32))
        ps_A = es2.enter_context(nc.psum_tensor(mk.name("ps_A"), [128, 512], F32))
        ps_C = es2.enter_context(nc.psum_tensor(mk.name("ps_C"), [128, 512], F32))
        ones128 = es2.enter_context(nc.sbuf_tensor(mk.name("o128"), [128, 1], BF16))
        ones1r = es2.enter_context(nc.sbuf_tensor(mk.name("o1r"), [1, 128], F32))
        eps_ln = es2.enter_context(nc.sbuf_tensor(mk.name("epsln"), [1, 1], F32))
        sch.op("vector", lambda e: e.memset(ones128[:, :], 1.0), writes=["po128"])
        sch.op("vector", lambda e: e.memset(ones1r[0:1, :], 1.0), writes=["po1r"])
        sch.op("vector", lambda e: e.memset(eps_ln[0:1, 0:1], 1e-5), writes=["peps"])

        for g in range(NG):
            tok0 = s * T + g * 512
            ltok = g * 512
            for mt in range(KT):
                pi = mt % 2
                xi = mt % 2
                sch.dma("gpsimd", xstg[xi][:, :],
                        dr_ap(xres_d, mt * 128, 128, [[1, 512]], col0=tok0),
                        writes=[("xstg", xi)], group=f"xstg{xi}", tag="xld")
                for k in range(KT):
                    sch.op("tensor", lambda e, mt=mt, k=k, pi=pi, ltok=ltok: e.matmul(
                        ps_p[pi][:, :],
                        w16[:, k * 1024 + mt * 128: k * 1024 + mt * 128 + 128],
                        act16[:, k * T + ltok: k * T + ltok + 512],
                        start=(k == 0), stop=(k == KT - 1)),
                        reads=[w_keys[k]] + act_keys, writes=[("ps_p", pi)], tag="pmm")
                sch.op("vector", lambda e, mt=mt, pi=pi, xi=xi: e.scalar_tensor_tensor(
                    xsum[:, mt * 512:(mt + 1) * 512], ps_p[pi][:, :],
                    bias_col[:, mt:mt + 1], xstg[xi][:, :], ALU.add, ALU.add),
                    reads=[("ps_p", pi), biaskey, ("xstg", xi)],
                    writes=[("xsum", mt)], tag="xsum")
                sch.op("gpsimd", lambda e, mt=mt: e.tensor_copy(
                    out=xs16[:, mt * 512:(mt + 1) * 512],
                    in_=xsum[:, mt * 512:(mt + 1) * 512]),
                    reads=[("xsum", mt)], writes=[("xs16", mt)], tag="xs16")
                sch.op("gpsimd", lambda e, mt=mt: e.tensor_mul(
                    xsq16[:, mt * 512:(mt + 1) * 512],
                    xs16[:, mt * 512:(mt + 1) * 512], xs16[:, mt * 512:(mt + 1) * 512]),
                    reads=[("xs16", mt)], writes=[("xsq16", mt)], tag="xsq")
            for k in range(KT):
                sch.op("tensor", lambda e, k=k: e.matmul(
                    ps_sum[0:1, :], ones128[:, 0:1], xs16[:, k * 512:(k + 1) * 512],
                    start=(k == 0), stop=(k == KT - 1)),
                    reads=["po128", ("xs16", k)], writes=["ps_sum"], tag="summ")
                sch.op("tensor", lambda e, k=k: e.matmul(
                    ps_sq[0:1, :], ones128[:, 0:1], xsq16[:, k * 512:(k + 1) * 512],
                    start=(k == 0), stop=(k == KT - 1)),
                    reads=["po128", ("xsq16", k)], writes=["ps_sq"], tag="sqm")
            sch.op("vector", lambda e: e.tensor_scalar(
                st_mu[0:1, :], ps_sum[0:1, :], 1.0 / 1024.0, None, ALU.mult),
                reads=["ps_sum"], writes=["st_mu"], tag="mu")
            sch.op("vector", lambda e: e.tensor_mul(
                st_t[0:1, :], st_mu[0:1, :], st_mu[0:1, :]),
                reads=["st_mu"], writes=["st_mu2"], tag="mu2")
            sch.op("vector", lambda e: e.scalar_tensor_tensor(
                st_t[0:1, :], ps_sq[0:1, :], 1.0 / 1024.0, st_t[0:1, :],
                ALU.mult, ALU.subtract),
                reads=["ps_sq", "st_mu2"], writes=["st_var"], tag="var")
            sch.op("scalar", lambda e: e.activation(
                st_t[0:1, :], st_t[0:1, :], AF.Ln, bias=eps_ln[0:1, 0:1]),
                reads=["st_var", "peps"], writes=["st_sd"], tag="sd")
            sch.op("scalar", lambda e: e.activation(
                st_a[0:1, :], st_t[0:1, :], AF.Exp, scale=-0.5),
                reads=["st_sd"], writes=["st_a"], tag="rstd")
            sch.op("vector", lambda e: e.scalar_tensor_tensor(
                st_c[0:1, :], st_mu[0:1, :], -1.0, st_a[0:1, :], ALU.mult, ALU.mult),
                reads=["st_mu", "st_a"], writes=["st_c"], tag="stc")
            sch.op("tensor", lambda e: e.matmul(
                ps_A[:, :], ones1r[0:1, :], st_a[0:1, :], start=True, stop=True),
                reads=["po1r", "st_a"], writes=["ps_A"], tag="bA")
            sch.op("tensor", lambda e: e.matmul(
                ps_C[:, :], ones1r[0:1, :], st_c[0:1, :], start=True, stop=True),
                reads=["po1r", "st_c"], writes=["ps_C"], tag="bC")
            for mt in range(KT):
                sch.op("vector", lambda e, mt=mt: e.tensor_mul(
                    x1r[:, mt * 512:(mt + 1) * 512],
                    xsum[:, mt * 512:(mt + 1) * 512], ps_A[:, :]),
                    reads=[("xsum", mt), "ps_A"], writes=[("x1ra", mt)], tag="lnA")
                sch.op("vector", lambda e, mt=mt: e.tensor_add(
                    x1r[:, mt * 512:(mt + 1) * 512],
                    x1r[:, mt * 512:(mt + 1) * 512], ps_C[:, :]),
                    reads=[("x1ra", mt), "ps_C"], writes=[("x1r", mt)], tag="lnC")
            sch.dma("scalar",
                    dr_ap(xout_d, 0, 128, [[128 * TT, 8], [1, 512]], col0=tok0),
                    sb_ap(x1r, 0, 128, [[512, 8], [1, 512]]),
                    reads=[("x1r", mtt) for mtt in range(KT)],
                    writes=[("xout", s, g)], group="xout_st", tag="xoutst")


def emit_phaseB(mk, es, xin, wts, host, x2_out):
    nc, sch = mk.nc, mk.sch
    B, T, TT = mk.B, mk.T, mk.TT
    D = 1024
    KT = 8
    W = 512
    NW = T // W
    NG = T // 512

    xb = [nc.dram_tensor(mk.name("xb_d"), [D, TT], F32).ap() for _ in range(2)]
    mrow_d = nc.dram_tensor("mrowB_d", [8, 512], BF16).ap()

    def hconst_in(es_c, name, rows, cols, dt=BF16):
        t = es_c.enter_context(nc.sbuf_tensor(mk.name("hB_" + name), [rows, cols], dt))
        q = "gpsimd" if dt == BF16 else "sync"
        sch.dma(q, t[0:rows, 0:cols], host[name][0:rows, 0:cols],
                writes=["hB" + name, ("chain", "hcB", q)], group="hcB" + q)
        return t

    lnBg = hconst_in(es, "lnBg", 128, 8 * 2, F32)
    lnBb = hconst_in(es, "lnBb", 128, 8 * 2, F32)
    b1a = hconst_in(es, "b1a", 128, 22 * 2, F32)
    b1g = hconst_in(es, "b1g", 128, 22 * 2, F32)
    b2c = hconst_in(es, "b2c", 128, 8 * 2, F32)

    for l in range(2):
        xb_cur = xin if l == 0 else xb[1]
        xb_cur_keys = ([] if l == 0 else [("xbd", 1, s2, w2_, mt2)
                                          for s2 in range(B) for w2_ in range(NW)
                                          for mt2 in range(KT)])
        with ExitStack() as es_a:
            wqkv16 = es_a.enter_context(nc.sbuf_tensor(mk.name("wqkvB"),
                                                       [128, KT * 1536], BF16))
            cast_stream(mk, es_a, wts["wqkv"], l * 1024, D, 0, 1536, wqkv16,
                        "wqkvB" + str(l), out_col_stride=1536)
            wo16 = es_a.enter_context(nc.sbuf_tensor(mk.name("woB"),
                                                     [128, 4 * 1024], BF16))
            cast_stream(mk, es_a, wts["wo"], l * 512, 512, 0, 1024, wo16,
                        "woB" + str(l), out_col_stride=1024)
            wqk = "wqkvB" + str(l)
            wok = "woB" + str(l)
            hm8 = hconst_in(es_a, "hm8", 128, 4 * 8)
            ind8 = hconst_in(es_a, "ind8", 8, 4 * 128)
            prot = hconst_in(es_a, "prot", 128, 128)
            csq = hconst_in(es_a, "csq", 128, T)
            ssq = hconst_in(es_a, "ssq", 128, T)
            csk = hconst_in(es_a, "csk", 128, T)
            ssk = hconst_in(es_a, "ssk", 128, T)
            mcur = hconst_in(es_a, "mcur", 128, 4 * 512)
            mprev = hconst_in(es_a, "mprev", 128, 4 * 512)
            ones1_32 = hconst_in(es_a, "ones1f", 1, 128, F32)
            qs_col = hconst_in(es_a, "qs_col", 128, 4 * 2, F32)
            ks_col = hconst_in(es_a, "ks_col", 128, 4 * 2, F32)
            identB = hconst_in(es_a, "identB", 128, 128)
            o128B = hconst_in(es_a, "o128B", 128, 1)
            epsB24 = es_a.enter_context(nc.sbuf_tensor(mk.name("epsB24"),
                                                       [8, 1], F32))
            sch.op("vector", lambda e: e.memset(epsB24[0:8, 0:1], 1e-24),
                   writes=["epsB24c"], tag="epsB24c")

            h16 = es_a.enter_context(nc.sbuf_tensor(mk.name("h16B"),
                                                    [128, KT * 513], BF16))
            qraw = es_a.enter_context(nc.sbuf_tensor(mk.name("qrawB"),
                                                     [128, 4 * 512], BF16))
            kraw = es_a.enter_context(nc.sbuf_tensor(mk.name("krawB"),
                                                     [128, 4 * 512], BF16))
            sqr = [es_a.enter_context(nc.sbuf_tensor(mk.name("sqrB"),
                                                     [128, 512], BF16))
                   for _ in range(2)]
            qh = es_a.enter_context(nc.sbuf_tensor(mk.name("qhB"),
                                                   [128, 4 * 512], BF16))
            kh = [es_a.enter_context(nc.sbuf_tensor(mk.name("khB"),
                                                    [128, 4 * 512], BF16))
                  for _ in range(2)]
            v_sb = [es_a.enter_context(nc.sbuf_tensor(mk.name("vsbB"),
                                                      [128, 4 * 520], BF16))
                    for _ in range(2)]
            for i in range(2):
                sch.op("vector", lambda e, i=i: e.memset(
                    sb_ap(v_sb[i], 0, 128, [[520, 4], [65, 8]], col0=64), 1.0),
                    writes=[("vones", i)], tag="vones")
            rot = es_a.enter_context(nc.sbuf_tensor(mk.name("rotB"),
                                                    [128, 512], BF16))
            n8a = es_a.enter_context(nc.sbuf_tensor(mk.name("n8a"), [8, 512], F32))
            n8b = es_a.enter_context(nc.sbuf_tensor(mk.name("n8b"), [8, 512], F32))
            f8 = [es_a.enter_context(nc.sbuf_tensor(mk.name("f8"), [8, 512], BF16))
                  for _ in range(2)]
            kmax = es_a.enter_context(nc.sbuf_tensor(mk.name("kmax"), [8, 2], F32))
            mneg16 = es_a.enter_context(nc.sbuf_tensor(mk.name("mnegB"),
                                                       [8, 512], BF16))
            qaug2 = [es_a.enter_context(nc.sbuf_tensor(mk.name("qaugB"),
                                                       [128, 512], BF16))
                     for _ in range(2)]
            kaug = [es_a.enter_context(nc.sbuf_tensor(mk.name("kaugB"),
                                                      [128, 1024], BF16))
                    for _ in range(2)]
            pbuf2 = [es_a.enter_context(nc.sbuf_tensor(mk.name("pbufB"),
                                                       [128, 8 * 512], BF16))
                     for _ in range(2)]
            rden = es_a.enter_context(nc.sbuf_tensor(mk.name("rdenB"), [1, 512], F32))
            rdenb2 = [es_a.enter_context(nc.sbuf_tensor(mk.name("rdenbB"),
                                                        [64, 512], F32))
                      for _ in range(2)]
            rden2 = [es_a.enter_context(nc.sbuf_tensor(mk.name("rdenB2"),
                                                       [1, 512], F32))
                     for _ in range(2)]
            obuf = es_a.enter_context(nc.sbuf_tensor(mk.name("obufB"),
                                                     [128, 4 * 512], BF16))
            xstg = [es_a.enter_context(nc.sbuf_tensor(mk.name("xstgB"),
                                                      [128, 512], F32))
                    for _ in range(2)]
            xnew = es_a.enter_context(nc.sbuf_tensor(mk.name("xnewB"),
                                                     [128, 512], F32))
            psB = [es_a.enter_context(nc.psum_tensor(mk.name("psB"), [128, 512], F32))
                   for _ in range(6)]
            ps8 = [es_a.enter_context(nc.psum_tensor(mk.name("ps8"), [8, 512], F32))
                   for _ in range(2)]
            maxn = es_a.enter_context(nc.sbuf_tensor(mk.name("maxnB"), [128, 8], F32))
            rdiag = es_a.enter_context(nc.sbuf_tensor(mk.name("rdiagB"),
                                                      [128, 128], BF16))
            for i in range(2):
                sch.op("vector", lambda e, i=i: e.memset(kaug[i][64:65, :], 1.0),
                       writes=[("kaones", i), ("kaones", i, 1)], tag="kaonesB")

            for s in range(B):
                for w in range(NW):
                    tok0 = s * T + w * W
                    wb = w % 2
                    for j in range(KT):
                        if w == 0:
                            sch.dma("gpsimd", h16[:, j * 513 + 1: j * 513 + 513],
                                    dr_ap(xb_cur, j * 128, 128, [[1, 512]], col0=tok0),
                                    reads=xb_cur_keys,
                                    writes=[("h16", j), ("chain", "h16", j % 2)],
                                    group=f"h16_{j % 2}", tag="hld")
                            sch.op("vector", lambda e, j=j: e.memset(
                                h16[:, j * 513: j * 513 + 1], 0.0),
                                reads=[], writes=[("h16z", j)], tag="hz")
                        else:
                            sch.dma("gpsimd", h16[:, j * 513: j * 513 + 513],
                                    dr_ap(xb_cur, j * 128, 128, [[1, 513]],
                                          col0=tok0 - 1),
                                    reads=xb_cur_keys,
                                    writes=[("h16", j), ("h16z", j),
                                            ("chain", "h16", j % 2)],
                                    group=f"h16_{j % 2}", tag="hld")

                    def hsl(j):
                        if j < 4:
                            return h16[:, j * 513 + 1: j * 513 + 513]
                        return h16[:, j * 513: j * 513 + 512]

                    def hkeys(j):
                        return [("h16", j)] + ([("h16z", j)] if j >= 4 else [])

                    for mt in range(8):
                        isq = mt < 4
                        pi = mt % 2
                        for k in range(KT):
                            sch.op("tensor", lambda e, mt=mt, k=k, pi=pi: e.matmul(
                                psB[pi][:, :],
                                wqkv16[:, k * 1536 + mt * 128: k * 1536 + mt * 128 + 128],
                                hsl(k),
                                start=(k == 0), stop=(k == KT - 1)),
                                reads=[(wqk, k)] + hkeys(k),
                                writes=[("psB", pi)], tag="qkBmm")
                        dst, dkey = (qraw, "qraw") if isq else (kraw, "kraw")
                        ti = mt % 4
                        sch.op("vector", lambda e, dst=dst, ti=ti, pi=pi: e.tensor_copy(
                            out=dst[:, ti * 512:(ti + 1) * 512], in_=psB[pi][:, :]),
                            reads=[("psB", pi)], writes=[(dkey, ti)], tag="qkBevac")
                        sch.op("gpsimd", lambda e, dst=dst, ti=ti, pi=pi: e.tensor_mul(
                            sqr[pi][:, :], dst[:, ti * 512:(ti + 1) * 512],
                            dst[:, ti * 512:(ti + 1) * 512]),
                            reads=[(dkey, ti)], writes=[("sqrB", pi)], tag="qkBsq")
                        wu = 0 if isq else 1
                        sch.op("tensor", lambda e, wu=wu, ti=ti, pi=pi: e.matmul(
                            ps8[wu][0:8, :], hm8[:, ti * 8:(ti + 1) * 8],
                            sqr[pi][:, :], start=(ti == 0), stop=(ti == 3)),
                            reads=["hBhm8", ("sqrB", pi)], writes=[("ps8", wu)],
                            tag="ssBmm")
                    for wu, nm in ((0, "n8a"), (1, "n8b")):
                        t8 = n8a if wu == 0 else n8b
                        sch.op("scalar", lambda e, t8=t8, wu=wu: e.activation(
                            t8[:, :], ps8[wu][0:8, :], AF.Ln,
                            bias=epsB24[0:8, 0:1]),
                            reads=[("ps8", wu), "epsB24c"], writes=[(nm, 0)],
                            tag="fB1")
                        sch.op("scalar", lambda e, t8=t8, wu=wu: e.activation(
                            f8[wu][:, :], t8[:, :], AF.Exp, scale=-0.5),
                            reads=[(nm, 0)], writes=[(nm, 2)], tag="fB3")
                    for isq in (True, False):
                        t8, nm = (n8a, "n8a") if isq else (n8b, "n8b")
                        raw, rkey = (qraw, "qraw") if isq else (kraw, "kraw")
                        dst = qh if isq else kh[wb]
                        dkey = "qh" if isq else ("kh", wb)
                        scol, sckey = (qs_col, "hBqs_col") if isq else (ks_col, "hBks_col")
                        cs, cskey = (csq, "hBcsq") if isq else (csk, "hBcsk")
                        ss, sskey = (ssq, "hBssq") if isq else (ssk, "hBssk")
                        wu0 = 0 if isq else 1
                        for ti in range(4):
                            pi = ti % 2
                            sch.op("tensor", lambda e, ti=ti, wu0=wu0, pi=pi: e.matmul(
                                psB[pi][:, :], ind8[0:8, ti * 128:(ti + 1) * 128],
                                f8[wu0][:, :], start=True, stop=True),
                                reads=["hBind8", (nm, 2)], writes=[("psB", pi)],
                                tag="fbBmm")
                            sch.op("vector", lambda e, ti=ti, raw=raw, scol=scol,
                                   pi=pi: e.scalar_tensor_tensor(
                                sqr[pi][:, :], psB[pi][:, :],
                                scol[:, l * 4 + ti: l * 4 + ti + 1],
                                raw[:, ti * 512:(ti + 1) * 512], ALU.mult, ALU.mult),
                                reads=[("psB", pi), sckey, (rkey, ti)],
                                writes=[("sqrB", pi)], tag="nrmB")
                            sch.op("tensor", lambda e, pi=pi: e.matmul(
                                psB[2 + pi][:, :], prot[:, :], sqr[pi][:, :],
                                start=True, stop=True),
                                reads=["hBprot", ("sqrB", pi)],
                                writes=[("psB", 2 + pi)], tag="rotBmm")
                            wt0 = w * W
                            sch.op("vector", lambda e, ti=ti, cs=cs, wt0=wt0, pi=pi:
                                   e.tensor_mul(
                                       rot[:, :], sqr[pi][:, :],
                                       cs[:, wt0: wt0 + 512]),
                                   reads=[("sqrB", pi), cskey], writes=["rotB"],
                                   tag="rcs")
                            sch.op("vector", lambda e, ti=ti, ss=ss, wt0=wt0, dst=dst,
                                   pi=pi: e.tensor_mul(
                                       dst[:, ti * 512:(ti + 1) * 512],
                                       psB[2 + pi][:, :], ss[:, wt0: wt0 + 512]),
                                   reads=[("psB", 2 + pi), sskey],
                                   writes=[(dkey, ti, "a")], tag="rss")
                            sch.op("vector", lambda e, ti=ti, dst=dst: e.tensor_add(
                                dst[:, ti * 512:(ti + 1) * 512],
                                dst[:, ti * 512:(ti + 1) * 512], rot[:, :]),
                                reads=[(dkey, ti, "a"), "rotB"],
                                writes=[(dkey, ti)], tag="radd")
                    for ti in range(4):
                        pi = ti % 2
                        for k in range(KT):
                            sch.op("tensor", lambda e, ti=ti, k=k, pi=pi: e.matmul(
                                psB[pi][:, :],
                                hsl(k)[:, ti * 128: ti * 128 + 128],
                                wqkv16[:, k * 1536 + 1024: k * 1536 + 1536],
                                start=(k == 0), stop=(k == KT - 1)),
                                reads=[(wqk, k)] + hkeys(k),
                                writes=[("psB", pi)], tag="vBmm")
                        sch.op("vector", lambda e, ti=ti, wb=wb, pi=pi: e.tensor_copy(
                            out=sb_ap(v_sb[wb], 0, 128, [[65, 8], [1, 64]],
                                      col0=ti * 520),
                            in_=psB[pi][:, :]),
                            reads=[("psB", pi), ("vones", wb)],
                            writes=[("vsb", wb, ti)], tag="vBevac")
                    for h in range(8):
                        ht = h // 2
                        hr = (h % 2) * 64
                        hp = h % 2
                        pbuf = pbuf2[hp]
                        rden = rden2[hp]
                        rdenb = rdenb2[hp]
                        kts = range(8) if w > 0 else range(4, 8)
                        for kt in kts:
                            sc_t = (psB[2], psB[3], psB[5])[kt % 3]
                            sc_k = (("psB", 2), ("psB", 3), ("psB", 5))[kt % 3]
                            ksrc, skey = ((kh[1 - wb], ("kh", 1 - wb, ht))
                                          if kt < 4 else (kh[wb], ("kh", wb, ht)))
                            sch.op("tensor", lambda e, kt=kt, ksrc=ksrc, ht=ht,
                                   hr=hr, sc_t=sc_t: e.matmul(
                                sc_t[:, :],
                                ksrc[hr:hr + 64, ht * 512 + (kt % 4) * 128:
                                     ht * 512 + (kt % 4) * 128 + 128],
                                qh[hr:hr + 64, ht * 512:(ht + 1) * 512],
                                start=True, stop=True),
                                reads=[skey, ("qh", ht)],
                                writes=[sc_k], tag="sBmm")
                            sch.op("scalar", lambda e, kt=kt, sc_t=sc_t: e.activation(
                                pbuf[:, kt * 512:(kt + 1) * 512], sc_t[:, :],
                                AF.Exp),
                                reads=[sc_k], writes=[("pbufB", hp, kt, 0)],
                                tag="eBexp")
                            if kt % 2 == 1:
                                msk = mprev if kt < 4 else mcur
                                mkk = "hBmprev" if kt < 4 else "hBmcur"
                                mi = (kt - 1) % 4
                                sch.op("vector", lambda e, kt=kt, msk=msk, mi=mi:
                                       e.tensor_mul(
                                           pbuf[:, (kt - 1) * 512:(kt + 1) * 512],
                                           pbuf[:, (kt - 1) * 512:(kt + 1) * 512],
                                           msk[:, mi * 512:(mi + 2) * 512]),
                                       reads=[("pbufB", hp, kt - 1, 0),
                                              ("pbufB", hp, kt, 0), mkk],
                                       writes=[("pbufB", hp, kt - 1),
                                               ("pbufB", hp, kt)], tag="eBmask")
                        for i, kt in enumerate(kts):
                            vs = v_sb[1 - wb] if kt < 4 else v_sb[wb]
                            vkey = ("vsb", 1 - wb, kt % 4) if kt < 4 else \
                                ("vsb", wb, kt % 4)
                            vok = ("vones", 1 - wb) if kt < 4 else ("vones", wb)
                            sch.op("tensor", lambda e, kt=kt, vs=vs, h=h, i=i,
                                   n=len(kts): e.matmul(
                                psB[0 if hp == 0 else 4][0:65, :],
                                vs[:, (kt % 4) * 520 + h * 65:
                                   (kt % 4) * 520 + h * 65 + 65],
                                pbuf[:, kt * 512:(kt + 1) * 512],
                                start=(i == 0), stop=(i == n - 1)),
                                reads=[vkey, vok, ("pbufB", hp, kt)],
                                writes=[("psB", 0 if hp == 0 else 4)], tag="pvB")
                        pv_b = 0 if hp == 0 else 4
                        rb_b = 1 if hp == 0 else 5
                        sch.op("scalar", lambda e, pv_b=pv_b: e.activation(
                            rden[0:1, :], psB[pv_b][64:65, :], AF.Ln),
                            reads=[("psB", pv_b)], writes=[("rdenB", hp, "ln")],
                            tag="rdBln")
                        sch.op("scalar", lambda e: e.activation(
                            rden[0:1, :], rden[0:1, :], AF.Exp, scale=-1.0),
                            reads=[("rdenB", hp, "ln")], writes=[("rdenB", hp)],
                            tag="rdB")
                        sch.op("tensor", lambda e, rb_b=rb_b: e.matmul(
                            psB[rb_b][0:64, :], ones1_32[0:1, 0:64], rden[0:1, :],
                            start=True, stop=True),
                            reads=["hBones1f", ("rdenB", hp)], writes=[("psB", rb_b)],
                            tag="rbB")
                        sch.op("scalar", lambda e, rb_b=rb_b: e.copy(
                            rdenb[:, :], psB[rb_b][0:64, :]),
                            reads=[("psB", rb_b)], writes=[("rdenbB", hp)], tag="rbcB")
                        sch.op("vector", lambda e, ht=ht, hr=hr, pv_b=pv_b: e.tensor_mul(
                            obuf[hr:hr + 64, ht * 512:(ht + 1) * 512],
                            psB[pv_b][0:64, :], rdenb[:, :]),
                            reads=[("psB", pv_b), ("rdenbB", hp)],
                            writes=[("obufB", h)], tag="onB")
                    for mt in range(KT):
                        pi = mt % 2
                        xi = mt % 2
                        sch.dma("sync", xstg[xi][:, :],
                                dr_ap(xb_cur, mt * 128, 128, [[1, 512]], col0=tok0),
                                reads=xb_cur_keys,
                                writes=[("xstgB", xi)], group=f"xstB{xi}", tag="xldB")
                        for k in range(4):
                            sch.op("tensor", lambda e, mt=mt, k=k, pi=pi: e.matmul(
                                psB[2 + pi][:, :],
                                wo16[:, k * 1024 + mt * 128: k * 1024 + mt * 128 + 128],
                                obuf[:, k * 512:(k + 1) * 512],
                                start=(k == 0), stop=(k == 3)),
                                reads=[(wok, k)] + [("obufB", hh) for hh in range(8)],
                                writes=[("psB", 2 + pi)], tag="woBmm")
                        sch.op("vector", lambda e, pi=pi, xi=xi: e.tensor_add(
                            xnew[:, :], psB[2 + pi][:, :], xstg[xi][:, :]),
                            reads=[("psB", 2 + pi), ("xstgB", xi)],
                            writes=["xnewB"], tag="xnB")
                        sch.dma("scalar",
                                dr_ap(xb[0], mt * 128, 128, [[1, 512]], col0=tok0),
                                xnew[:, :],
                                reads=["xnewB"], writes=[("xbd", 0, s, w, mt)],
                                group="xb0_st", tag="xstB")
            sch.barrier("endBattn")

        with ExitStack() as es_f:
            w116 = es_f.enter_context(nc.sbuf_tensor(mk.name("w1B"),
                                                     [128, KT * 5460], BF16))
            cast_stream(mk, es_f, wts["w1"], l * 1024, D, 0, 5460, w116,
                        "w1B" + str(l), out_col_stride=5460)
            w216 = es_f.enter_context(nc.sbuf_tensor(mk.name("w2B"),
                                                     [128, 22 * 1024], BF16))
            for kk in range(22):
                rows = 128 if kk < 21 else 42
                sch.dma("gpsimd", w216[0:rows, kk * 1024:(kk + 1) * 1024],
                        dr_ap(wts["w2"], l * 2730 + kk * 128, rows, [[1, 1024]]),
                        writes=[("w2B" + str(l), kk), ("chain", "w2B", kk % 2)],
                        group=f"w2B_{kk % 2}", tag="w2l")
            w1k = "w1B" + str(l)
            w2k = "w2B" + str(l)
            psF = [es_f.enter_context(nc.psum_tensor(mk.name("psFB"),
                                                     [128, 512], F32))
                   for _ in range(4)]
            ext = {"ps_sum": (psF[0], ("psFB", 0)), "ps_sq": (psF[1], ("psFB", 1)),
                   "ps_A": (psF[2], ("psFB", 2)), "ps_C": (psF[3], ("psFB", 3))}
            lnt = alloc_ln_tail(mk, es_f, ext_psums=ext, need_stage=False)
            hln16 = es_f.enter_context(nc.sbuf_tensor(mk.name("hlnB"),
                                                      [128, KT * 512], BF16))
            gg16 = [es_f.enter_context(nc.sbuf_tensor(mk.name("ggB"),
                                                      [128, 512], BF16))
                    for _ in range(2)]
            hh16 = es_f.enter_context(nc.sbuf_tensor(mk.name("hhB"),
                                                     [128, 22 * 512], BF16))
            xstg2 = [es_f.enter_context(nc.sbuf_tensor(mk.name("xstg2B"),
                                                       [128, 512], F32))
                     for _ in range(2)]
            xnew2 = es_f.enter_context(nc.sbuf_tensor(mk.name("xnew2B"),
                                                      [128, 512], F32))

            for s in range(B):
                for w in range(NW):
                    tok0 = s * T + w * W
                    for mt in range(KT):
                        if mt < 4 or w > 0:
                            c0 = tok0 if mt < 4 else tok0 - 1
                            sch.dma("sync", lnt["xsum"][:, mt * 512:(mt + 1) * 512],
                                    dr_ap(xb[0], mt * 128, 128, [[1, 512]], col0=c0),
                                    reads=[("xbd", 0, s, w2_, mt2)
                                           for w2_ in range(NW)
                                           for mt2 in range(KT)],
                                    writes=[(lnt["kid"], "pre", mt),
                                            ("chain", "lnB", mt % 2)],
                                    group=f"lnB{mt % 2}", tag="lnBi")
                        else:
                            sch.dma("sync",
                                    lnt["xsum"][:, mt * 512 + 1:(mt + 1) * 512],
                                    dr_ap(xb[0], mt * 128, 128, [[1, 511]],
                                          col0=tok0),
                                    reads=[("xbd", 0, s, w2_, mt2)
                                           for w2_ in range(NW)
                                           for mt2 in range(KT)],
                                    writes=[(lnt["kid"], "pre", mt, "d"),
                                            ("chain", "lnB", mt % 2)],
                                    group=f"lnB{mt % 2}", tag="lnBi0")
                            sch.op("vector", lambda e, mt=mt: e.memset(
                                lnt["xsum"][:, mt * 512: mt * 512 + 1], 0.0),
                                reads=[(lnt["kid"], "pre", mt, "d")],
                                writes=[(lnt["kid"], "pre", mt)], tag="lnBz")
                    emit_ln_gen(mk, lnt, f"BL{l}", [],
                                [("sbuf", lambda mt: (
                                    hln16[:, mt * 512:(mt + 1) * 512],
                                    ("hlnB", mt), []))],
                                gamma=lnBg[:, l * 8: l * 8 + 8],
                                beta=lnBb[:, l * 8: l * 8 + 8],
                                gkeys=["hBlnBg", "hBlnBb"], pre_filled=True)
                    for i in range(22):
                        mw = 128 if i < 21 else 42
                        pi = i % 2
                        for k in range(KT):
                            sch.op("tensor", lambda e, i=i, k=k, mw=mw, pi=pi: e.matmul(
                                psF[2 + pi][0:mw, :],
                                w116[:, k * 5460 + 2730 + i * 128:
                                     k * 5460 + 2730 + i * 128 + mw],
                                hln16[:, k * 512:(k + 1) * 512],
                                start=(k == 0), stop=(k == KT - 1)),
                                reads=[(w1k, k)] + [("hlnB", kk2) for kk2 in range(KT)],
                                writes=[("psFB", 2 + pi)], tag="gBmm")
                        sch.op("scalar", lambda e, i=i, mw=mw, pi=pi: e.activation(
                            gg16[pi][0:mw, :], psF[2 + pi][0:mw, :], AF.Gelu,
                            bias=b1g[0:mw, l * 22 + i: l * 22 + i + 1]),
                            reads=[("psFB", 2 + pi), "hBb1g"],
                            writes=[("ggB", pi)], tag="gBgelu")
                        for k in range(KT):
                            sch.op("tensor", lambda e, i=i, k=k, mw=mw, pi=pi: e.matmul(
                                psF[2 + pi][0:mw, :],
                                w116[:, k * 5460 + i * 128: k * 5460 + i * 128 + mw],
                                hln16[:, k * 512:(k + 1) * 512],
                                start=(k == 0), stop=(k == KT - 1)),
                                reads=[(w1k, k)] + [("hlnB", kk2) for kk2 in range(KT)],
                                writes=[("psFB", 2 + pi)], tag="aBmm")
                        sch.op("vector", lambda e, i=i, mw=mw, pi=pi:
                               e.scalar_tensor_tensor(
                                   hh16[0:mw, i * 512:(i + 1) * 512],
                                   psF[2 + pi][0:mw, :],
                                   b1a[0:mw, l * 22 + i: l * 22 + i + 1],
                                   gg16[pi][0:mw, :], ALU.add, ALU.mult),
                               reads=[("psFB", 2 + pi), "hBb1a", ("ggB", pi)],
                               writes=[("hhB", i)], tag="hB")
                    for mt in range(KT):
                        pi = mt % 2
                        xi = mt % 2
                        sch.dma("sync", xstg2[xi][:, :],
                                dr_ap(xb[0], mt * 128, 128, [[1, 512]], col0=tok0),
                                reads=[("xbd", 0, s, w, mt)],
                                writes=[("xstg2B", xi)], group=f"xs2B{xi}",
                                tag="xldB2")
                        for kk in range(22):
                            kw = 128 if kk < 21 else 42
                            sch.op("tensor", lambda e, mt=mt, kk=kk, kw=kw, pi=pi:
                                   e.matmul(
                                psF[2 + pi][:, :],
                                w216[0:kw, kk * 1024 + mt * 128:
                                     kk * 1024 + mt * 128 + 128],
                                hh16[0:kw, kk * 512:(kk + 1) * 512],
                                start=(kk == 0), stop=(kk == 21)),
                                reads=[(w2k, kk), ("hhB", kk)],
                                writes=[("psFB", 2 + pi)], tag="w2Bmm")
                        sch.op("vector", lambda e, mt=mt, pi=pi, xi=xi:
                               e.scalar_tensor_tensor(
                                   xnew2[:, :], psF[2 + pi][:, :],
                                   b2c[:, l * 8 + mt: l * 8 + mt + 1],
                                   xstg2[xi][:, :], ALU.add, ALU.add),
                               reads=[("psFB", 2 + pi), "hBb2c", ("xstg2B", xi)],
                               writes=["xnew2B"], tag="xnB2")
                        sch.dma("scalar",
                                dr_ap(xb[1], mt * 128, 128, [[1, 512]], col0=tok0),
                                xnew2[:, :],
                                reads=["xnew2B"], writes=[("xbd", 1, s, w, mt)],
                                group="xb1_st", tag="xstB2")
            sch.barrier("endBffn")

    with ExitStack() as es_l:
        lnt2 = alloc_ln_tail(mk, es_l)
        for s in range(B):
            for g in range(NG):
                tok0 = s * T + g * 512
                emit_ln_gen(mk, lnt2, "BF",
                            [(xb[1], tok0, [("xbd", 1, s, g, mt2)
                                            for mt2 in range(KT)]),
                             (xin, tok0, [])],
                            [("dram", x2_out, tok0)])
        sch.barrier("endB")


def emit_ln_gen(mk, d, name, inputs, sinks, gamma=None, beta=None, gkeys=(),
                pre_filled=False):
    sch, nc = mk.sch, mk.nc
    KT = 8
    xsum = d["xsum"]
    kid = d["kid"]
    for ii, (ap, tok0, rkeys) in enumerate(inputs):
        init = (not pre_filled) and ii == 0
        for mt in range(KT):
            xi = mt % 2
            if init:
                sch.dma("sync", xsum[:, mt * 512:(mt + 1) * 512],
                        dr_ap(ap, mt * 128, 128, [[1, 512]], col0=tok0),
                        reads=list(rkeys),
                        writes=[(kid, "pre", mt), ("chain", name, "xi", mt % 2)],
                        group=f"{name}xi{mt % 2}", tag="lnxi")
            else:
                sch.dma("sync", d["xstg"][xi][:, :],
                        dr_ap(ap, mt * 128, 128, [[1, 512]], col0=tok0),
                        reads=list(rkeys),
                        writes=[(kid, "xstg", xi)], group=f"{name}xs{xi}",
                        tag="lnxa")
                sch.op("vector", lambda e, mt=mt, xi=xi: e.tensor_add(
                    xsum[:, mt * 512:(mt + 1) * 512],
                    xsum[:, mt * 512:(mt + 1) * 512], d["xstg"][xi][:, :]),
                    reads=[(kid, "pre", mt), (kid, "xstg", xi)],
                    writes=[(kid, "pre", mt)], tag="lnadd")
    for mt in range(KT):
        xi = mt % 2
        sch.op("gpsimd", lambda e, mt=mt, xi=xi: e.tensor_copy(
            out=d["xs16"][xi][:, :], in_=xsum[:, mt * 512:(mt + 1) * 512]),
            reads=[(kid, "pre", mt)], writes=[(kid, "xs16", xi)], tag="xs16")
        sch.op("gpsimd", lambda e, mt=mt, xi=xi: e.tensor_mul(
            d["sqr"][xi][:, :], d["xs16"][xi][:, :], d["xs16"][xi][:, :]),
            reads=[(kid, "xs16", xi)], writes=[(kid, "sqr", xi)], tag="xsq")
        sch.op("tensor", lambda e, mt=mt, xi=xi: e.matmul(
            d["ps_sum"][0:1, :], d["o128"][:, 0:1], d["xs16"][xi][:, :],
            start=(mt == 0), stop=(mt == KT - 1)),
            reads=[d["k_o128"], (kid, "xs16", xi)], writes=[d["k_ps_sum"]],
            tag="summ")
        sch.op("tensor", lambda e, mt=mt, xi=xi: e.matmul(
            d["ps_sq"][0:1, :], d["o128"][:, 0:1], d["sqr"][xi][:, :],
            start=(mt == 0), stop=(mt == KT - 1)),
            reads=[d["k_o128"], (kid, "sqr", xi)], writes=[d["k_ps_sq"]],
            tag="sqm")
    sch.op("vector", lambda e: e.tensor_scalar(
        d["st_mu"][0:1, :], d["ps_sum"][0:1, :], 1.0 / 1024.0, None, ALU.mult),
        reads=[d["k_ps_sum"]], writes=[(kid, "st_mu")], tag="mu")
    sch.op("vector", lambda e: e.tensor_mul(
        d["st_t"][0:1, :], d["st_mu"][0:1, :], d["st_mu"][0:1, :]),
        reads=[(kid, "st_mu")], writes=[(kid, "st_mu2")], tag="mu2")
    sch.op("vector", lambda e: e.scalar_tensor_tensor(
        d["st_t"][0:1, :], d["ps_sq"][0:1, :], 1.0 / 1024.0, d["st_t"][0:1, :],
        ALU.mult, ALU.subtract),
        reads=[d["k_ps_sq"], (kid, "st_mu2")], writes=[(kid, "st_var")], tag="var")
    sch.op("scalar", lambda e: e.activation(
        d["st_t"][0:1, :], d["st_t"][0:1, :], AF.Ln, bias=d["eps"][0:1, 0:1]),
        reads=[(kid, "st_var"), d["k_eps"]], writes=[(kid, "st_sd")], tag="sd")
    sch.op("scalar", lambda e: e.activation(
        d["st_a"][0:1, :], d["st_t"][0:1, :], AF.Exp, scale=-0.5),
        reads=[(kid, "st_sd")], writes=[(kid, "st_a")], tag="rstd")
    sch.op("vector", lambda e: e.scalar_tensor_tensor(
        d["st_c"][0:1, :], d["st_mu"][0:1, :], -1.0, d["st_a"][0:1, :],
        ALU.mult, ALU.mult),
        reads=[(kid, "st_mu"), (kid, "st_a")], writes=[(kid, "st_c")], tag="stc")
    sch.op("tensor", lambda e: e.matmul(
        d["ps_A"][:, :], d["o1r"][0:1, :], d["st_a"][0:1, :], start=True, stop=True),
        reads=[d["k_o1r"], (kid, "st_a")], writes=[d["k_ps_A"]], tag="bA")
    sch.op("tensor", lambda e: e.matmul(
        d["ps_C"][:, :], d["o1r"][0:1, :], d["st_c"][0:1, :], start=True, stop=True),
        reads=[d["k_o1r"], (kid, "st_c")], writes=[d["k_ps_C"]], tag="bC")
    for mt in range(KT):
        xi = mt % 2
        sch.op("vector", lambda e, mt=mt, xi=xi: e.tensor_mul(
            d["x1r"][xi][:, :], xsum[:, mt * 512:(mt + 1) * 512], d["ps_A"][:, :]),
            reads=[(kid, "pre", mt), d["k_ps_A"]],
            writes=[(kid, "x1r", xi)], tag="lnA")
        sch.op("vector", lambda e, mt=mt, xi=xi: e.tensor_add(
            d["x1r"][xi][:, :], d["x1r"][xi][:, :], d["ps_C"][:, :]),
            reads=[(kid, "x1r", xi), d["k_ps_C"]],
            writes=[(kid, "x1r", xi)], tag="lnC")
        if gamma is not None:
            sch.op("vector", lambda e, mt=mt, xi=xi: e.tensor_scalar(
                d["x1r"][xi][:, :], d["x1r"][xi][:, :], gamma[:, mt:mt + 1],
                beta[:, mt:mt + 1], ALU.mult, ALU.add),
                reads=[(kid, "x1r", xi)] + list(gkeys),
                writes=[(kid, "x1r", xi)], tag="lngb")
        for sink in sinks:
            if sink[0] == "dram":
                _, ap, tok0s = sink
                sch.dma("scalar",
                        dr_ap(ap, mt * 128, 128, [[1, 512]], col0=tok0s),
                        d["x1r"][xi][:, :],
                        reads=[(kid, "x1r", xi)],
                        writes=[(name, "snk", tok0s, mt)], group=f"{name}xo{xi}",
                        tag="lnout")
            elif sink[0] == "dram16":
                _, ap, tok0s, stg = sink
                sch.op("vector", lambda e, xi=xi: e.tensor_copy(
                    out=stg[xi][:, :], in_=d["x1r"][xi][:, :]),
                    reads=[(kid, "x1r", xi)], writes=[(kid, "st16", xi)],
                    tag="ln16")
                sch.dma("scalar",
                        dr_ap(ap, mt * 128, 128, [[1, 512]], col0=tok0s),
                        stg[xi][:, :],
                        reads=[(kid, "st16", xi)],
                        writes=[(name, "snk", tok0s, mt)], group=f"{name}x6{xi}",
                        tag="lnout16")
            else:
                _, fn = sink
                out_ap, wkey, rk2 = fn(mt)
                sch.op("vector", lambda e, out_ap=out_ap, xi=xi: e.tensor_copy(
                    out=out_ap, in_=d["x1r"][xi][:, :]),
                    reads=[(kid, "x1r", xi)] + list(rk2),
                    writes=[wkey], tag="lnsb")


def emit_phaseC(mk, es, xin, wts, host, x3_out):
    nc, sch = mk.nc, mk.sch
    B, T, TT = mk.B, mk.T, mk.TT
    D, H = 1024, 16
    KT = 8
    CH = 256
    NCH = T // CH
    SUB = 128
    NSB = CH // SUB
    NG = T // 512

    xln0_d = nc.dram_tensor("xln0_d", [D, TT], F32).ap()
    xint_d = nc.dram_tensor("xint_d", [D, TT], F32).ap()
    xf_d = nc.dram_tensor("xf_d", [D, TT + B], BF16).ap()
    xa_d = nc.dram_tensor("xa_d", [D, TT + B], BF16).ap()

    def hconst(name, rows, cols, dt=BF16):
        t = es.enter_context(nc.sbuf_tensor(mk.name("hc_" + name), [rows, cols], dt))
        q = "gpsimd" if dt == BF16 else "sync"
        sch.dma(q, t[0:rows, 0:cols], host[name],
                writes=["hc" + name, ("chain", "hcC", q)], group="hcC" + q)
        return t

    hm16 = hconst("hm16", 128, 8 * 16)
    ind16p = hconst("ind16p", 16, 8 * 128)
    u_col = hconst("u_col", 128, 8, F32)
    ident = hconst("ident", 128, 128)
    u_inc = hconst("u_inc", 128, 128)
    u_str = hconst("u_str", 128, 128)
    u01s = hconst("u01s", 128, 128)
    maa = {nm: hconst("maa_" + nm, 128, 8, F32)
           for nm in ("x", "w", "k", "v", "r", "g", "fk", "fr")}
    lncol = {nm: hconst("ln_" + nm, 128, 8, F32)
             for nm in ("g0", "b0", "g1", "b1", "g2", "b2", "gx", "bx")}

    with ExitStack() as es_w:
        def wload(name, ap, din, dout):
            t = es_w.enter_context(nc.sbuf_tensor(mk.name("w_" + name),
                                                  [128, (din // 128) * dout], BF16))
            cast_stream(mk, es_w, ap, 0, din, 0, dout, t, "w" + name,
                        out_col_stride=dout)
            return t

        wr16 = wload("wr", wts["Wr"], D, 1024)
        wk16 = wload("wk", wts["Wk"], D, 1024)
        wv16 = wload("wv", wts["Wv"], D, 1024)
        wg16 = wload("wg", wts["Wg"], D, 1024)
        wo16 = wload("wo", wts["Wo"], D, 1024)
        tm1_16 = wload("tm1", wts["tm_w1"], D, 160)
        td1_16 = wload("td1", wts["td_w1"], D, 64)
        tm2t = [es_w.enter_context(nc.sbuf_tensor(mk.name("tm2t"),
                                                   [64, 1024], BF16))
                for _ in range(2)] + \
               [es_w.enter_context(nc.sbuf_tensor(mk.name("tm2t"), [32, 1024], BF16))]
        for f in range(5):
            tt_, r0 = (tm2t[f // 2], (f % 2) * 32) if f < 4 else (tm2t[2], 0)
            sch.dma("gpsimd", tt_[r0:r0 + 32, 0:1024],
                    dr_ap(wts["tm_w2"], f * 32, 32, [[1, 1024]]),
                    writes=[("tm2", f), ("chain", "tm2")], group="tm2_ld", tag="tm2")
        td2aug = es_w.enter_context(nc.sbuf_tensor("td2aug", [128, 1024], BF16))
        sch.dma("gpsimd", td2aug[0:64, :], dr_ap(wts["td_w2"], 0, 64, [[1, 1024]]),
                writes=["td2a"], group="td2_ld", tag="td2")
        sch.dma("gpsimd", td2aug[64:65, :], host["tdrow"],
                writes=["td2b"], group="td2b_ld", tag="td2b")

        for s in range(B):
            with ExitStack() as es_a:
                xa1s = [es_a.enter_context(
                    nc.sbuf_tensor(mk.name("xa1s"), [128, 512], BF16))
                    for _ in range(2)]
                xac = es_a.enter_context(
                    nc.sbuf_tensor(mk.name("xac"), [128, KT * (CH + 1)], BF16))
                zcola = es_a.enter_context(
                    nc.sbuf_tensor(mk.name("zcola"), [128, 8], BF16))
                soff = s * T + s
                sch.op("vector", lambda e: e.memset(zcola[:, :], 0.0),
                       writes=["zcola"], tag="zca")
                sch.dma("scalar",
                        dr_ap(xa_d, 0, 128, [[128 * (TT + B), 8], [1, 1]], col0=soff),
                        zcola[:, 0:8],
                        reads=["zcola"], writes=[("xaz", s)], group="xaz_st",
                        tag="xaz", slow_ok=True)
                S_all = es_a.enter_context(nc.sbuf_tensor(mk.name("S_all"),
                                                          [128, 8 * 64], F32))
                S16 = es_a.enter_context(nc.sbuf_tensor(mk.name("S16"),
                                                        [128, 8 * 64], BF16))
                sch.op("vector", lambda e: e.memset(S_all[:, :], 0.0),
                       writes=["S_all"], tag="S0")
                sch.op("vector", lambda e: e.memset(S16[:, :], 0.0),
                       writes=["S16"], tag="S016")
                psC = [es_a.enter_context(nc.psum_tensor(mk.name("psC"),
                                                         [128, 512], F32))
                       for _ in range(4)]
                psW = [es_a.enter_context(nc.psum_tensor(mk.name("psW"),
                                                         [128, 1024], F32))
                       for _ in range(2)]
                ext = {"ps_sum": (psC[0], ("psC", 0)), "ps_sq": (psC[1], ("psC", 1)),
                       "ps_A": (psC[2], ("psC", 2)), "ps_C": (psC[3], ("psC", 3))}
                lnt = alloc_ln_tail(mk, es_a, ext_psums=ext, need_stage=False)

                for g in range(NG):
                    tok0 = s * T + g * 512
                    ltok = g * 512
                    emit_ln_gen(mk, lnt, "C0", [(xin, tok0, [])],
                                [("dram", xln0_d, tok0)],
                                gamma=lncol["g0"], beta=lncol["b0"],
                                gkeys=["hcln_g0", "hcln_b0"])
                    emit_ln_gen(mk, lnt, "C1",
                                [(xln0_d, tok0, [("C0", "snk", tok0, mt2)
                                                 for mt2 in range(KT)])],
                                [("dram16", xa_d, soff + 1 + ltok, xa1s)],
                                gamma=lncol["g1"], beta=lncol["b1"],
                                gkeys=["hcln_g1", "hcln_b1"])

                dx16 = es_a.enter_context(nc.sbuf_tensor(mk.name("dx16"),
                                                         [128, KT * CH], BF16))
                xmix = es_a.enter_context(nc.sbuf_tensor(mk.name("xmix"),
                                                         [128, KT * CH], BF16))
                rT = es_a.enter_context(nc.sbuf_tensor(mk.name("rT"),
                                                       [128, KT * CH], BF16))
                kT = es_a.enter_context(nc.sbuf_tensor(mk.name("kT"),
                                                       [128, KT * CH], BF16))
                g16 = es_a.enter_context(nc.sbuf_tensor(mk.name("g16"),
                                                        [128, KT * CH], BF16))
                ktm = es_a.enter_context(nc.sbuf_tensor(mk.name("ktm"),
                                                        [128, NSB * 1024], BF16))
                vtm = es_a.enter_context(nc.sbuf_tensor(mk.name("vtm"),
                                                        [128, NSB * 1024], BF16))
                eh16 = es_a.enter_context(nc.sbuf_tensor(mk.name("eh16"),
                                                         [128, NSB * 1024], BF16))
                el16 = es_a.enter_context(nc.sbuf_tensor(mk.name("el16"),
                                                         [128, NSB * 1024], BF16))
                e32 = es_a.enter_context(nc.sbuf_tensor(mk.name("e32"),
                                                        [128, 512], F32))
                att16 = es_a.enter_context(nc.sbuf_tensor(mk.name("att16"),
                                                          [128, KT * CH], BF16))
                htd = es_a.enter_context(nc.sbuf_tensor(mk.name("htd"),
                                                        [128, CH], BF16))
                sch.op("vector", lambda e: e.memset(htd[64:65, :], 1.0),
                       writes=["htd_ones"], tag="htdo")
                ebv = es_a.enter_context(nc.sbuf_tensor(mk.name("ebv"), [128, 1024], BF16))
                ebvf = es_a.enter_context(nc.sbuf_tensor(mk.name("ebvf"), [128, 1024], BF16))
                ea16 = es_a.enter_context(nc.sbuf_tensor(mk.name("ea16"), [128, 1024], BF16))
                al16 = es_a.enter_context(nc.sbuf_tensor(mk.name("al16"), [128, 8], F32))
                rtf = es_a.enter_context(nc.sbuf_tensor(mk.name("rtf"), [128, 1024], BF16))
                ktf = es_a.enter_context(nc.sbuf_tensor(mk.name("ktf"), [128, 1024], BF16))
                ruf = es_a.enter_context(nc.sbuf_tensor(mk.name("ruf"), [128, 1024], BF16))
                ktmf = es_a.enter_context(nc.sbuf_tensor(mk.name("ktmf"), [128, 1024], BF16))
                p2b = [es_a.enter_context(nc.sbuf_tensor(mk.name("p2b"), [128, 128], BF16))
                       for _ in range(2)]
                t3b2 = [es_a.enter_context(nc.sbuf_tensor(mk.name("t3b"), [128, 128], BF16))
                        for _ in range(2)]
                o16 = es_a.enter_context(nc.sbuf_tensor(mk.name("o16"), [128, 1024], BF16))
                osq = es_a.enter_context(nc.sbuf_tensor(mk.name("osq"), [128, 1024], BF16))
                d16s = es_a.enter_context(nc.sbuf_tensor(mk.name("d16s"), [16, 512], F32))
                eps16 = es_a.enter_context(nc.sbuf_tensor(mk.name("eps16"), [16, 1], F32))
                sch.op("vector", lambda e: e.memset(eps16[:, :], 1e-5),
                       writes=["eps16"], tag="eps16")
                gab16 = es_a.enter_context(nc.sbuf_tensor(mk.name("gab16"), [16, 256], BF16))
                onrm = [es_a.enter_context(nc.sbuf_tensor(mk.name("onrm"),
                                                          [128, 128], BF16))
                        for _ in range(2)]
                lorA = es_a.enter_context(nc.sbuf_tensor(mk.name("lorA"), [64, CH], BF16))
                lorB = es_a.enter_context(nc.sbuf_tensor(mk.name("lorB"), [64, CH], BF16))
                lorC = es_a.enter_context(nc.sbuf_tensor(mk.name("lorC"), [32, CH], BF16))
                xl0t = [es_a.enter_context(nc.sbuf_tensor(mk.name("xl0t"),
                                                          [128, CH], F32))
                        for _ in range(2)]
                xintt = es_a.enter_context(nc.sbuf_tensor(mk.name("xintt"),
                                                          [128, CH], F32))

                for c in range(NCH):
                    c0 = c * CH
                    for j in range(KT):
                        sch.dma("sync", xac[:, j * (CH + 1): (j + 1) * (CH + 1)],
                                dr_ap(xa_d, j * 128, 128, [[1, CH + 1]],
                                      col0=soff + c0),
                                reads=[("C1", "snk", soff + 1 + g2 * 512, mt2)
                                       for g2 in range(NG) for mt2 in range(KT)] +
                                      [("xaz", s)],
                                writes=[("xac", j), ("chain", "xac", j % 2)],
                                group=f"xac{j % 2}", tag="xacl")
                    for j in range(KT):
                        base = j * (CH + 1)
                        sch.op("vector", lambda e, j=j, base=base: e.tensor_sub(
                            dx16[:, j * CH:(j + 1) * CH],
                            xac[:, base: base + CH],
                            xac[:, base + 1: base + 1 + CH]),
                            reads=[("xac", j)],
                            writes=[("dx16", j)], tag="dx")
                    for j in range(KT):
                        base = j * (CH + 1)
                        sch.op("vector", lambda e, j=j, base=base: e.scalar_tensor_tensor(
                            xmix[:, j * CH:(j + 1) * CH], dx16[:, j * CH:(j + 1) * CH],
                            maa["x"][:, j:j + 1], xac[:, base + 1: base + 1 + CH],
                            ALU.mult, ALU.add),
                            reads=[("dx16", j), "hcmaa_x", ("xac", j)],
                            writes=[("xmix", j)], tag="xxx")
                    for mi, (lor, lkey, m0, mw) in enumerate(
                            ((lorA, "lorA", 0, 64), (lorB, "lorB", 64, 64),
                             (lorC, "lorC", 128, 32))):
                        pi = mi % 2
                        for k in range(KT):
                            sch.op("tensor", lambda e, k=k, m0=m0, mw=mw, pi=pi: e.matmul(
                                psC[pi][0:mw, 0:CH],
                                tm1_16[:, k * 160 + m0: k * 160 + m0 + mw],
                                xmix[:, k * CH:(k + 1) * CH],
                                start=(k == 0), stop=(k == KT - 1)),
                                reads=[("wtm1", k)] + [("xmix", kk) for kk in range(KT)],
                                writes=[("psC", pi)], tag="loramm")
                        sch.op("scalar", lambda e, lor=lor, mw=mw, pi=pi: e.activation(
                            lor[0:mw, 0:CH], psC[pi][0:mw, 0:CH], AF.Tanh),
                            reads=[("psC", pi)], writes=[lkey], tag="tanh")
                    for fname, f in (("w", 0), ("k", 1), ("v", 2), ("r", 3), ("g", 4)):
                        ltensor, lrow, lkey = (
                            (lorA, 0, "lorA"), (lorA, 32, "lorA"), (lorB, 0, "lorB"),
                            (lorB, 32, "lorB"), (lorC, 0, "lorC"))[f]
                        for j in range(KT):
                            pi = j % 2
                            tt_, r0 = (tm2t[f // 2], (f % 2) * 32) if f < 4 \
                                else (tm2t[2], 0)
                            sch.op("tensor", lambda e, j=j, pi=pi, tt_=tt_, r0=r0,
                                   ltensor=ltensor, lrow=lrow: e.matmul(
                                psC[2 + pi][:, 0:CH],
                                tt_[r0:r0 + 32, j * 128: j * 128 + 128],
                                ltensor[lrow:lrow + 32, 0:CH],
                                start=True, stop=True),
                                reads=[("tm2", f), lkey], writes=[("psC", 2 + pi)],
                                tag="memm")
                            sch.op("vector", lambda e, j=j, pi=pi, fname=fname:
                                   e.scalar_tensor_tensor(
                                       xmix[:, j * CH:(j + 1) * CH],
                                       psC[2 + pi][:, 0:CH], maa[fname][:, j:j + 1],
                                       dx16[:, j * CH:(j + 1) * CH],
                                       ALU.add, ALU.mult),
                                   reads=[("psC", 2 + pi), "hcmaa_" + fname,
                                          ("dx16", j)],
                                   writes=[("xmix", j)], tag="xfm")
                            base = j * (CH + 1)
                            sch.op("vector", lambda e, j=j, base=base: e.tensor_add(
                                xmix[:, j * CH:(j + 1) * CH],
                                xmix[:, j * CH:(j + 1) * CH],
                                xac[:, base + 1: base + 1 + CH]),
                                reads=[("xmix", j), ("xac", j)],
                                writes=[("xmix", j)], tag="xfa")
                        if fname == "w":
                            for k in range(KT):
                                sch.op("tensor", lambda e, k=k: e.matmul(
                                    psC[0][0:64, 0:CH],
                                    td1_16[:, k * 64:(k + 1) * 64],
                                    xmix[:, k * CH:(k + 1) * CH],
                                    start=(k == 0), stop=(k == KT - 1)),
                                    reads=[("wtd1", k)] + [("xmix", kk) for kk in range(KT)],
                                    writes=[("psC", 0)], tag="tdmm")
                            sch.op("scalar", lambda e: e.activation(
                                htd[0:64, 0:CH], psC[0][0:64, 0:CH], AF.Tanh),
                                reads=[("psC", 0)], writes=["htd"], tag="tdtanh")
                            for tl in range(NSB):
                                for nn in range(2):
                                    sch.op("tensor", lambda e, tl=tl, nn=nn: e.matmul(
                                        psW[0][:, nn * 512:(nn + 1) * 512],
                                        htd[0:65, tl * 128:(tl + 1) * 128],
                                        td2aug[0:65, nn * 512:(nn + 1) * 512],
                                        start=True, stop=True),
                                        reads=["htd", "htd_ones", "td2a", "td2b"],
                                        writes=[("psW", 0)], tag="wrawmm")
                                for hh2 in range(2):
                                    sch.op("scalar", lambda e, hh2=hh2: e.activation(
                                        e32[:, :], psW[0][:, hh2 * 512:(hh2 + 1) * 512],
                                        AF.Exp),
                                        reads=[("psW", 0)], writes=["e32"], tag="expw")
                                    c0e = tl * 1024 + hh2 * 512
                                    sch.op("gpsimd", lambda e, c0e=c0e: e.tensor_copy(
                                        out=eh16[:, c0e: c0e + 512], in_=e32[:, :]),
                                        reads=["e32"], writes=[("eh16", tl)], tag="ehi")
                                    sch.op("vector", lambda e, c0e=c0e: e.tensor_sub(
                                        el16[:, c0e: c0e + 512], e32[:, :],
                                        eh16[:, c0e: c0e + 512]),
                                        reads=["e32", ("eh16", tl)],
                                        writes=[("el16", tl)], tag="elo")
                        else:
                            wtile, wkey = {"k": (wk16, "wwk"), "v": (wv16, "wwv"),
                                           "r": (wr16, "wwr"), "g": (wg16, "wwg")}[fname]
                            if fname in ("r", "k", "g"):
                                dst = {"r": rT, "k": kT, "g": g16}[fname]
                                dkey = {"r": "rT", "k": "kT", "g": "g16"}[fname]
                                for mt in range(KT):
                                    pi = mt % 2
                                    for k in range(KT):
                                        sch.op("tensor", lambda e, mt=mt, k=k, pi=pi,
                                               wtile=wtile: e.matmul(
                                            psC[pi][:, 0:CH],
                                            wtile[:, k * 1024 + mt * 128:
                                                  k * 1024 + mt * 128 + 128],
                                            xmix[:, k * CH:(k + 1) * CH],
                                            start=(k == 0), stop=(k == KT - 1)),
                                            reads=[(wkey, k)] + [("xmix", kk)
                                                            for kk in range(KT)],
                                            writes=[("psC", pi)], tag="prmm")
                                    if fname == "g":
                                        sch.op("scalar", lambda e, mt=mt, pi=pi:
                                               e.activation(
                                                   dst[:, mt * CH:(mt + 1) * CH],
                                                   psC[pi][:, 0:CH], AF.Silu),
                                               reads=[("psC", pi)],
                                               writes=[(dkey, mt)], tag="pevacg")
                                    else:
                                        sch.op("vector", lambda e, mt=mt, pi=pi,
                                               dst=dst: e.tensor_copy(
                                                   out=dst[:, mt * CH:(mt + 1) * CH],
                                                   in_=psC[pi][:, 0:CH]),
                                               reads=[("psC", pi)],
                                               writes=[(dkey, mt)], tag="pevac")
                            if fname in ("k", "v"):
                                dst = ktm if fname == "k" else vtm
                                dkey = "ktm" if fname == "k" else "vtm"
                                for tl in range(NSB):
                                    for nn in range(2):
                                        pi = (tl * 2 + nn) % 2
                                        for k in range(KT):
                                            sch.op("tensor", lambda e, tl=tl, nn=nn,
                                                   k=k, pi=pi, wtile=wtile: e.matmul(
                                                psC[2 + pi][:, :],
                                                xmix[:, k * CH + tl * 128:
                                                     k * CH + tl * 128 + 128],
                                                wtile[:, k * 1024 + nn * 512:
                                                      k * 1024 + nn * 512 + 512],
                                                start=(k == 0), stop=(k == KT - 1)),
                                                reads=[(wkey, k)] + [("xmix", kk)
                                                                for kk in range(KT)],
                                                writes=[("psC", 2 + pi)], tag="ptmm")
                                        sch.op("vector", lambda e, tl=tl, nn=nn,
                                               pi=pi, dst=dst: e.tensor_copy(
                                                   out=dst[:, tl * 1024 + nn * 512:
                                                           tl * 1024 + nn * 512 + 512],
                                                   in_=psC[2 + pi][:, :]),
                                               reads=[("psC", 2 + pi)],
                                               writes=[(dkey, tl, nn)], tag="ptevac")
                    for sb in range(NSB):
                        for nn in range(2):
                            for ho in range(2):
                                sch.op("tensor", lambda e, sb=sb, ho=ho, nn=nn: e.matmul(
                                    psW[1][:, nn * 512:(nn + 1) * 512],
                                    u_inc[:, :],
                                    (eh16 if ho == 0 else el16)[
                                        :, sb * 1024 + nn * 512:
                                        sb * 1024 + nn * 512 + 512],
                                    start=(ho == 0), stop=(ho == 1)),
                                    reads=["hcu_inc", ("eh16", sb), ("el16", sb)],
                                    writes=[("psW", 1)], tag="cbtm")
                        sch.op("scalar", lambda e: e.activation(
                            ebv[:, :], psW[1][:, :], AF.Exp, scale=-1.0),
                            reads=[("psW", 1)], writes=["ebv"], tag="ebv")
                        for ct in range(8):
                            for ho in range(2):
                                sch.op("tensor", lambda e, sb=sb, ct=ct, ho=ho: e.matmul(
                                    psW[0][:, ct * 128:(ct + 1) * 128],
                                    (eh16 if ho == 0 else el16)[
                                        :, sb * 1024 + ct * 128:
                                        sb * 1024 + ct * 128 + 128],
                                    u_str[:, :],
                                    start=(ho == 0), stop=(ho == 1)),
                                    reads=["hcu_str", ("eh16", sb), ("el16", sb)],
                                    writes=[("psW", 0)], tag="cafm")
                        sch.op("scalar", lambda e: e.activation(
                            ea16[:, :], psW[0][:, :], AF.Exp),
                            reads=[("psW", 0)], writes=["ea16"], tag="ea")
                        for ct in range(8):
                            for ho in range(2):
                                sch.op("tensor", lambda e, sb=sb, ct=ct, ho=ho: e.matmul(
                                    psW[1][:, ct * 128:(ct + 1) * 128],
                                    (eh16 if ho == 0 else el16)[
                                        :, sb * 1024 + ct * 128:
                                        sb * 1024 + ct * 128 + 128],
                                    u_inc[:, :],
                                    start=(ho == 0), stop=(ho == 1)),
                                    reads=["hcu_inc", ("eh16", sb), ("el16", sb)],
                                    writes=[("psW", 1)], tag="cbfm")
                        sch.op("scalar", lambda e: e.activation(
                            ebvf[:, :], psW[1][:, :], AF.Exp, scale=-1.0),
                            reads=[("psW", 1)], writes=["ebvf"], tag="ebvf")
                        sch.op("scalar", lambda e: e.activation(
                            al16[:, 0:8],
                            sb_ap(psW[1], 0, 128, [[128, 8]], col0=127),
                            AF.Exp),
                            reads=[("psW", 1)], writes=["al16"], tag="al")
                        for ct in range(8):
                            scol = ct * CH + sb * 128
                            sch.op("vector", lambda e, ct=ct, scol=scol: e.tensor_mul(
                                rtf[:, ct * 128:(ct + 1) * 128],
                                rT[:, scol:scol + 128],
                                ea16[:, ct * 128:(ct + 1) * 128]),
                                reads=[("rT", ct), "ea16"], writes=[("rtf", ct)],
                                tag="rtf")
                            sch.op("vector", lambda e, ct=ct, scol=scol: e.tensor_mul(
                                ktf[:, ct * 128:(ct + 1) * 128],
                                kT[:, scol:scol + 128],
                                ebvf[:, ct * 128:(ct + 1) * 128]),
                                reads=[("kT", ct), "ebvf"], writes=[("ktf", ct)],
                                tag="ktf")
                            sch.op("vector", lambda e, ct=ct, scol=scol: e.tensor_scalar(
                                ruf[:, ct * 128:(ct + 1) * 128],
                                rT[:, scol:scol + 128],
                                u_col[:, ct:ct + 1], None, ALU.mult),
                                reads=[("rT", ct), "hcu_col"], writes=[("ruf", ct)],
                                tag="ruf")
                        sch.op("vector", lambda e, sb=sb: e.tensor_mul(
                            ktmf[:, :], ktm[:, sb * 1024:(sb + 1) * 1024], ebv[:, :]),
                            reads=[("ktm", sb, 0), ("ktm", sb, 1), "ebv"],
                            writes=["ktmf"], tag="ktmf")
                        for h in range(H):
                            p = h // 2
                            rw = (h % 2) * 64
                            pb = h % 2
                            t3b = t3b2[pb]
                            if pb == 0:
                                pP, pPk = psC[2][0:128, 0:128], ("psC", 2)
                                pR, pRk = psC[3][0:128, 0:128], ("psC", 3)
                            else:
                                pP, pPk = psW[1][0:128, 0:128], ("psW", 1)
                                pR, pRk = psW[1][0:128, 512:640], ("psW", 1, "b")
                            sch.op("tensor", lambda e, p=p, rw=rw, pP=pP: e.matmul(
                                pP,
                                ktf[rw:rw + 64, p * 128:(p + 1) * 128],
                                rtf[rw:rw + 64, p * 128:(p + 1) * 128],
                                start=True, stop=True),
                                reads=[("ktf", p), ("rtf", p)], writes=[pPk],
                                tag="pmm")
                            sch.op("tensor", lambda e, p=p, rw=rw, sb=sb, pR=pR: e.matmul(
                                pR,
                                ruf[rw:rw + 64, p * 128:(p + 1) * 128],
                                kT[rw:rw + 64, p * CH + sb * 128:
                                   p * CH + sb * 128 + 128],
                                start=True, stop=True),
                                reads=[("ruf", p), ("kT", p)], writes=[pRk],
                                tag="rkmm")
                            sch.op("vector", lambda e, pb=pb, pP=pP: e.tensor_mul(
                                p2b[pb][:, :], pP, u01s[:, :]),
                                reads=[pPk, "hcu01s"], writes=[("p2b", pb)],
                                tag="pmask")
                            sch.op("vector", lambda e, t3b=t3b, pR=pR: e.tensor_mul(
                                t3b[:, :], pR, ident[:, :]),
                                reads=[pRk, "hcident"], writes=[("t3b", pb)],
                                tag="dmask")
                            sch.op("vector", lambda e, pb=pb, t3b=t3b: e.tensor_add(
                                p2b[pb][:, :], p2b[pb][:, :], t3b[:, :]),
                                reads=[("p2b", pb), ("t3b", pb)], writes=[("p2b", pb)],
                                tag="paddd")
                            sch.op("tensor", lambda e, p=p, rw=rw: e.matmul(
                                psW[0][rw:rw + 64, p * 128:(p + 1) * 128],
                                S16[rw:rw + 64, p * 64:(p + 1) * 64],
                                rtf[rw:rw + 64, p * 128:(p + 1) * 128],
                                start=True, stop=False),
                                reads=["S16", ("rtf", p)], writes=[("psW", 0)],
                                tag="ointer")
                            sch.op("tensor", lambda e, h=h, p=p, rw=rw, sb=sb,
                                   pb=pb: e.matmul(
                                psW[0][rw:rw + 64, p * 128:(p + 1) * 128],
                                vtm[:, sb * 1024 + h * 64: sb * 1024 + h * 64 + 64],
                                p2b[pb][:, :],
                                start=False, stop=True),
                                reads=[("vtm", sb, (h * 64) // 512), ("p2b", pb)],
                                writes=[("psW", 0)], tag="ointra")
                            sch.op("tensor", lambda e, h=h, p=p, rw=rw, sb=sb: e.matmul(
                                psC[1][rw:rw + 64, p * 64:(p + 1) * 64],
                                ktmf[:, h * 64:(h + 1) * 64],
                                vtm[:, sb * 1024 + h * 64: sb * 1024 + h * 64 + 64],
                                start=True, stop=True),
                                reads=["ktmf", ("vtm", sb, (h * 64) // 512)],
                                writes=[("psC", 1)], tag="tmm")
                        for p in range(8):
                            sch.op("vector", lambda e, p=p: e.tensor_add(
                                S_all[:, p * 64:(p + 1) * 64],
                                S_all[:, p * 64:(p + 1) * 64],
                                psC[1][:, p * 64:(p + 1) * 64]),
                                reads=["S_all", ("psC", 1)], writes=["S_all"],
                                tag="sadd")
                            sch.op("vector", lambda e, p=p: e.tensor_scalar(
                                S_all[:, p * 64:(p + 1) * 64],
                                S_all[:, p * 64:(p + 1) * 64],
                                al16[:, p:p + 1], None, ALU.mult),
                                reads=["S_all", "al16"], writes=["S_all"], tag="smul")
                        sch.op("gpsimd", lambda e: e.tensor_copy(
                            out=S16[:, :], in_=S_all[:, :]),
                            reads=["S_all"], writes=["S16"], tag="s16c")
                        for ct in range(8):
                            sch.op("vector", lambda e, ct=ct: e.tensor_copy(
                                out=o16[:, ct * 128:(ct + 1) * 128],
                                in_=psW[0][:, ct * 128:(ct + 1) * 128]),
                                reads=[("psW", 0)], writes=[("o16", ct)], tag="oevac")
                            sch.op("gpsimd", lambda e, ct=ct: e.tensor_mul(
                                osq[:, ct * 128:(ct + 1) * 128],
                                o16[:, ct * 128:(ct + 1) * 128],
                                o16[:, ct * 128:(ct + 1) * 128]),
                                reads=[("o16", ct)], writes=[("osq", ct)], tag="osq")
                        for ct in range(8):
                            sch.op("tensor", lambda e, ct=ct: e.matmul(
                                psC[0][0:16, 0:128], hm16[:, ct * 16:(ct + 1) * 16],
                                o16[:, ct * 128:(ct + 1) * 128],
                                start=(ct == 0), stop=(ct == 7)),
                                reads=["hchm16", ("o16", ct)], writes=[("psC", 0)],
                                tag="osum")
                        for ct in range(8):
                            sch.op("tensor", lambda e, ct=ct: e.matmul(
                                psC[0][0:16, 128:256], hm16[:, ct * 16:(ct + 1) * 16],
                                osq[:, ct * 128:(ct + 1) * 128],
                                start=(ct == 0), stop=(ct == 7)),
                                reads=["hchm16", ("osq", ct)], writes=[("psC", 0)],
                                tag="osqs")
                        sch.op("vector", lambda e: e.tensor_scalar(
                            d16s[:, 0:128], psC[0][0:16, 0:128], 1.0 / 64.0, None,
                            ALU.mult),
                            reads=[("psC", 0)], writes=["gmu"], tag="gmu")
                        sch.op("vector", lambda e: e.tensor_mul(
                            d16s[:, 128:256], d16s[:, 0:128], d16s[:, 0:128]),
                            reads=["gmu"], writes=["gmu2"], tag="gmu2")
                        sch.op("vector", lambda e: e.scalar_tensor_tensor(
                            d16s[:, 128:256], psC[0][0:16, 128:256], 1.0 / 64.0,
                            d16s[:, 128:256], ALU.mult, ALU.subtract),
                            reads=[("psC", 0), "gmu2"], writes=["gvar"], tag="gvar")
                        sch.op("scalar", lambda e: e.activation(
                            d16s[:, 128:256], d16s[:, 128:256], AF.Ln,
                            bias=eps16[0:16, 0:1]),
                            reads=["gvar", "eps16"], writes=["gsd"], tag="gsd")
                        sch.op("scalar", lambda e: e.activation(
                            d16s[:, 256:384], d16s[:, 128:256], AF.Exp,
                            scale=-0.5),
                            reads=["gsd"], writes=["gA"], tag="gA")
                        sch.op("vector", lambda e: e.scalar_tensor_tensor(
                            d16s[:, 384:512], d16s[:, 0:128], -1.0, d16s[:, 256:384],
                            ALU.mult, ALU.mult),
                            reads=["gmu", "gA"], writes=["gC"], tag="gC")
                        sch.op("gpsimd", lambda e: e.tensor_copy(
                            out=gab16[0:16, 0:128], in_=d16s[:, 256:384]),
                            reads=["gA"], writes=["gA16"], tag="gA16")
                        sch.op("gpsimd", lambda e: e.tensor_copy(
                            out=gab16[0:16, 128:256], in_=d16s[:, 384:512]),
                            reads=["gC"], writes=["gC16"], tag="gC16")
                        for ct in range(8):
                            sch.op("tensor", lambda e, ct=ct: e.matmul(
                                psW[1][:, ct * 128:(ct + 1) * 128],
                                ind16p[0:16, ct * 128:(ct + 1) * 128],
                                gab16[0:16, 0:128],
                                start=True, stop=True),
                                reads=["hcind16p", "gA16"], writes=[("psW", 1)],
                                tag="gbA")
                        for ct in range(8):
                            po = ct % 2
                            sch.op("tensor", lambda e, ct=ct, po=po: e.matmul(
                                psC[2 + po][:, 0:128],
                                ind16p[0:16, ct * 128:(ct + 1) * 128],
                                gab16[0:16, 128:256],
                                start=True, stop=True),
                                reads=["hcind16p", "gC16"], writes=[("psC", 2 + po)],
                                tag="gbC")
                            sch.op("vector", lambda e, ct=ct, po=po: e.tensor_mul(
                                onrm[po][:, :], o16[:, ct * 128:(ct + 1) * 128],
                                psW[1][:, ct * 128:(ct + 1) * 128]),
                                reads=[("o16", ct), ("psW", 1)],
                                writes=[("onrm", po)], tag="gapA")
                            sch.op("vector", lambda e, po=po: e.tensor_add(
                                onrm[po][:, :], onrm[po][:, :], psC[2 + po][:, 0:128]),
                                reads=[("onrm", po), ("psC", 2 + po)],
                                writes=[("onrm", po)], tag="gapC")
                            sch.op("vector", lambda e, ct=ct, po=po: e.tensor_scalar(
                                onrm[po][:, :], onrm[po][:, :],
                                lncol["gx"][:, ct:ct + 1], lncol["bx"][:, ct:ct + 1],
                                ALU.mult, ALU.add),
                                reads=[("onrm", po), "hcln_gx", "hcln_bx"],
                                writes=[("onrm", po)], tag="gapgb")
                            acol = ct * CH + sb * 128
                            sch.op("vector", lambda e, po=po, acol=acol, ct=ct:
                                   e.tensor_mul(
                                       att16[:, acol:acol + 128], onrm[po][:, :],
                                       g16[:, acol:acol + 128]),
                                   reads=[("onrm", po), ("g16", ct)],
                                   writes=[("att16", ct, sb)], tag="attg")
                    for mt in range(KT):
                        pi = mt % 2
                        xi = mt % 2
                        sch.dma("sync", xl0t[xi][:, :],
                                dr_ap(xln0_d, mt * 128, 128, [[1, CH]],
                                      col0=s * T + c0),
                                reads=[("C0", "snk", s * T + (c0 // 512) * 512, mt)],
                                writes=[("xl0t", xi)], group=f"xl0{xi}", tag="xl0")
                        for k in range(KT):
                            sch.op("tensor", lambda e, mt=mt, k=k, pi=pi: e.matmul(
                                psC[pi][:, 0:CH],
                                wo16[:, k * 1024 + mt * 128: k * 1024 + mt * 128 + 128],
                                att16[:, k * CH:(k + 1) * CH],
                                start=(k == 0), stop=(k == KT - 1)),
                                reads=[("wwo", k)] + [("att16", k, sb2)
                                                 for sb2 in range(NSB)],
                                writes=[("psC", pi)], tag="womm")
                        sch.op("vector", lambda e, pi=pi, xi=xi: e.tensor_add(
                            xintt[:, :], psC[pi][:, 0:CH], xl0t[xi][:, :]),
                            reads=[("psC", pi), ("xl0t", xi)],
                            writes=["xintt"], tag="xint")
                        sch.dma("scalar",
                                dr_ap(xint_d, mt * 128, 128, [[1, CH]],
                                      col0=s * T + c0),
                                xintt[:, 0:CH],
                                reads=["xintt"], writes=[("xint_d", s, c0, mt)],
                                group="xint_st", tag="xintst")
                sch.barrier("endCatt")

    with ExitStack() as es_f:
        wkf16 = es_f.enter_context(nc.sbuf_tensor(mk.name("wkf16"),
                                                  [128, 8 * 3584], BF16))
        cast_stream(mk, es_f, wts["Wk_ffn"], 0, D, 0, 3584, wkf16, "wkf",
                    out_col_stride=3584)
        wvf16 = es_f.enter_context(nc.sbuf_tensor(mk.name("wvf16"),
                                                  [128, 28 * 1024], BF16))
        cast_stream(mk, es_f, wts["Wv_ffn"], 0, 3584, 0, 1024, wvf16, "wvf",
                    out_col_stride=1024)
        wrf16 = es_f.enter_context(nc.sbuf_tensor(mk.name("wrf16"),
                                                  [128, 8 * 1024], BF16))
        cast_stream(mk, es_f, wts["Wr_ffn"], 0, D, 0, 1024, wrf16, "wrf",
                    out_col_stride=1024)
        psF = [es_f.enter_context(nc.psum_tensor(mk.name("psF"), [128, 512], F32))
               for _ in range(2)]
        extF = {"ps_sum": (psF[0], ("psF", 0)),
                "ps_sq": (psF[1], ("psF", 1)),
                "ps_A": None, "ps_C": None}
        psA2 = es_f.enter_context(nc.psum_tensor(mk.name("psA2"), [128, 512], F32))
        psC2_ = es_f.enter_context(nc.psum_tensor(mk.name("psC2_"), [128, 512], F32))
        extF["ps_A"] = (psA2, "psA2k")
        extF["ps_C"] = (psC2_, "psC2k")
        lntf = alloc_ln_tail(mk, es_f, ext_psums=extF)
        zcol = es_f.enter_context(nc.sbuf_tensor(mk.name("zcol"), [128, 8], BF16))
        sch.op("vector", lambda e: e.memset(zcol[:, :], 0.0), writes=["zcol"])
        xf16g = es_f.enter_context(nc.sbuf_tensor(mk.name("xf16g"),
                                                  [128, 8 * 512], BF16))
        xfc = es_f.enter_context(nc.sbuf_tensor(mk.name("xfc"),
                                                [128, KT * (CH + 1)], BF16))
        xmix2 = es_f.enter_context(nc.sbuf_tensor(mk.name("xmix2"),
                                                  [128, KT * CH], BF16))
        kk16 = es_f.enter_context(nc.sbuf_tensor(mk.name("kk16"),
                                                 [128, 28 * CH], BF16))
        sg16 = es_f.enter_context(nc.sbuf_tensor(mk.name("sg16"),
                                                 [128, KT * CH], BF16))
        r16t = [es_f.enter_context(nc.sbuf_tensor(mk.name("r16t"), [128, CH], BF16))
                for _ in range(2)]
        xin2 = [es_f.enter_context(nc.sbuf_tensor(mk.name("xin2"), [128, CH], F32))
                for _ in range(2)]

        for s in range(B):
            soff = s * T + s
            sch.dma("scalar",
                    dr_ap(xf_d, 0, 128, [[128 * (TT + B), 8], [1, 1]], col0=soff),
                    zcol[:, 0:8],
                    reads=["zcol"], writes=[("xfz", s)], group="xfz_st", tag="xfz",
                    slow_ok=True)
            for g in range(NG):
                ltok = g * 512
                emit_ln_gen(mk, lntf, "C2",
                            [(xint_d, s * T + ltok,
                              [("xint_d", s, cc * CH, mt2) for cc in
                               range(ltok // CH, (ltok + 512) // CH)
                               for mt2 in range(KT)])],
                            [("sbuf", lambda mt: (
                                xf16g[:, mt * 512:(mt + 1) * 512],
                                ("xf16g", mt), []))],
                            gamma=lncol["g2"], beta=lncol["b2"],
                            gkeys=["hcln_g2", "hcln_b2"])
                sch.dma("scalar",
                        dr_ap(xf_d, 0, 128, [[128 * (TT + B), 8], [1, 512]],
                              col0=soff + 1 + ltok),
                        sb_ap(xf16g, 0, 128, [[512, 8], [1, 512]]),
                        reads=[("xf16g", mt2) for mt2 in range(KT)],
                        writes=[("xf_d", s, g)], group="xf_st", tag="xfst")
            for c in range(NCH):
                c0 = c * CH
                for j in range(KT):
                    sch.dma("sync", xfc[:, j * (CH + 1): (j + 1) * (CH + 1)],
                            dr_ap(xf_d, j * 128, 128, [[1, CH + 1]], col0=soff + c0),
                            reads=[("xf_d", s, g2) for g2 in range(NG)] +
                                  [("xfz", s)],
                            writes=[("xfc", j), ("chain", "xfc", j % 2)],
                            group=f"xfc{j % 2}", tag="xfl")
                for which, mcol, wt, wk_, nmt in (
                        ("k", "fk", wkf16, "wkf", 28), ("r", "fr", wrf16, "wrf", 8)):
                    for j in range(KT):
                        sch.op("vector", lambda e, j=j: e.tensor_sub(
                            xmix2[:, j * CH:(j + 1) * CH],
                            xfc[:, j * (CH + 1): j * (CH + 1) + CH],
                            xfc[:, j * (CH + 1) + 1: j * (CH + 1) + 1 + CH]),
                            reads=[("xfc", j)], writes=[("xmix2", j)], tag="dxf")
                        sch.op("vector", lambda e, j=j, mcol=mcol: e.scalar_tensor_tensor(
                            xmix2[:, j * CH:(j + 1) * CH],
                            xmix2[:, j * CH:(j + 1) * CH], maa[mcol][:, j:j + 1],
                            xfc[:, j * (CH + 1) + 1: j * (CH + 1) + 1 + CH],
                            ALU.mult, ALU.add),
                            reads=[("xmix2", j), "hcmaa_" + mcol, ("xfc", j)],
                            writes=[("xmix2", j)], tag="fmix")
                    for mt in range(nmt):
                        pi = mt % 2
                        for k in range(KT):
                            sch.op("tensor", lambda e, mt=mt, k=k, pi=pi, wt=wt,
                                   nmt=nmt: e.matmul(
                                psF[pi][:, 0:CH],
                                wt[:, k * (nmt * 128) + mt * 128:
                                   k * (nmt * 128) + mt * 128 + 128],
                                xmix2[:, k * CH:(k + 1) * CH],
                                start=(k == 0), stop=(k == KT - 1)),
                                reads=[(wk_, k)] + [("xmix2", kk) for kk in range(KT)],
                                writes=[("psF", pi)], tag="ffmm")
                        if which == "k":
                            sch.op("scalar", lambda e, pi=pi: e.activation(
                                r16t[pi][:, :], psF[pi][:, 0:CH], AF.Relu),
                                reads=[("psF", pi)], writes=[("r16t", pi)],
                                tag="frelu")
                            sch.op("gpsimd", lambda e, mt=mt, pi=pi: e.tensor_mul(
                                kk16[:, mt * CH:(mt + 1) * CH],
                                r16t[pi][:, :], r16t[pi][:, :]),
                                reads=[("r16t", pi)], writes=[("kk16", mt)],
                                tag="fsq")
                        else:
                            sch.op("scalar", lambda e, mt=mt, pi=pi: e.activation(
                                sg16[:, mt * CH:(mt + 1) * CH], psF[pi][:, 0:CH],
                                AF.Sigmoid),
                                reads=[("psF", pi)], writes=[("sg16", mt)],
                                tag="fsig")
                for mt in range(KT):
                    pi = mt % 2
                    for kk in range(28):
                        sch.op("tensor", lambda e, mt=mt, kk=kk, pi=pi: e.matmul(
                            psF[pi][:, 0:CH],
                            wvf16[:, kk * 1024 + mt * 128: kk * 1024 + mt * 128 + 128],
                            kk16[:, kk * CH:(kk + 1) * CH],
                            start=(kk == 0), stop=(kk == 27)),
                            reads=[("wvf", kk), ("kk16", kk)], writes=[("psF", pi)],
                            tag="fdmm")
                    xoff = c0 % 512
                    sch.dma("sync", xin2[pi][:, 0:CH],
                            dr_ap(xint_d, mt * 128, 128, [[1, CH]], col0=s * T + c0),
                            reads=[("xint_d", s, c0, mt)],
                            writes=[("xin2", pi)], group=f"xin2{pi}", tag="xi2")
                    sch.op("vector", lambda e, mt=mt, pi=pi, xoff=xoff: e.tensor_mul(
                        lntf["xsum"][:, mt * 512 + xoff: mt * 512 + xoff + CH],
                        psF[pi][:, 0:CH], sg16[:, mt * CH:(mt + 1) * CH]),
                        reads=[("psF", pi), ("sg16", mt)],
                        writes=[(lntf["kid"], "pre", mt)], tag="fgate")
                    sch.op("vector", lambda e, mt=mt, pi=pi, xoff=xoff: e.tensor_add(
                        lntf["xsum"][:, mt * 512 + xoff: mt * 512 + xoff + CH],
                        lntf["xsum"][:, mt * 512 + xoff: mt * 512 + xoff + CH],
                        xin2[pi][:, 0:CH]),
                        reads=[(lntf["kid"], "pre", mt), ("xin2", pi)],
                        writes=[(lntf["kid"], "pre", mt)], tag="faddint")
                if c % 2 == 1:
                    base = (c - 1) * CH
                    emit_ln_gen(mk, lntf, "C3",
                                [(xin, s * T + base, [])],
                                [("dram", x3_out, s * T + base)],
                                pre_filled=True)
        sch.barrier("endCffn")


def alloc_ln_tail(mk, es2, ext_psums=None, need_stage=True, need_xsum=True):
    nc = mk.nc
    d = {}
    d["xstg"] = ([es2.enter_context(nc.sbuf_tensor(mk.name("xstg"), [128, 512], F32))
                  for _ in range(2)] if need_stage else None)
    d["xsum"] = (es2.enter_context(nc.sbuf_tensor(mk.name("xsum"), [128, 8 * 512], F32))
                 if need_xsum else None)
    d["xs16"] = [es2.enter_context(nc.sbuf_tensor(mk.name("xs16"), [128, 512], BF16))
                 for _ in range(2)]
    d["sqr"] = [es2.enter_context(nc.sbuf_tensor(mk.name("sqr"), [128, 512], BF16))
                for _ in range(2)]
    d["x1r"] = [es2.enter_context(nc.sbuf_tensor(mk.name("x1r"), [128, 512], F32))
                for _ in range(2)]
    for nm in ("st_mu", "st_a", "st_t", "st_c"):
        d[nm] = es2.enter_context(nc.sbuf_tensor(mk.name(nm), [1, 512], F32))
    if ext_psums is None:
        for nm, shp in (("ps_sum", [1, 512]), ("ps_sq", [1, 512]),
                        ("ps_A", [128, 512]), ("ps_C", [128, 512])):
            t = es2.enter_context(nc.psum_tensor(mk.name(nm), shp, F32))
            d[nm] = t
            d["k_" + nm] = mk.name("k" + nm)
    else:
        for nm in ("ps_sum", "ps_sq", "ps_A", "ps_C"):
            ap, key = ext_psums[nm]
            d[nm] = ap
            d["k_" + nm] = key
    d["o128"] = es2.enter_context(nc.sbuf_tensor(mk.name("o128"), [128, 1], BF16))
    d["o1r"] = es2.enter_context(nc.sbuf_tensor(mk.name("o1r"), [1, 128], F32))
    d["eps"] = es2.enter_context(nc.sbuf_tensor(mk.name("epsln"), [1, 1], F32))
    sch = mk.sch
    d["kid"] = mk.name("lnid")
    ck = mk.name("Lc")
    d["k_o128"] = ck + "o128"
    d["k_o1r"] = ck + "o1r"
    d["k_eps"] = ck + "eps"
    sch.op("vector", lambda e: e.memset(d["o128"][:, :], 1.0), writes=[d["k_o128"]])
    sch.op("vector", lambda e: e.memset(d["o1r"][0:1, :], 1.0), writes=[d["k_o1r"]])
    sch.op("vector", lambda e: e.memset(d["eps"][0:1, 0:1], 1e-5), writes=[d["k_eps"]])
    return d


def emit_ln_tail(mk, d, xres_d, xout_d, tok0, name="L", xres_keys=(),
                 xsum_t=None, xsum_base=0, pre_keys=None):
    sch, nc = mk.sch, mk.nc
    KT = 8
    TT = mk.TT
    xsum0 = xsum_t if xsum_t is not None else d["xsum"]

    class _XV:
        def __getitem__(self, idx):
            p, c = idx
            return xsum0[p, c.start + xsum_base: c.stop + xsum_base]
    xsum = _XV()
    if pre_keys is None:
        pre_keys = [pre_keys[mt] for mt in range(KT)]
    for mt in range(KT):
        xi = mt % 2
        sch.dma("sync", d["xstg"][xi][:, :],
                dr_ap(xres_d, mt * 128, 128, [[1, 512]], col0=tok0),
                reads=list(xres_keys),
                writes=[(name, "xstg", xi)], group=f"{name}xstg{xi}", tag="xld")
        sch.op("vector", lambda e, mt=mt, xi=xi: e.tensor_add(
            xsum[:, mt * 512:(mt + 1) * 512], xsum[:, mt * 512:(mt + 1) * 512],
            d["xstg"][xi][:, :]),
            reads=[pre_keys[mt], (name, "xstg", xi)],
            writes=[pre_keys[mt]], tag="xsum")
        sch.op("gpsimd", lambda e, mt=mt, xi=xi: e.tensor_copy(
            out=d["xs16"][xi][:, :], in_=xsum[:, mt * 512:(mt + 1) * 512]),
            reads=[pre_keys[mt]], writes=[(name, "xs16", xi)], tag="xs16")
        sch.op("gpsimd", lambda e, mt=mt, xi=xi: e.tensor_mul(
            d["sqr"][xi][:, :], d["xs16"][xi][:, :], d["xs16"][xi][:, :]),
            reads=[(name, "xs16", xi)], writes=[(name, "sqr", xi)], tag="xsq")
        sch.op("tensor", lambda e, mt=mt, xi=xi: e.matmul(
            d["ps_sum"][0:1, :], d["o128"][:, 0:1], d["xs16"][xi][:, :],
            start=(mt == 0), stop=(mt == KT - 1)),
            reads=[d["k_o128"], (name, "xs16", xi)], writes=[d["k_ps_sum"]], tag="summ")
        sch.op("tensor", lambda e, mt=mt, xi=xi: e.matmul(
            d["ps_sq"][0:1, :], d["o128"][:, 0:1], d["sqr"][xi][:, :],
            start=(mt == 0), stop=(mt == KT - 1)),
            reads=[d["k_o128"], (name, "sqr", xi)], writes=[d["k_ps_sq"]], tag="sqm")
    sch.op("vector", lambda e: e.tensor_scalar(
        d["st_mu"][0:1, :], d["ps_sum"][0:1, :], 1.0 / 1024.0, None, ALU.mult),
        reads=[d["k_ps_sum"]], writes=[(name, "st_mu")], tag="mu")
    sch.op("vector", lambda e: e.tensor_mul(
        d["st_t"][0:1, :], d["st_mu"][0:1, :], d["st_mu"][0:1, :]),
        reads=[(name, "st_mu")], writes=[(name, "st_mu2")], tag="mu2")
    sch.op("vector", lambda e: e.scalar_tensor_tensor(
        d["st_t"][0:1, :], d["ps_sq"][0:1, :], 1.0 / 1024.0, d["st_t"][0:1, :],
        ALU.mult, ALU.subtract),
        reads=[d["k_ps_sq"], (name, "st_mu2")], writes=[(name, "st_var")], tag="var")
    sch.op("scalar", lambda e: e.activation(
        d["st_t"][0:1, :], d["st_t"][0:1, :], AF.Sqrt, bias=d["eps"][0:1, 0:1]),
        reads=[(name, "st_var"), d["k_eps"]], writes=[(name, "st_sd")], tag="sd")
    sch.op("vector", lambda e: e.reciprocal(d["st_a"][0:1, :], d["st_t"][0:1, :]),
           reads=[(name, "st_sd")], writes=[(name, "st_a")], tag="rstd")
    sch.op("vector", lambda e: e.scalar_tensor_tensor(
        d["st_c"][0:1, :], d["st_mu"][0:1, :], -1.0, d["st_a"][0:1, :],
        ALU.mult, ALU.mult),
        reads=[(name, "st_mu"), (name, "st_a")], writes=[(name, "st_c")], tag="stc")
    sch.op("tensor", lambda e: e.matmul(
        d["ps_A"][:, :], d["o1r"][0:1, :], d["st_a"][0:1, :], start=True, stop=True),
        reads=[d["k_o1r"], (name, "st_a")], writes=[d["k_ps_A"]], tag="bA")
    sch.op("tensor", lambda e: e.matmul(
        d["ps_C"][:, :], d["o1r"][0:1, :], d["st_c"][0:1, :], start=True, stop=True),
        reads=[d["k_o1r"], (name, "st_c")], writes=[d["k_ps_C"]], tag="bC")
    for mt in range(KT):
        xi = mt % 2
        sch.op("vector", lambda e, mt=mt, xi=xi: e.tensor_mul(
            d["x1r"][xi][:, :], xsum[:, mt * 512:(mt + 1) * 512], d["ps_A"][:, :]),
            reads=[pre_keys[mt], d["k_ps_A"]],
            writes=[(name, "x1r", xi)], tag="lnA")
        sch.op("vector", lambda e, mt=mt, xi=xi: e.tensor_add(
            d["x1r"][xi][:, :], d["x1r"][xi][:, :], d["ps_C"][:, :]),
            reads=[(name, "x1r", xi), d["k_ps_C"]],
            writes=[(name, "x1r", xi)], tag="lnC")
        sch.dma("scalar",
                dr_ap(xout_d, mt * 128, 128, [[1, 512]], col0=tok0),
                d["x1r"][xi][:, :],
                reads=[(name, "x1r", xi)],
                writes=[("xout", tok0, mt)], group=f"{name}xo{xi}", tag="xoutst")


def emit_phaseD(mk, es, xin, rwT, wg_d, wu_d, wd_d, swg_d, swu_d, swd_d,
                host, x4_out):
    nc, sch = mk.nc, mk.sch
    TT = mk.TT
    D, HID, NE = 1024, 4096, 7
    KT = 8
    TG = min(1024, TT)
    NTG = TT // TG

    c_d = nc.dram_tensor("c_d", [NE, TT], BF16).ap()

    ident16 = es.enter_context(nc.sbuf_tensor(mk.name("identD"), [128, 128], BF16))
    sch.dma("gpsimd", ident16[:, :], host["ident"],
            writes=["identD", ("chain", "hcD")], group="hcD")
    ones1b = es.enter_context(nc.sbuf_tensor(mk.name("ones1b"), [1, 128], BF16))
    sch.dma("gpsimd", ones1b[0:1, :], host["ones1"],
            writes=["ones1b", ("chain", "hcD")], group="hcD")
    rw16 = es.enter_context(nc.sbuf_tensor("rw16", [128, KT * 7], BF16))
    cast_stream(mk, es, rwT, 0, D, 0, 7, rw16, "rw16", out_col_stride=7)

    with ExitStack() as es_e:
        x16 = es_e.enter_context(nc.sbuf_tensor("x16D", [128, KT * TG], BF16))
        mixed = es_e.enter_context(nc.sbuf_tensor("mixedD", [128, KT * TG], F32))
        h2 = es_e.enter_context(nc.sbuf_tensor("h2D", [128, 16 * TG], BF16))
        wbuf = es_e.enter_context(nc.sbuf_tensor("wbufD", [128, 32 * 1024], BF16))
        wdbuf = es_e.enter_context(nc.sbuf_tensor("wdbufD", [128, 16 * 1024], BF16))
        c_row = es_e.enter_context(nc.sbuf_tensor("c_rowD", [1, TG], BF16))
        c_brd = es_e.enter_context(nc.sbuf_tensor("c_brdD", [128, TG], BF16))
        gtmp = [es_e.enter_context(nc.sbuf_tensor(mk.name("gtmpD"), [128, 512], BF16))
                for _ in range(2)]
        u16 = [es_e.enter_context(nc.sbuf_tensor(mk.name("u16D"), [128, 512], BF16))
               for _ in range(2)]
        lg = es_e.enter_context(nc.sbuf_tensor("lgD", [128, 7], F32))
        lg2 = es_e.enter_context(nc.sbuf_tensor("lg2D", [128, 7], F32))
        msk1 = es_e.enter_context(nc.sbuf_tensor("msk1D", [128, 7], F32))
        msk2 = es_e.enter_context(nc.sbuf_tensor("msk2D", [128, 7], F32))
        m1c = es_e.enter_context(nc.sbuf_tensor("m1cD", [128, 1], F32))
        m2c = es_e.enter_context(nc.sbuf_tensor("m2cD", [128, 1], F32))
        w2c = es_e.enter_context(nc.sbuf_tensor("w2cD", [128, 1], F32))
        cmb = es_e.enter_context(nc.sbuf_tensor("cmbD", [128, 7], F32))
        cmb16 = es_e.enter_context(nc.sbuf_tensor("cmb16D", [128, 7], BF16))
        cstg = es_e.enter_context(nc.sbuf_tensor("cstgD", [1, 7 * 128], BF16))
        ps = [es_e.enter_context(nc.psum_tensor(mk.name("psD"), [128, 512], F32))
              for _ in range(4)]
        lnt = alloc_ln_tail(mk, es_e, need_xsum=False)

        for tgi in range(NTG):
            tg0 = tgi * TG
            for k in range(KT):
                sch.dma("gpsimd", x16[:, k * TG:(k + 1) * TG],
                        dr_ap(xin, k * 128, 128, [[1, TG]], col0=tg0),
                        writes=[("x16", k), ("chain", "x16D", k % 4)],
                        group=f"x16D_{k % 4}", tag="xc")
            for ttl in range(TG // 128):
                tt = tg0 // 128 + ttl
                pi = ttl % 2
                for k in range(KT):
                    sch.op("tensor", lambda e, ttl=ttl, k=k, pi=pi: e.matmul(
                        ps[pi][:, 0:7],
                        x16[:, k * TG + ttl * 128: k * TG + ttl * 128 + 128],
                        rw16[:, k * 7:(k + 1) * 7],
                        start=(k == 0), stop=(k == KT - 1)),
                        reads=[("x16", k), ("rw16", k)], writes=[("ps", pi)], tag="rlmm")
                sch.op("vector", lambda e, pi=pi: e.tensor_copy(lg[:, :], ps[pi][:, 0:7]),
                       reads=[("ps", pi)], writes=["lg"], tag="lgc")
                sch.op("vector", lambda e: e.reduce_max(m1c[:, :], lg[:, :], axis=AX.X),
                       reads=["lg"], writes=["m1c"], tag="m1")
                sch.op("vector", lambda e: e.tensor_scalar(
                    msk1[:, :], lg[:, :], m1c[:, 0:1], None, ALU.is_ge),
                    reads=["lg", "m1c"], writes=["msk1"], tag="msk1")
                sch.op("vector", lambda e: e.scalar_tensor_tensor(
                    lg2[:, :], msk1[:, :], -1e30, lg[:, :], ALU.mult, ALU.add),
                    reads=["msk1", "lg"], writes=["lg2"], tag="lg2")
                sch.op("vector", lambda e: e.reduce_max(m2c[:, :], lg2[:, :], axis=AX.X),
                       reads=["lg2"], writes=["m2c"], tag="m2")
                sch.op("vector", lambda e: e.tensor_scalar(
                    msk2[:, :], lg2[:, :], m2c[:, 0:1], None, ALU.is_ge),
                    reads=["lg2", "m2c"], writes=["msk2"], tag="msk2")
                sch.op("vector", lambda e: e.tensor_sub(m2c[:, :], m2c[:, :], m1c[:, :]),
                       reads=["m2c", "m1c"], writes=["dm"], tag="dm")
                sch.op("scalar", lambda e: e.activation(w2c[:, :], m2c[:, :], AF.Sigmoid),
                       reads=["dm"], writes=["w2c"], tag="w2")
                sch.op("vector", lambda e: e.tensor_sub(cmb[:, :], msk2[:, :], msk1[:, :]),
                       reads=["msk2", "msk1"], writes=["cmb0"], tag="cmb0")
                sch.op("vector", lambda e: e.scalar_tensor_tensor(
                    cmb[:, :], cmb[:, :], w2c[:, 0:1], msk1[:, :], ALU.mult, ALU.add),
                    reads=["cmb0", "w2c", "msk1"], writes=["cmb"], tag="cmb")
                sch.op("gpsimd", lambda e: e.tensor_copy(cmb16[:, :], cmb[:, :]),
                       reads=["cmb"], writes=["cmb16"], tag="cmb16")
                for ee in range(NE):
                    bnk = 2 + ee // 4
                    off = (ee % 4) * 128
                    sch.op("tensor", lambda e, ee=ee, bnk=bnk, off=off: e.matmul(
                        ps[bnk][0:1, off:off + 128], cmb16[:, ee:ee + 1], ident16[:, :],
                        start=True, stop=True),
                        reads=["cmb16", "identD"], writes=[("ps", bnk)], tag="ctr")
                sch.op("vector", lambda e: e.tensor_copy(cstg[0:1, 0:512], ps[2][0:1, :]),
                       reads=[("ps", 2)], writes=["cstg0"], tag="cst0")
                sch.op("vector", lambda e: e.tensor_copy(cstg[0:1, 512:896], ps[3][0:1, 0:384]),
                       reads=[("ps", 3)], writes=["cstg1"], tag="cst1")
                sch.dma("scalar",
                        dr_ap(c_d, 0, NE, [[1, 128]], col0=tt * 128),
                        sb_ap(cstg, 0, 1, [[128, NE], [1, 128]]),
                        reads=["cstg0", "cstg1"], writes=[("c_d", tt)], group="c_st",
                        tag="cdst")
            for e_i in range(8):
                shared = e_i == 7
                if shared:
                    sch.op("vector", lambda e: e.memset(c_brd[:, :], 1.0),
                           reads=[], writes=["c_brd"], tag="cb1")
                else:
                    sch.dma("sync", c_row[0:1, 0:TG],
                            dr_ap(c_d, e_i, 1, [[1, TG]], col0=tg0),
                            reads=[("c_d", t2) for t2 in
                                   range(tg0 // 128, (tg0 + TG) // 128)],
                            writes=["c_row"], group="c_row_ld", tag="crl")
                    for tgb in range(TG // 512):
                        sch.op("tensor", lambda e, tgb=tgb: e.matmul(
                            ps[0][:, :], ones1b[0:1, :],
                            c_row[0:1, tgb * 512:(tgb + 1) * 512],
                            start=True, stop=True),
                            reads=["ones1b", "c_row"], writes=[("ps", 0)], tag="cbmm")
                        sch.op("scalar", lambda e, tgb=tgb: e.copy(
                            c_brd[:, tgb * 512:(tgb + 1) * 512], ps[0][:, :]),
                            reads=[("ps", 0)], writes=["c_brd"], tag="cbc")
                for half in range(2):
                    hid0 = half * 2048
                    for k in range(KT):
                        if shared:
                            gap, uap, ero = swg_d, swu_d, 0
                        else:
                            gap, uap, ero = wg_d, wu_d, e_i * D
                        sch.dma("gpsimd", wbuf[:, k * 2048: k * 2048 + 2048],
                                dr_ap(gap, ero + k * 128, 128, [[1, 2048]], col0=hid0),
                                writes=[("wbuf", 2 * k), ("wbuf", 2 * k + 1),
                                        ("chain", "wg", k % 4)],
                                group=f"wg_{k % 4}", tag="wgl")
                        sch.dma("gpsimd", wbuf[:, 16384 + k * 2048: 16384 + k * 2048 + 2048],
                                dr_ap(uap, ero + k * 128, 128, [[1, 2048]], col0=hid0),
                                writes=[("wbuf", 16 + 2 * k), ("wbuf", 16 + 2 * k + 1),
                                        ("chain", "wu", k % 4)],
                                group=f"wu_{k % 4}", tag="wul")
                    for m in range(16):
                        for tgb in range(TG // 512):
                            pi = (m * (TG // 512) + tgb) % 2
                            bi = pi
                            for k in range(KT):
                                sch.op("tensor", lambda e, m=m, tgb=tgb, k=k, pi=pi: e.matmul(
                                    ps[pi][:, :],
                                    wbuf[:, k * 2048 + m * 128: k * 2048 + m * 128 + 128],
                                    x16[:, k * TG + tgb * 512: k * TG + tgb * 512 + 512],
                                    start=(k == 0), stop=(k == KT - 1)),
                                    reads=[("wbuf", 2 * k + m // 8), ("x16", k)],
                                    writes=[("ps", pi)], tag="gmm")
                            for k in range(KT):
                                sch.op("tensor", lambda e, m=m, tgb=tgb, k=k, pi=pi: e.matmul(
                                    ps[2 + pi][:, :],
                                    wbuf[:, 16384 + k * 2048 + m * 128:
                                         16384 + k * 2048 + m * 128 + 128],
                                    x16[:, k * TG + tgb * 512: k * TG + tgb * 512 + 512],
                                    start=(k == 0), stop=(k == KT - 1)),
                                    reads=[("wbuf", 16 + 2 * k + m // 8), ("x16", k)],
                                    writes=[("ps", 2 + pi)], tag="umm")
                            sch.op("scalar", lambda e, pi=pi, bi=bi: e.activation(
                                gtmp[bi][:, :], ps[pi][:, :], AF.Silu),
                                reads=[("ps", pi)], writes=[("gtmp", bi)], tag="silu")
                            sch.op("scalar", lambda e, pi=pi, bi=bi: e.copy(
                                u16[bi][:, :], ps[2 + pi][:, :]),
                                reads=[("ps", 2 + pi)], writes=[("u16", bi)], tag="ucp")
                            sch.op("vector", lambda e, bi=bi: e.tensor_mul(
                                gtmp[bi][:, :], u16[bi][:, :], gtmp[bi][:, :]),
                                reads=[("u16", bi), ("gtmp", bi)],
                                writes=[("gtmp", bi)], tag="hmul")
                            hcol = m * TG + tgb * 512
                            sch.op("vector", lambda e, bi=bi, hcol=hcol, tgb=tgb: e.tensor_mul(
                                h2[:, hcol:hcol + 512], gtmp[bi][:, :],
                                c_brd[:, tgb * 512:(tgb + 1) * 512]),
                                reads=[("gtmp", bi), "c_brd"],
                                writes=[("h2", m, tgb)], tag="hscl")
                    for kk in range(16):
                        if shared:
                            dap, ero = swd_d, 0
                        else:
                            dap, ero = wd_d, e_i * HID
                        sch.dma("gpsimd", wdbuf[:, kk * 1024: kk * 1024 + 1024],
                                dr_ap(dap, ero + hid0 + kk * 128, 128, [[1, 1024]]),
                                writes=[("wdb", kk), ("chain", "wd", kk % 4)],
                                group=f"wd_{kk % 4}", tag="wdl")
                    for m in range(KT):
                        for tgb in range(TG // 512):
                            pi = (m * (TG // 512) + tgb) % 2
                            for kk in range(16):
                                sch.op("tensor", lambda e, m=m, tgb=tgb, kk=kk, pi=pi: e.matmul(
                                    ps[pi][:, :],
                                    wdbuf[:, kk * 1024 + m * 128: kk * 1024 + m * 128 + 128],
                                    h2[:, kk * TG + tgb * 512: kk * TG + tgb * 512 + 512],
                                    start=(kk == 0), stop=(kk == 15)),
                                    reads=[("wdb", kk), ("h2", kk, tgb)],
                                    writes=[("ps", pi)], tag="dmm")
                            mcol = tgb * 4096 + m * 512
                            if e_i == 0 and half == 0:
                                sch.op("vector", lambda e, mcol=mcol, pi=pi: e.tensor_copy(
                                    mixed[:, mcol:mcol + 512], ps[pi][:, :]),
                                    reads=[("ps", pi)], writes=[("mixed", m, tgb)],
                                    tag="mxc")
                            else:
                                sch.op("vector", lambda e, mcol=mcol, pi=pi: e.tensor_add(
                                    mixed[:, mcol:mcol + 512],
                                    mixed[:, mcol:mcol + 512], ps[pi][:, :]),
                                    reads=[("mixed", m, tgb), ("ps", pi)],
                                    writes=[("mixed", m, tgb)], tag="mxa")
            for tgb in range(TG // 512):
                emit_ln_tail(mk, lnt, xin, x4_out, tg0 + tgb * 512, name="Dln",
                             xsum_t=mixed, xsum_base=tgb * 4096,
                             pre_keys=[("mixed", mt, tgb) for mt in range(KT)])
        sch.barrier("endexperts")


def phaseA_host_consts(q_scale, k_scale, bproj):
    hm16 = np.zeros((128, 8 * 16), np.float32)
    ind16c = np.zeros((16, 8 * 128), np.float32)
    ind16p = np.zeros((16, 8 * 128), np.float32)
    for j in range(8):
        for p in range(128):
            h = (j * 128 + p) // 64
            hm16[p, j * 16 + h] = 1.0
            ind16c[h, j * 128 + p] = 0.125
            ind16p[h, j * 128 + p] = 1.0
    ones1 = np.ones((1, 128), np.float32)
    qs_col = np.ascontiguousarray(np.tile(q_scale, 16).reshape(8, 128).T)
    ks_col = np.ascontiguousarray(np.tile(k_scale, 16).reshape(8, 128).T)
    bproj_col = np.ascontiguousarray(bproj.reshape(8, 128).T)
    return dict(hm16=hm16, ind16c=ind16c, ind16p=ind16p, ones1=ones1,
                qs_col=qs_col, ks_col=ks_col, bproj_col=bproj_col)


PHASEA_HOST_SHAPES = {"hm16": [128, 128], "ind16c": [16, 1024], "ind16p": [16, 1024],
                      "ones1": [1, 128], "qs_col": [128, 8], "ks_col": [128, 8],
                      "bproj_col": [128, 8]}


PHASEB_HOST_SHAPES = {
    "hm8": [128, 32], "ind8": [8, 512], "prot": [128, 128],
    "csq": [128, "T"], "ssq": [128, "T"], "csk": [128, "T"], "ssk": [128, "T"],
    "mcur": [128, 2048], "mprev": [128, 2048], "ones1f": [1, 128],
    "qs_col": [128, 8], "ks_col": [128, 8], "lnBg": [128, 16], "lnBb": [128, 16],
    "b1a": [128, 44], "b1g": [128, 44], "b2c": [128, 16],
    "identB": [128, 128], "o128B": [128, 1],
}

PHASEB_WTS = {"wqkv": [2 * 1024, 1536], "wo": [2 * 512, 1024],
              "w1": [2 * 1024, 5460], "w2": [2 * 2730, 1024]}


def phaseB_host_consts(ins, T=2048):
    DIM_HEAD, WINDOW = 64, 512
    hm8 = np.zeros((128, 32), np.float32)
    ind8 = np.zeros((8, 512), np.float32)
    for ti in range(4):
        for p in range(128):
            h = (ti * 128 + p) // 64
            hm8[p, ti * 8 + h] = 1.0
            ind8[h, ti * 128 + p] = 1.0
    prot = np.zeros((128, 128), np.float32)
    for b in range(2):
        for c in range(32):
            prot[b * 64 + c + 32, b * 64 + c] = -1.0
            prot[b * 64 + c, b * 64 + c + 32] = 1.0
    half_idx = np.arange(0, DIM_HEAD, 2, dtype=np.float32)
    inv_freq = 1.0 / 10000.0 ** (half_idx / DIM_HEAD)
    t = np.arange(T, dtype=np.float32)
    freqs = t[:, None] * inv_freq[None, :]
    freqs = np.concatenate([freqs, freqs], axis=-1)
    base = (half_idx + 0.4 * DIM_HEAD) / (1.4 * DIM_HEAD)
    power = (t - T // 2) / float(WINDOW // 2)
    sc = base[None, :] ** power[:, None]
    sc = np.concatenate([sc, sc], axis=-1)
    cos, sin = np.cos(freqs), np.sin(freqs)
    def tile2(a):
        return np.ascontiguousarray(np.tile(a.T, (2, 1)).astype(np.float32))
    csq = tile2(8.0 * cos * sc)
    ssq = tile2(8.0 * sin * sc)
    csk = tile2(cos / sc)
    ssk = tile2(sin / sc)
    mcur = np.zeros((128, 2048), np.float32)
    mprev = np.zeros((128, 2048), np.float32)
    for ci in range(4):
        for kr in range(128):
            for q in range(512):
                if ci * 128 + kr <= q:
                    mcur[kr, ci * 512 + q] = 1.0
                if ci * 128 + kr > q:
                    mprev[kr, ci * 512 + q] = 1.0
    qs = np.asarray(ins["L0_ffn_q_scale"], np.float32)
    ks = np.asarray(ins["L0_ffn_k_scale"], np.float32)
    qs_col = np.zeros((128, 8), np.float32)
    ks_col = np.zeros((128, 8), np.float32)
    for l in range(2):
        qs_col[:, l * 4: l * 4 + 4] = np.tile(qs[l], 2).reshape(1, 128).T.repeat(4, 1)
        ks_col[:, l * 4: l * 4 + 4] = np.tile(ks[l], 2).reshape(1, 128).T.repeat(4, 1)
    lnBg = np.concatenate([col8(np.asarray(ins["L0_ffn_ln_g"], np.float32)[l])
                           for l in range(2)], axis=1)
    lnBb = np.concatenate([col8(np.asarray(ins["L0_ffn_ln_b"], np.float32)[l])
                           for l in range(2)], axis=1)
    b1 = np.asarray(ins["L0_ffn_b1"], np.float32)
    b2 = np.asarray(ins["L0_ffn_b2"], np.float32)
    b1a = np.zeros((128, 44), np.float32)
    b1g = np.zeros((128, 44), np.float32)
    for l in range(2):
        for i in range(22):
            n = 128 if i < 21 else 42
            b1a[:n, l * 22 + i] = b1[l, i * 128: i * 128 + n]
            b1g[:n, l * 22 + i] = b1[l, 2730 + i * 128: 2730 + i * 128 + n]
    b2c = np.concatenate([col8(b2[l]) for l in range(2)], axis=1)
    return dict(hm8=hm8, ind8=ind8, prot=prot, csq=csq, ssq=ssq, csk=csk, ssk=ssk,
                mcur=mcur, mprev=mprev, ones1f=np.ones((1, 128), np.float32),
                qs_col=qs_col, ks_col=ks_col, lnBg=lnBg, lnBb=lnBb,
                b1a=b1a, b1g=b1g, b2c=b2c,
                identB=np.eye(128, dtype=np.float32),
                o128B=np.ones((128, 1), np.float32))


def build_phaseB(B=2, T=2048):
    nc = bass.Bass("TRN2", target_bir_lowering=False)
    TT = B * T
    xin = nc.dram_tensor("xin", [1024, TT], F32, kind="ExternalInput").ap()
    wts = {n: nc.dram_tensor("w_" + n, s, F32, kind="ExternalInput").ap()
           for n, s in PHASEB_WTS.items()}
    host = {n: nc.dram_tensor("h_" + n,
                              [d if d != "T" else T for d in s], F32,
                              kind="ExternalInput").ap()
            for n, s in PHASEB_HOST_SHAPES.items()}
    x2_out = nc.dram_tensor("x2", [1024, TT], F32, kind="ExternalOutput").ap()
    sch = Sched(nc)
    mk = MK(nc, sch, B, T)
    with ExitStack() as es:
        emit_phaseB(mk, es, xin, wts, host, x2_out)
        sch.barrier("final")
        sch.emit(es)
    return nc


def phaseB_in_map(inputs, xin, T=2048):
    m = {"xin": xin,
         "w_wqkv": np.ascontiguousarray(np.asarray(inputs["L0_ffn_wqkv"], np.float32).reshape(2 * 1024, 1536)),
         "w_wo": np.ascontiguousarray(np.asarray(inputs["L0_ffn_wo"], np.float32).reshape(2 * 512, 1024)),
         "w_w1": np.ascontiguousarray(np.asarray(inputs["L0_ffn_w1"], np.float32).reshape(2 * 1024, 5460)),
         "w_w2": np.ascontiguousarray(np.asarray(inputs["L0_ffn_w2"], np.float32).reshape(2 * 2730, 1024))}
    for n, v in phaseB_host_consts(inputs, T).items():
        m["h_" + n] = v
    return m


def col8(v):
    return np.ascontiguousarray(np.asarray(v, np.float32).reshape(8, 128).T)


def phaseC_host_consts(ins):
    hm16 = np.zeros((128, 8 * 16), np.float32)
    ind16p = np.zeros((16, 8 * 128), np.float32)
    for j in range(8):
        for p in range(128):
            h = (j * 128 + p) // 64
            hm16[p, j * 16 + h] = 1.0
            ind16p[h, j * 128 + p] = 1.0
    tri = np.ones((128, 128), np.float32)
    d = {
        "hm16": hm16, "ind16p": ind16p,
        "u_col": col8(np.asarray(ins["L1_attn_time_faaaa"], np.float32).reshape(-1)),
        "ident": np.eye(128, dtype=np.float32),
        "u_inc": -np.triu(tri, 0),
        "u_str": -np.triu(tri, 1),
        "u01s": np.triu(tri, 1),
        "tdrow": np.asarray(ins["L1_attn_time_decay"], np.float32).reshape(1, 1024),
    }
    for nm in ("x", "w", "k", "v", "r", "g"):
        d["maa_" + nm] = col8(ins["L1_attn_time_maa_" + nm])
    d["maa_fk"] = col8(ins["L1_attn_ffn_maa_k"])
    d["maa_fr"] = col8(ins["L1_attn_ffn_maa_r"])
    for src_, dst in (("ln0_g", "ln_g0"), ("ln0_b", "ln_b0"), ("ln1_g", "ln_g1"),
                      ("ln1_b", "ln_b1"), ("ln2_g", "ln_g2"), ("ln2_b", "ln_b2"),
                      ("lnx_g", "ln_gx"), ("lnx_b", "ln_bx")):
        d[dst] = col8(ins["L1_attn_" + src_])
    return d


PHASEC_HOST_SHAPES = {
    "hm16": [128, 128], "ind16p": [16, 1024], "u_col": [128, 8],
    "ident": [128, 128], "u_inc": [128, 128], "u_str": [128, 128],
    "u01s": [128, 128], "tdrow": [1, 1024],
    **{"maa_" + nm: [128, 8] for nm in ("x", "w", "k", "v", "r", "g", "fk", "fr")},
    **{"ln_" + nm: [128, 8] for nm in ("g0", "b0", "g1", "b1", "g2", "b2", "gx", "bx")},
}

PHASEC_WTS = {"Wr": [1024, 1024], "Wk": [1024, 1024], "Wv": [1024, 1024],
              "Wg": [1024, 1024], "Wo": [1024, 1024], "tm_w1": [1024, 160],
              "tm_w2": [160, 1024], "td_w1": [1024, 64], "td_w2": [64, 1024],
              "Wk_ffn": [1024, 3584], "Wv_ffn": [3584, 1024],
              "Wr_ffn": [1024, 1024]}


def build_phaseC(B=2, T=2048):
    nc = bass.Bass("TRN2", target_bir_lowering=False)
    TT = B * T
    xin = nc.dram_tensor("xin", [1024, TT], F32, kind="ExternalInput").ap()
    wts = {n: nc.dram_tensor("w_" + n, s, F32, kind="ExternalInput").ap()
           for n, s in PHASEC_WTS.items()}
    host = {n: nc.dram_tensor("h_" + n, s, F32, kind="ExternalInput").ap()
            for n, s in PHASEC_HOST_SHAPES.items()}
    x3_out = nc.dram_tensor("x3", [1024, TT], F32, kind="ExternalOutput").ap()
    sch = Sched(nc)
    mk = MK(nc, sch, B, T)
    with ExitStack() as es:
        emit_phaseC(mk, es, xin, wts, host, x3_out)
        sch.barrier("final")
        sch.emit(es)
    return nc


def phaseC_in_map(inputs, xin):
    m = {"xin": xin}
    for n in PHASEC_WTS:
        v = np.asarray(inputs["L1_attn_" + n], np.float32)
        if n == "tm_w2":
            v = v.reshape(160, 1024)
        m["w_" + n] = np.ascontiguousarray(v)
    for n, v in phaseC_host_consts(inputs).items():
        m["h_" + n] = v
    return m


PHASED_HOST = {"ident": [128, 128], "ones1": [1, 128]}


def phaseD_host_consts():
    return {"ident": np.eye(128, dtype=np.float32),
            "ones1": np.ones((1, 128), np.float32)}


def build_phaseD(B=2, T=2048):
    nc = bass.Bass("TRN2", target_bir_lowering=False)
    TT = B * T
    xin = nc.dram_tensor("xin", [1024, TT], F32, kind="ExternalInput").ap()
    rwT = nc.dram_tensor("rwT", [1024, 7], F32, kind="ExternalInput").ap()
    wg = nc.dram_tensor("wg", [7 * 1024, 4096], F32, kind="ExternalInput").ap()
    wu = nc.dram_tensor("wu", [7 * 1024, 4096], F32, kind="ExternalInput").ap()
    wd = nc.dram_tensor("wd", [7 * 4096, 1024], F32, kind="ExternalInput").ap()
    swg = nc.dram_tensor("swg", [1024, 4096], F32, kind="ExternalInput").ap()
    swu = nc.dram_tensor("swu", [1024, 4096], F32, kind="ExternalInput").ap()
    swd = nc.dram_tensor("swd", [4096, 1024], F32, kind="ExternalInput").ap()
    host = {n: nc.dram_tensor("h_" + n, s, F32, kind="ExternalInput").ap()
            for n, s in PHASED_HOST.items()}
    x4_out = nc.dram_tensor("x4", [1024, TT], F32, kind="ExternalOutput").ap()
    sch = Sched(nc)
    mk = MK(nc, sch, B, T)
    with ExitStack() as es:
        emit_phaseD(mk, es, xin, rwT, wg, wu, wd, swg, swu, swd, host, x4_out)
        sch.barrier("final")
        sch.emit(es)
    return nc


def phaseD_in_map(inputs, xin):
    return {"xin": xin,
            "rwT": np.ascontiguousarray(np.asarray(inputs["L1_ffn_router_w"], np.float32).T),
            "wg": np.ascontiguousarray(np.asarray(inputs["L1_ffn_w_gate"], np.float32).reshape(7 * 1024, 4096)),
            "wu": np.ascontiguousarray(np.asarray(inputs["L1_ffn_w_up"], np.float32).reshape(7 * 1024, 4096)),
            "wd": np.ascontiguousarray(np.asarray(inputs["L1_ffn_w_down"], np.float32).reshape(7 * 4096, 1024)),
            "swg": np.ascontiguousarray(np.asarray(inputs["L1_ffn_sw_gate"], np.float32).reshape(1024, 4096)),
            "swu": np.ascontiguousarray(np.asarray(inputs["L1_ffn_sw_up"], np.float32).reshape(1024, 4096)),
            "swd": np.ascontiguousarray(np.asarray(inputs["L1_ffn_sw_down"], np.float32).reshape(4096, 1024)),
            **{"h_" + n: v for n, v in phaseD_host_consts().items()}}


def build_phaseA(B=2, T=2048):
    nc = bass.Bass("TRN2", target_bir_lowering=False)
    TT = B * T
    xin = nc.dram_tensor("xin", [1024, TT], F32, kind="ExternalInput").ap()
    Wqkv = nc.dram_tensor("Wqkv", [1024, 3072], F32, kind="ExternalInput").ap()
    Wproj = nc.dram_tensor("Wproj", [1024, 1024], F32, kind="ExternalInput").ap()
    host = {n: nc.dram_tensor(n, s, F32, kind="ExternalInput").ap()
            for n, s in PHASEA_HOST_SHAPES.items()}
    x1_out = nc.dram_tensor("x1", [1024, TT], F32, kind="ExternalOutput").ap()

    sch = Sched(nc)
    mk = MK(nc, sch, B, T)
    with ExitStack() as es:
        emit_phaseA(mk, es, xin, Wqkv, Wproj, host, x1_out)
        sch.barrier("final")
        sch.emit(es)
    return nc


def build_full(B=2, T=2048):
    nc = bass.Bass("TRN2", target_bir_lowering=False)
    TT = B * T
    x0 = nc.dram_tensor("x0", [1024, TT], F32, kind="ExternalInput").ap()
    x4 = nc.dram_tensor("x4", [1024, TT], F32, kind="ExternalOutput").ap()
    x1_d = nc.dram_tensor("x1_d", [1024, TT], F32).ap()
    x2_d = nc.dram_tensor("x2_d", [1024, TT], F32).ap()
    x3_d = nc.dram_tensor("x3_d", [1024, TT], F32).ap()

    def din(name, shape):
        return nc.dram_tensor(name, shape, F32, kind="ExternalInput").ap()

    aw = {"Wqkv": din("A_Wqkv", [1024, 3072]), "Wproj": din("A_Wproj", [1024, 1024])}
    ah = {n: din("A_h_" + n, s) for n, s in PHASEA_HOST_SHAPES.items()}
    bw = {n: din("B_w_" + n, s) for n, s in PHASEB_WTS.items()}
    bh = {n: din("B_h_" + n, [d if d != "T" else T for d in s])
          for n, s in PHASEB_HOST_SHAPES.items()}
    cw = {n: din("C_w_" + n, s) for n, s in PHASEC_WTS.items()}
    ch = {n: din("C_h_" + n, s) for n, s in PHASEC_HOST_SHAPES.items()}
    dw = {n: din("D_w_" + n, s) for n, s in
          (("rwT", [1024, 7]), ("wg", [7 * 1024, 4096]), ("wu", [7 * 1024, 4096]),
           ("wd", [7 * 4096, 1024]), ("swg", [1024, 4096]), ("swu", [1024, 4096]),
           ("swd", [4096, 1024]))}
    dh = {n: din("D_h_" + n, s) for n, s in PHASED_HOST.items()}

    sch = Sched(nc)
    mk = MK(nc, sch, B, T)
    with ExitStack() as es_all:
        with ExitStack() as esA:
            emit_phaseA(mk, esA, x0, aw["Wqkv"], aw["Wproj"], ah, x1_d)
            sch.barrier("endA")
        sch.new_epoch()
        with ExitStack() as esB:
            emit_phaseB(mk, esB, x1_d, bw, bh, x2_d)
            sch.barrier("endB2")
        sch.new_epoch()
        with ExitStack() as esC:
            emit_phaseC(mk, esC, x2_d, cw, ch, x3_d)
            sch.barrier("endC2")
        sch.new_epoch()
        with ExitStack() as esD:
            emit_phaseD(mk, esD, x3_d, dw["rwT"], dw["wg"], dw["wu"], dw["wd"],
                        dw["swg"], dw["swu"], dw["swd"], dh, x4)
            sch.barrier("final")
        sch.emit(es_all)
    return nc


def full_in_map(inputs, xshard, T=2048):
    m = {"x0": xshard,
         "A_Wqkv": np.ascontiguousarray(np.asarray(inputs["L0_attn_Wqkv"], np.float32)),
         "A_Wproj": np.ascontiguousarray(np.asarray(inputs["L0_attn_Wproj"], np.float32))}
    for n, v in phaseA_host_consts(
            np.asarray(inputs["L0_attn_q_scale"], np.float32),
            np.asarray(inputs["L0_attn_k_scale"], np.float32),
            np.asarray(inputs["L0_attn_bproj"], np.float32)).items():
        m["A_h_" + n] = v
    bm = phaseB_in_map(inputs, None, T)
    for k, v in bm.items():
        if k == "xin":
            continue
        m["B_" + k] = v
    cm = phaseC_in_map(inputs, None)
    for k, v in cm.items():
        if k == "xin":
            continue
        m["C_" + k] = v
    dm = phaseD_in_map(inputs, None)
    for k, v in dm.items():
        if k == "xin":
            continue
        if not k.startswith("h_"):
            m["D_w_" + k] = v
        else:
            m["D_" + k] = v
    return m


_CACHE = {}


def kernel(**inputs):
    B_full, T, D = 16, 2048, 1024
    n_cores = 8
    Bc = B_full // n_cores

    if "F" not in _CACHE:
        _CACHE["F"] = build_full(Bc, T)

    x = np.ascontiguousarray(np.asarray(inputs["L0_attn_x"], np.float32))
    in_maps = []
    for c in range(n_cores):
        xs = np.ascontiguousarray(x[c * Bc:(c + 1) * Bc].reshape(Bc * T, D).T)
        in_maps.append(full_in_map(inputs, xs, T))
    res = run_bass_kernel_spmd(_CACHE["F"], in_maps, core_ids=list(range(n_cores)))
    outs = [np.asarray(res.results[c]["x4"]).T.reshape(Bc, T, D)
            for c in range(n_cores)]
    return np.concatenate(outs, axis=0)
